# version 14
# baseline (speedup 1.0000x reference)
import numpy as np
import ml_dtypes
import concourse.bass as bass
import concourse.mybir as mybir
from concourse.bass_utils import run_bass_kernel_spmd

F32 = mybir.dt.float32
BF16 = mybir.dt.bfloat16
AF = mybir.ActivationFunctionType
ALU = mybir.AluOpType
AX = mybir.AxisListType

B, T, D = 4, 4096, 1024
NT = T // 128
NKC = T // 512
ND = D // 128
HTOK = T // 2
NHT = HTOK // 128
GAMMA = 0.9
WIN = 1024
FF = 4096
NF = FF // 128
NEG = -1e30
BF_NP = ml_dtypes.bfloat16
KB = 1024


def _host_tables():
    half = D // 2
    scale_vec = (np.arange(0, D, 2, dtype=np.float32) + 0.4 * D) / (1.4 * D)
    power = np.arange(T, dtype=np.float32) / 512.0
    scale = scale_vec[None, :] ** power[:, None]
    inv_freq = 1.0 / 10000.0 ** (np.arange(half, dtype=np.float32) / half)
    sinus = np.arange(T, dtype=np.float32)[:, None] * inv_freq[None, :]
    sn, cs = np.sin(sinus).astype(np.float32), np.cos(sinus).astype(np.float32)
    t = {}
    t["cosq"] = np.ascontiguousarray((cs * scale).T)
    t["sinq"] = np.ascontiguousarray((sn * scale).T)
    t["cosk"] = np.ascontiguousarray((cs / scale).T)
    t["sink"] = np.ascontiguousarray((sn / scale).T)
    i = np.arange(128, dtype=np.float64)
    j = np.arange(512, dtype=np.float64)
    t["gvpos"] = (GAMMA ** i).astype(np.float32).reshape(128, 1)
    t["gvneg"] = (GAMMA ** (-i)).astype(np.float32).reshape(128, 1)
    t["gneg"] = np.ascontiguousarray(
        np.broadcast_to((GAMMA ** (-j)).astype(np.float32), (128, 512)))
    t["gpos"] = np.ascontiguousarray(
        np.broadcast_to((GAMMA ** j).astype(np.float32), (128, 512)))
    dd = np.zeros((128, 4 * 512), dtype=np.float32)
    for r in range(4):
        dd[:, r * 512:(r + 1) * 512] = (
            GAMMA ** np.abs(128.0 * r + i[:, None] - j[None, :])
        ).astype(np.float32)
    t["ddiag"] = dd
    cm = np.zeros((128, 128), dtype=np.float32)
    cm[np.triu_indices(128, 1)] = NEG
    t["cmask"] = cm
    t["ident_bf"] = np.eye(128, dtype=BF_NP)
    t["ident_f"] = np.eye(128, dtype=np.float32)
    t["ones_row"] = np.ones((1, 128), dtype=BF_NP)
    t["eids_row"] = np.arange(8, dtype=np.float32).reshape(1, 8)
    t["iota_p"] = np.arange(128, dtype=np.float32).reshape(128, 1)
    return t


def _prep_inputs(inp):
    g = {}
    wq, wk = inp["L0_attn_W_Q"], inp["L0_attn_W_K"]
    g["wq0d"] = np.ascontiguousarray(np.concatenate([wq[:, 0::2], wq[:, 1::2]], 1))
    g["wk0d"] = np.ascontiguousarray(np.concatenate([wk[:, 0::2], wk[:, 1::2]], 1))
    g["wv0"] = np.ascontiguousarray(inp["L0_attn_W_V"])
    g["gwT"] = np.ascontiguousarray(inp["L0_ffn_gate_w"].T)
    g["gbT"] = np.ascontiguousarray(inp["L0_ffn_gate_b"].reshape(8, 1))
    g["ew_g"] = np.ascontiguousarray(
        np.concatenate([inp["L0_ffn_expert_w"][e].T for e in range(8)], axis=0))
    g["eb0"] = np.ascontiguousarray(inp["L0_ffn_expert_b"])
    g["wq1"] = np.ascontiguousarray(inp["L1_attn_w_q"])
    g["wk1"] = np.ascontiguousarray(inp["L1_attn_w_k"])
    g["wv1"] = np.ascontiguousarray(inp["L1_attn_w_v"])
    g["gamma_bc"] = np.ascontiguousarray(np.broadcast_to(inp["L1_ffn_gamma"], (128, D)))
    g["beta_bc"] = np.ascontiguousarray(np.broadcast_to(inp["L1_ffn_beta"], (128, D)))
    g["Wr"] = np.ascontiguousarray(inp["L1_ffn_Wr"])
    g["br_row"] = np.ascontiguousarray(inp["L1_ffn_br"].reshape(1, 8))
    g["W1"] = np.ascontiguousarray(inp["L1_ffn_W1"])
    g["b1T"] = np.ascontiguousarray(inp["L1_ffn_b1"].T)
    g["W2"] = np.ascontiguousarray(inp["L1_ffn_W2"])
    g["b2"] = np.ascontiguousarray(inp["L1_ffn_b2"])
    return g


class Prog:

    QSEM = {"ld": 16, "gp": 16, "so": 16, "t": 1, "v": 1, "s": 1}
    SEC = {"ld": "sync", "gp": "gpsimd", "so": "scalar", "s": "scalar",
           "t": "tensor", "v": "vector"}
    DMAQ = ("ld", "gp", "so")
    NPOOL = 16

    def __init__(self, nc):
        self.nc = nc
        self.sem = {q: nc.alloc_semaphore("q_" + q) for q in self.QSEM}
        self.done = {q: nc.alloc_semaphore("d_" + q) for q in self.DMAQ}
        self.pool = {q: [nc.alloc_semaphore(f"p_{q}{i}")
                         for i in range(self.NPOOL)] for q in self.DMAQ}
        self.pool_cnt = {q: [0] * self.NPOOL for q in self.DMAQ}
        self.pool_rr = {q: 0 for q in self.DMAQ}
        self.cnt = {q: 0 for q in self.QSEM}
        self.done_cnt = {q: 0 for q in self.DMAQ}
        self.all = []
        self.start = 0
        self.lastw = {}
        self.readers = {}
        self.waited = {}

    def op(self, q, fn, reads=(), writes=(), deps=(), pre=(), bgsem=None):
        i = len(self.all)
        dep = set(d for d in deps if d is not None)
        for r in reads:
            w = self.lastw.get(r)
            if w is not None:
                dep.add(w)
        for w in writes:
            lw = self.lastw.get(w)
            if lw is not None:
                dep.add(lw)
            dep |= self.readers.get(w, set())
        dep.discard(i)
        self.all.append(dict(q=q, fn=fn, deps=dep, ticket=None, psem=None,
                             inc=False, pre=tuple(pre), bgsem=bgsem))
        for r in reads:
            self.readers.setdefault(r, set()).add(i)
        for w in writes:
            self.lastw[w] = i
            self.readers[w] = set()
        return i

    def dma(self, q, out, in_, reads=(), writes=(), deps=()):
        return self.op(q, (lambda e, o=out, s=in_: e.dma_start(out=o, in_=s)),
                       reads, writes, deps)

    def _skip(self, d, o_q, n0):
        p = self.all[d]
        if p["q"] == "t" and o_q == "t":
            return True
        if d < n0:
            return True
        return False

    def flush(self):
        n0 = self.start
        batch = self.all[n0:]
        if not batch:
            return
        prev_q = {q: self.cnt[q] for q in self.DMAQ}
        prev_pool = {q: list(self.pool_cnt[q]) for q in self.DMAQ}

        for j, o in enumerate(batch, n0):
            for d in o["deps"]:
                if self._skip(d, o["q"], n0):
                    continue
                self.all[d]["inc"] = True
        for o in batch:
            if o["bgsem"] is not None:
                continue
            if o["q"] in self.DMAQ:
                if o["inc"]:
                    qq = o["q"]
                    i = self.pool_rr[qq]
                    self.pool_rr[qq] = (i + 1) % self.NPOOL
                    o["psem"] = (qq, i)
                    o["pprev"] = self.pool_cnt[qq][i]
                    self.pool_cnt[qq][i] += 16
                    o["ticket"] = self.pool_cnt[qq][i]
                else:
                    self.cnt[o["q"]] += 16
                    o["ticket"] = self.cnt[o["q"]]
            elif o["inc"]:
                self.cnt[o["q"]] += 1
                o["ticket"] = self.cnt[o["q"]]
        fence = {}
        for q in self.DMAQ:
            self.done_cnt[q] += 1
            fence[q] = self.done_cnt[q]

        def emit(eng, queues):
            ekey = str(eng.engine)

            def waitv(semkey, sem, v):
                if v <= 0:
                    return
                key = (ekey, semkey)
                if self.waited.get(key, -1) < v:
                    eng.wait_ge(sem, v)
                    self.waited[key] = v

            for q in queues:
                if q in self.DMAQ:
                    waitv(q, self.sem[q], prev_q[q])
                    for i in range(self.NPOOL):
                        waitv(("pool", q, i), self.pool[q][i],
                              prev_pool[q][i])
                    eng.sem_inc(self.done[q], 1)
            for q in self.DMAQ:
                waitv("d_" + q, self.done[q], fence[q])

            for j, o in enumerate(batch, n0):
                if o["q"] not in queues:
                    continue
                for (sm, smkey, v) in o["pre"]:
                    waitv(smkey, sm, v)
                if o["bgsem"] is not None:
                    inst = o["fn"](eng)
                    inst.then_inc(o["bgsem"], 16)
                    continue
                if o["psem"] is not None:
                    qq, i = o["psem"]
                    waitv(("pool", qq, i), self.pool[qq][i], o["pprev"])
                for d in sorted(o["deps"]):
                    if self._skip(d, o["q"], n0):
                        continue
                    p = self.all[d]
                    pq = p["q"]
                    if pq in self.DMAQ:
                        if p["psem"] is not None:
                            qq, i = p["psem"]
                            waitv(("pool", qq, i), self.pool[qq][i],
                                  p["ticket"])
                        else:
                            assert self.SEC[pq] == self.SEC[o["q"]],                                 (d, pq, o["q"])
                            waitv(pq, self.sem[pq], p["ticket"])
                    else:
                        assert p["ticket"] is not None, (d, p["q"])
                        waitv(pq, self.sem[pq], p["ticket"])
                inst = o["fn"](eng)
                if o["q"] in self.DMAQ:
                    if o["psem"] is not None:
                        qq, i = o["psem"]
                        inst.then_inc(self.pool[qq][i], 16)
                    else:
                        inst.then_inc(self.sem[o["q"]], 16)
                elif o["inc"]:
                    inst.then_inc(self.sem[o["q"]], 1)

        with self.nc.Block() as blk:
            blk.sync(lambda e: emit(e, ("ld",)))
            blk.gpsimd(lambda e: emit(e, ("gp",)))
            blk.scalar(lambda e: emit(e, ("s", "so")))
            blk.tensor(lambda e: emit(e, ("t",)))
            blk.vector(lambda e: emit(e, ("v",)))
        self.start = len(self.all)


SB_BIG = 35840
SB_TOP = 229376
import os
_KPH = int(os.environ.get("KPH", "99"))


def build_program():
    nc = bass.Bass("TRN2", target_bir_lowering=False)

    def EI(name, shape, dtype=F32):
        return nc.dram_tensor(name, list(shape), dtype, kind="ExternalInput").ap()

    x0 = EI("x0", (T, D))
    wq0d = EI("wq0d", (D, D)); wk0d = EI("wk0d", (D, D)); wv0 = EI("wv0", (D, D))
    cosq = EI("cosq", (512, T)); sinq = EI("sinq", (512, T))
    cosk = EI("cosk", (512, T)); sink = EI("sink", (512, T))
    gwT_d = EI("gwT", (D, 8)); gbT_d = EI("gbT", (8, 1))
    ew_g = EI("ew_g", (8 * D, 1024)); eb0 = EI("eb0", (8, 1024))
    eids_d = EI("eids_row", (1, 8)); iotap_d = EI("iota_p", (128, 1))
    wq1 = EI("wq1", (D, D)); wk1 = EI("wk1", (D, D)); wv1 = EI("wv1", (D, D))
    gamma_d = EI("gamma_bc", (128, D)); beta_d = EI("beta_bc", (128, D))
    Wr_d = EI("Wr", (D, 8)); br_d = EI("br_row", (1, 8))
    W1_d = EI("W1", (8, D, FF)); b1T_d = EI("b1T", (FF, 8))
    W2_d = EI("W2", (8, FF, D)); b2_d = EI("b2", (8, 1024))
    gvpos_d = EI("gvpos", (128, 1)); gvneg_d = EI("gvneg", (128, 1))
    gneg_d = EI("gneg", (128, 512)); gpos_d = EI("gpos", (128, 512))
    ddiag_d = EI("ddiag", (128, 4 * 512)); cmask_d = EI("cmask", (128, 128))
    idx_d = EI("idx_own", (128, NHT), mybir.dt.int32)
    identbf_d = EI("ident_bf", (128, 128), BF16)
    identf_d = EI("ident_f", (128, 128))
    ones_d = EI("ones_row", (1, 128), BF16)

    x2own = nc.dram_tensor("x2own", [HTOK, D], F32, kind="ExternalOutput").ap()

    def DSc(name, shape, dtype=F32):
        return nc.dram_tensor(name, list(shape), dtype, kind="Internal").ap()

    xnT0_d = DSc("xnT0_d", (ND, 128, T), BF16)
    qT0_d = DSc("qT0_d", (ND, 128, T), BF16)
    attn0_d = DSc("attn0_d", (T, D))
    x1_d = DSc("x1_d", (T, D))
    xn1f_d = DSc("xn1f_d", (T, D))
    xn1T_d = DSc("xn1T_d", (ND, 128, T), BF16)
    qT1_d = DSc("qT1_d", (ND, 128, T), BF16)
    attn1_d = DSc("attn1_d", (T, D))
    xn2T_d = DSc("xn2T_d", (ND, 128, HTOK), BF16)
    row8_d = DSc("row8_d", (8, 1))
    grow_d = DSc("grow_d", (1, 8))
    se2_d = DSc("se2_d", (1, 2))
    g2_d = DSc("g2_d", (1, 2))
    eb2_d = DSc("eb2_d", (2, 1024))
    ctil_d = DSc("ctil_d", (8, 1024))
    w1b_d = DSc("w1b_d", (8, D, FF), BF16)
    w2b_d = DSc("w2b_d", (8, FF, D), BF16)
    cpp_d = DSc("cpp_d", (8, 1024), BF16)

    P = Prog(nc)

    ident_bf = nc.alloc_sbuf_tensor("ident_bf_s", [128, 128], BF16)
    ident_f = nc.alloc_sbuf_tensor("ident_f_s", [128, 128], F32)
    ones_bf = nc.alloc_sbuf_tensor("ones_bf_s", [1, 128], BF16)
    gv_pos = nc.alloc_sbuf_tensor("gv_pos", [128, 1], F32)
    gv_neg = nc.alloc_sbuf_tensor("gv_neg", [128, 1], F32)
    gneg_s = nc.alloc_sbuf_tensor("gneg_s", [128, 512], F32)
    gpos_s = nc.alloc_sbuf_tensor("gpos_s", [128, 512], F32)
    cmask_s = nc.alloc_sbuf_tensor("cmask_s", [128, 128], F32)
    gates_bc = nc.alloc_sbuf_tensor("gates_bc", [128, 8], F32)
    stat = nc.alloc_sbuf_tensor("stat", [128, 64], F32)
    st8 = nc.alloc_sbuf_tensor("st8", [1, 72], F32)
    eids_s = nc.alloc_sbuf_tensor("eids_s", [1, 8], F32)
    iop_s = nc.alloc_sbuf_tensor("iop_s", [128, 1], F32)
    sef_bc = nc.alloc_sbuf_tensor("sef_bc", [128, 2], F32)
    gbc2 = nc.alloc_sbuf_tensor("gbc2", [128, 2], F32)
    idxg = nc.alloc_sbuf_tensor("idxg", [128, 16], mybir.dt.int32)
    idx1e = nc.alloc_sbuf_tensor("idx1e", [2, 1], mybir.dt.int32)
    se21 = nc.alloc_sbuf_tensor("se21", [2, 1], F32)
    idx_sb = nc.alloc_sbuf_tensor("idx_sb", [128, NHT], mybir.dt.int32)
    assert nc.bytes_allocated(bass.MemorySpace.SBUF) <= SB_BIG, \
        nc.bytes_allocated(bass.MemorySpace.SBUF)

    _atn = [0]
    def AT(shape, dtype, off):
        _atn[0] += 1
        end = SB_BIG + off + int(np.prod(shape[1:])) * mybir.dt.size(dtype)
        assert end <= SB_TOP, (shape, off, end)
        return nc.alloc_sbuf_tensor_at(f"at{_atn[0]}", list(shape), dtype,
                                       offset=SB_BIG + off)

    S = lambda c: ("stat", c)

    for dst, src in ((ident_bf, identbf_d), (ident_f, identf_d),
                     (ones_bf, ones_d), (gv_pos, gvpos_d), (gv_neg, gvneg_d),
                     (gneg_s, gneg_d), (gpos_s, gpos_d), (cmask_s, cmask_d)):
        P.dma("ld", dst[:], src, writes=[("const", dst.name)])
    P.dma("ld", idx_sb[:], idx_d, writes=[("idx",)])
    P.dma("ld", eids_s[:], eids_d, writes=[("eids",)])
    P.dma("ld", iop_s[:], iotap_d, writes=[("iop",)])
    KID = ("const", "ident_bf_s")
    KIDF = ("const", "ident_f_s")
    KONE = ("const", "ones_bf_s")

    def ph_ln(src_d, xnT, tag, src_key=None, xnf_d=None):
        xt = [AT([128, D], F32, 128 * KB + i * 4 * KB) for i in range(2)]
        scr = [AT([128, D], F32, 136 * KB + i * 4 * KB) for i in range(2)]
        xnb = [AT([128, D], BF16, 144 * KB + i * 2 * KB) for i in range(2)]
        xnf = [AT([128, D], F32, 148 * KB + i * 4 * KB) for i in range(2)]
        with (nc.psum_tensor([128, 128], BF16) as pt0,
              nc.psum_tensor([128, 128], BF16) as pt1):
            ptr = [pt0, pt1]
            for j in range(NT):
                par = j % 2
                b = 16 * par
                xb = xt[par]
                xnbp, xnfp, scrp = xnb[par], xnf[par], scr[par]
                kxt = (tag, "xt", par)
                P.dma("ld", xb[:], src_d[j * 128:(j + 1) * 128, :],
                      reads=([src_key(j)] if src_key else []), writes=[kxt])
                P.op("v", lambda e, xb=xb, b=b: e.tensor_reduce(
                    stat[:, b + 0:b + 1], xb[:], AX.X, ALU.add),
                    [kxt], [S(b + 0)])
                P.op("s", lambda e, xb=xb, b=b, scrp=scrp: e.activation(
                    scrp[:], xb[:], AF.Square, accum_out=stat[:, b + 1:b + 2]),
                    [kxt], [(tag, "scr", par), S(b + 1)])
                P.op("v", lambda e, b=b: e.tensor_scalar(
                    stat[:, b + 2:b + 3], stat[:, b + 0:b + 1], 1.0 / D, None,
                    ALU.mult), [S(b + 0)], [S(b + 2)])
                P.op("v", lambda e, b=b: e.tensor_tensor(
                    stat[:, b + 3:b + 4], stat[:, b + 2:b + 3],
                    stat[:, b + 2:b + 3], ALU.mult), [S(b + 2)], [S(b + 3)])
                P.op("v", lambda e, b=b: e.tensor_scalar(
                    stat[:, b + 4:b + 5], stat[:, b + 1:b + 2], 1.0 / D, None,
                    ALU.mult), [S(b + 1)], [S(b + 4)])
                P.op("v", lambda e, b=b: e.tensor_tensor(
                    stat[:, b + 5:b + 6], stat[:, b + 4:b + 5],
                    stat[:, b + 3:b + 4], ALU.subtract),
                    [S(b + 3), S(b + 4)], [S(b + 5)])
                P.op("v", lambda e, b=b: e.tensor_scalar(
                    stat[:, b + 5:b + 6], stat[:, b + 5:b + 6], 1e-5, None,
                    ALU.add), [S(b + 5)], [S(b + 5)])
                P.op("s", lambda e, b=b: e.activation(
                    stat[:, b + 6:b + 7], stat[:, b + 5:b + 6], AF.Sqrt),
                    [S(b + 5)], [S(b + 6)])
                P.op("v", lambda e, b=b: e.reciprocal(
                    stat[:, b + 7:b + 8], stat[:, b + 6:b + 7]),
                    [S(b + 6)], [S(b + 7)])
                P.op("v", lambda e, b=b: e.scalar_tensor_tensor(
                    stat[:, b + 8:b + 9], stat[:, b + 2:b + 3], -1.0,
                    stat[:, b + 7:b + 8], ALU.mult, ALU.mult),
                    [S(b + 2), S(b + 7)], [S(b + 8)])
                P.op("s", lambda e, xb=xb, b=b, xnbp=xnbp: e.activation(
                    xnbp[:], xb[:], AF.Identity, bias=stat[:, b + 8:b + 9],
                    scale=stat[:, b + 7:b + 8]),
                    [kxt, S(b + 7), S(b + 8)], [(tag, "xnb", par)])
                if xnf_d is not None:
                    P.op("s", lambda e, xb=xb, b=b, xnfp=xnfp: e.activation(
                        xnfp[:], xb[:], AF.Identity, bias=stat[:, b + 8:b + 9],
                        scale=stat[:, b + 7:b + 8]),
                        [kxt, S(b + 7), S(b + 8)], [(tag, "xnf", par)])
                    P.dma("so", xnf_d[j * 128:(j + 1) * 128, :], xnfp[:],
                          reads=[(tag, "xnf", par)], writes=[("xnf_d", j)])
                for dc in range(ND):
                    pp = ptr[dc % 2]
                    kp = (tag, "pt", dc % 2)
                    P.op("t", lambda e, pp=pp, dc=dc, xnbp=xnbp: e.transpose(
                        pp[:], xnbp[:, dc * 128:(dc + 1) * 128], ident_bf[:]),
                        [(tag, "xnb", par), KID], [kp])
                    P.op("v", lambda e, pp=pp, dc=dc, j=j: e.tensor_copy(
                        xnT[:, dc * T + j * 128: dc * T + (j + 1) * 128], pp[:]),
                        [kp], [("xnT" + tag, dc, j // 4)])
            P.flush()

    def ph_qk_xpos(w_d, cos_d, sin_d, xnT, tag, out_kT=None, out_q_d=None):
        t0 = tag[0]
        w_sb = AT([128, ND * D], BF16, 128 * KB)
        cs = [AT([128, 512], F32, 144 * KB + i * 2 * KB) for i in range(4)]
        sn = [AT([128, 512], F32, 172 * KB + i * 2 * KB) for i in range(4)]
        tm1 = AT([128, 512], BF16, 152 * KB)
        tm2 = AT([128, 512], BF16, 153 * KB)
        eva = [AT([128, 512], BF16, 154 * KB + i * KB) for i in range(2)]
        evb = [AT([128, 512], BF16, 180 * KB + i * KB) for i in range(2)]
        qst = AT([128, 2 * T], BF16, 156 * KB)
        P.dma("gp", w_sb[:].rearrange("p (dc o) -> p dc o", dc=ND),
              w_d.rearrange("(dc p) o -> p dc o", p=128), writes=[(tag, "w")])
        with (nc.psum_tensor([128, 512], F32) as pA0,
              nc.psum_tensor([128, 512], F32) as pA1,
              nc.psum_tensor([128, 512], F32) as pB0,
              nc.psum_tensor([128, 512], F32) as pB1):
            chunks = [(pr, c) for pr in range(4) for c in range(NKC)]
            for ci, (pr, c) in enumerate(chunks):
                    ib4 = ci % 4
                    kc, ks = (tag, "cs", ib4), (tag, "sn", ib4)
                    if ci == 0:
                        for pf in range(4):
                            pr2, c2 = chunks[pf]
                            P.dma("ld", cs[pf][:],
                                  cos_d[(pr2 * 128):(pr2 + 1) * 128,
                                        c2 * 512:(c2 + 1) * 512],
                                  writes=[(tag, "cs", pf)])
                            P.dma("ld", sn[pf][:],
                                  sin_d[(pr2 * 128):(pr2 + 1) * 128,
                                        c2 * 512:(c2 + 1) * 512],
                                  writes=[(tag, "sn", pf)])
                    if ci + 4 < len(chunks):
                        pr2, c2 = chunks[ci + 4]
                        ib2 = (ci + 4) % 4
                        P.dma("ld", cs[ib2][:],
                              cos_d[(pr2 * 128):(pr2 + 1) * 128,
                                    c2 * 512:(c2 + 1) * 512],
                              writes=[(tag, "cs", ib2)])
                        P.dma("ld", sn[ib2][:],
                              sin_d[(pr2 * 128):(pr2 + 1) * 128,
                                    c2 * 512:(c2 + 1) * 512],
                              writes=[(tag, "sn", ib2)])
                    ib = c % 2
                    pa, pb = (pA0, pB0) if ib == 0 else (pA1, pB1)
                    ka, kb = (tag, "pa", ib), (tag, "pb", ib)
                    for jo, ps, kk in ((pr, pa, ka), (pr + 4, pb, kb)):
                        for dc in range(ND):
                            P.op("t", lambda e, ps=ps, jo=jo, dc=dc, c=c:
                                 e.matmul(ps[:],
                                          w_sb[:, dc * D + jo * 128:
                                               dc * D + (jo + 1) * 128],
                                          xnT[:, dc * T + c * 512:
                                              dc * T + (c + 1) * 512],
                                          start=(dc == 0), stop=(dc == ND - 1)),
                                 [(tag, "w"), ("xnT" + t0, dc, c)], [kk])
                    kea, keb = (tag, "eva", ib), (tag, "evb", ib)
                    P.op("v", lambda e, pa=pa, ib=ib: e.tensor_copy(
                        eva[ib][:], pa[:]), [ka], [kea])
                    P.op("v", lambda e, pb=pb, ib=ib: e.tensor_copy(
                        evb[ib][:], pb[:]), [kb], [keb])
                    P.op("v", lambda e, ib=ib, ib4=ib4: e.tensor_tensor(
                        tm1[:], eva[ib][:], cs[ib4][:], ALU.mult),
                        [kea, kc], [(tag, "tm1")])
                    P.op("v", lambda e, ib=ib, ib4=ib4: e.tensor_tensor(
                        tm2[:], evb[ib][:], sn[ib4][:], ALU.mult),
                        [keb, ks], [(tag, "tm2")])
                    if out_kT is not None:
                        P.op("v", lambda e, pr=pr, c=c: e.tensor_tensor(
                            out_kT[:, pr * T + c * 512: pr * T + (c + 1) * 512],
                            tm1[:], tm2[:], ALU.subtract),
                            [(tag, "tm1"), (tag, "tm2")], [("kT" + t0, pr, c)])
                    else:
                        P.op("v", lambda e, c=c: e.tensor_tensor(
                            qst[:, c * 512:(c + 1) * 512],
                            tm1[:], tm2[:], ALU.subtract),
                            [(tag, "tm1"), (tag, "tm2")], [(tag, "qst", 0, c)])
                    P.op("v", lambda e, ib=ib, ib4=ib4: e.tensor_tensor(
                        tm1[:], evb[ib][:], cs[ib4][:], ALU.mult),
                        [keb, kc], [(tag, "tm1")])
                    P.op("v", lambda e, ib=ib, ib4=ib4: e.tensor_tensor(
                        tm2[:], eva[ib][:], sn[ib4][:], ALU.mult),
                        [kea, ks], [(tag, "tm2")])
                    if out_kT is not None:
                        P.op("v", lambda e, pr=pr, c=c: e.tensor_tensor(
                            out_kT[:, (pr + 4) * T + c * 512:
                                   (pr + 4) * T + (c + 1) * 512],
                            tm1[:], tm2[:], ALU.add),
                            [(tag, "tm1"), (tag, "tm2")],
                            [("kT" + t0, pr + 4, c)])
                    else:
                        P.op("v", lambda e, c=c: e.tensor_tensor(
                            qst[:, T + c * 512: T + (c + 1) * 512],
                            tm1[:], tm2[:], ALU.add),
                            [(tag, "tm1"), (tag, "tm2")], [(tag, "qst", 1, c)])
                    if out_q_d is not None and c == NKC - 1:
                        for hh, jo in ((0, pr), (1, pr + 4)):
                            P.dma("so", out_q_d[jo][:, :],
                                  qst[:, hh * T:(hh + 1) * T],
                                  reads=[(tag, "qst", hh, cc)
                                         for cc in range(NKC)],
                                  writes=[("qT_d" + t0, jo)])
            P.flush()

    def ph_spill_xnT(xnT, spill_d, tag):
        for dc in range(ND):
            P.dma("so", spill_d[dc][:, :], xnT[:, dc * T:(dc + 1) * T],
                  reads=[("xnT" + tag, dc, c) for c in range(NKC)],
                  writes=[("xnTd" + tag, dc)])

    def ph_v(wv_d, spill_d, v_sb, tag, xoff):
        wv_sb = AT([128, ND * D], BF16, 128 * KB)
        xc = [AT([128, ND * 128], BF16, xoff + i * 2 * KB) for i in range(2)]
        P.dma("gp", wv_sb[:].rearrange("p (dc o) -> p dc o", dc=ND),
              wv_d.rearrange("(dc p) o -> p dc o", p=128), writes=[(tag, "wv")])
        with (nc.psum_tensor([128, 512], F32) as p0,
              nc.psum_tensor([128, 512], F32) as p1):
            for j in range(NT):
                xb = xc[j % 2]
                kx = (tag, "vxc", j % 2)
                P.dma("ld", xb[:].rearrange("p (dc k) -> p dc k", dc=ND),
                      spill_d[:, :, j * 128:(j + 1) * 128]
                      .rearrange("dc p k -> p dc k"),
                      reads=[("xnTd" + tag, dc) for dc in range(ND)],
                      writes=[kx])
                for n in range(2):
                    ps = (p0, p1)[n]
                    kk = (tag, "vps", n)
                    for dc in range(ND):
                        P.op("t", lambda e, ps=ps, xb=xb, dc=dc, n=n: e.matmul(
                            ps[:], xb[:, dc * 128:(dc + 1) * 128],
                            wv_sb[:, dc * D + n * 512: dc * D + (n + 1) * 512],
                            start=(dc == 0), stop=(dc == ND - 1)),
                            [(tag, "wv"), kx], [kk])
                    P.op("v", lambda e, ps=ps, j=j, n=n: e.tensor_copy(
                        v_sb[:, j * D + n * 512: j * D + (n + 1) * 512], ps[:]),
                        [kk], [("v" + tag, j)])
            P.flush()

    def ph_plain_kq(wk_d, wq_d, xnT, kT, q_d, tag):
        w_sb = AT([128, ND * D], BF16, 128 * KB)
        qst = AT([128, T], BF16, 152 * KB)
        for w_d, dest in ((wk_d, "k"), (wq_d, "q")):
            P.dma("gp", w_sb[:].rearrange("p (dc o) -> p dc o", dc=ND),
                  w_d.rearrange("(dc p) o -> p dc o", p=128),
                  writes=[(tag, "w")])
            with (nc.psum_tensor([128, 512], F32) as p0,
                  nc.psum_tensor([128, 512], F32) as p1):
                for jo in range(ND):
                    for c in range(NKC):
                        ps = (p0, p1)[c % 2]
                        kk = (tag, "ps", c % 2)
                        for dc in range(ND):
                            P.op("t", lambda e, ps=ps, jo=jo, dc=dc, c=c:
                                 e.matmul(ps[:],
                                          w_sb[:, dc * D + jo * 128:
                                               dc * D + (jo + 1) * 128],
                                          xnT[:, dc * T + c * 512:
                                              dc * T + (c + 1) * 512],
                                          start=(dc == 0), stop=(dc == ND - 1)),
                                 [(tag, "w"), ("xnT" + tag, dc, c)], [kk])
                        if dest == "k":
                            P.op("v", lambda e, ps=ps, jo=jo, c=c: e.tensor_copy(
                                kT[:, jo * T + c * 512: jo * T + (c + 1) * 512],
                                ps[:]), [kk], [("kT" + tag, jo, c)])
                        else:
                            P.op("v", lambda e, ps=ps, c=c: e.tensor_copy(
                                qst[:, c * 512:(c + 1) * 512], ps[:]),
                                [kk], [(tag, "qst", c)])
                    if dest == "q":
                        P.dma("so", q_d[jo][:, :], qst[:],
                              reads=[(tag, "qst", c) for c in range(NKC)],
                              writes=[("qT_d" + tag, jo)])
                P.flush()

    def ph_attn(kT, v_sb, q_d, out_d, tag, causal, kept_fn):
        Ssb = AT([128, T], F32, 128 * KB)
        Ebf = [AT([128, T], BF16, 144 * KB), AT([128, T], BF16, 172 * KB)]
        dd_s = AT([128, 2048], F32, 152 * KB)
        qt = [AT([128, ND * 128], BF16, 160 * KB + i * 2 * KB) for i in range(2)]
        abf = [AT([128, 512], BF16, 164 * KB + i * KB) for i in range(2)]
        ats = [AT([128, 128], BF16, 166 * KB + i * KB) for i in range(2)]
        att = [AT([128, D], F32, 168 * KB), AT([128, D], F32, 180 * KB)]
        if not causal:
            P.dma("ld", dd_s[:], ddiag_d, writes=[("ddiag",)])
        with (nc.psum_tensor([128, 512], F32) as ps0,
              nc.psum_tensor([128, 512], F32) as ps1,
              nc.psum_tensor([128, 512], F32) as pav0,
              nc.psum_tensor([128, 512], F32) as pav1,
              nc.psum_tensor([128, 128], BF16) as ptA,
              nc.psum_tensor([128, 128], BF16) as ptB):
            for j in range(NT):
                par = j % 2
                b = 32 + 16 * par
                Eb = Ebf[par]
                kE = (tag, "E", par)
                atb = att[par]
                ext = (j + 1) * 128 if causal else T
                nch = (ext + 511) // 512
                qb = qt[par]
                kq = (tag, "qt", par)
                P.dma("ld", qb[:].rearrange("p (dc k) -> p dc k", dc=ND),
                      q_d[:, :, j * 128:(j + 1) * 128]
                      .rearrange("dc p k -> p dc k"),
                      reads=[("qT_d" + tag, dc) for dc in range(ND)],
                      writes=[kq])
                srd = []
                for c in range(nch):
                    ncols = min(512, ext - c * 512)
                    ps = (ps0, ps1)[c % 2]
                    kk = (tag, "ps", c % 2)
                    for dc in range(ND):
                        P.op("t", lambda e, ps=ps, qb=qb, dc=dc, c=c, ncols=ncols:
                             e.matmul(ps[:, 0:ncols],
                                      qb[:, dc * 128:(dc + 1) * 128],
                                      kT[:, dc * T + c * 512:
                                         dc * T + c * 512 + ncols],
                                      start=(dc == 0), stop=(dc == ND - 1)),
                             [kq, ("kT" + tag, dc, c)], [kk])
                    evsc = (1.0 / 32.0) if causal else 1.0
                    if causal and c == nch - 1:
                        if ncols > 128:
                            P.op("v", lambda e, ps=ps, c=c, ncols=ncols:
                                 e.tensor_scalar(
                                     Ssb[:, c * 512: c * 512 + ncols - 128],
                                     ps[:, 0:ncols - 128], evsc, None,
                                     ALU.mult),
                                 [kk], [(tag, "S", c)])
                            srd.append((tag, "S", c))
                        P.op("v", lambda e, ps=ps, c=c, ncols=ncols:
                             e.scalar_tensor_tensor(
                                 Ssb[:, c * 512 + ncols - 128: c * 512 + ncols],
                                 ps[:, ncols - 128:ncols], evsc, cmask_s[:],
                                 ALU.mult, ALU.add),
                             [kk, ("const", "cmask_s")], [(tag, "Sm", c)])
                        srd.append((tag, "Sm", c))
                    elif causal:
                        P.op("v", lambda e, ps=ps, c=c, ncols=ncols:
                             e.tensor_scalar(Ssb[:, c * 512:c * 512 + ncols],
                                             ps[:, 0:ncols], evsc, None,
                                             ALU.mult),
                             [kk], [(tag, "S", c)])
                        srd.append((tag, "S", c))
                    else:
                        P.op("v", lambda e, ps=ps, c=c, ncols=ncols:
                             e.tensor_copy(Ssb[:, c * 512:c * 512 + ncols],
                                           ps[:, 0:ncols]),
                             [kk], [(tag, "S", c)])
                        srd.append((tag, "S", c))
                P.op("v", lambda e, ext=ext, b=b: e.tensor_reduce(
                    stat[:, b + 9 - 9:b + 10 - 9], Ssb[:, 0:ext], AX.X,
                    ALU.max), srd, [S(b + 0)])
                P.op("v", lambda e, b=b: e.tensor_scalar(
                    stat[:, b + 1:b + 2], stat[:, b + 0:b + 1], -1.0, None,
                    ALU.mult), [S(b + 0)], [S(b + 1)])
                P.op("s", lambda e, ext=ext, b=b, Eb=Eb: e.activation(
                    Eb[:, 0:ext], Ssb[:, 0:ext], AF.Exp,
                    bias=stat[:, b + 1:b + 2],
                    accum_out=stat[:, b + 2:b + 3]),
                    srd + [S(b + 1)], [kE, S(b + 2)])
                P.op("v", lambda e, b=b: e.reciprocal(
                    stat[:, b + 3:b + 4], stat[:, b + 2:b + 3]),
                    [S(b + 2)], [S(b + 3)])
                kept = kept_fn(j)
                nsub_tot = sum((min(512, ext - c * 512) + 127) // 128
                               for (c, _, _) in kept)
                kav = [(tag, "pav", 0), (tag, "pav", 1)]
                sdone = 0
                for (c, typ, coef) in kept:
                    ncols = min(512, ext - c * 512)
                    ab = abf[c % 2]
                    kab = (tag, "abf", c % 2)
                    if causal:
                        P.op("v", lambda e, ab=ab, c=c, ncols=ncols, b=b, Eb=Eb:
                             e.tensor_scalar(ab[:, 0:ncols],
                                             Eb[:, c * 512:c * 512 + ncols],
                                             stat[:, b + 3:b + 4], None,
                                             ALU.mult),
                             [kE, S(b + 3)], [kab])
                    elif typ == "d":
                        P.op("v", lambda e, ab=ab, c=c, r=coef, b=b, Eb=Eb:
                             e.scalar_tensor_tensor(
                                 ab[:], Eb[:, c * 512:(c + 1) * 512],
                                 stat[:, b + 3:b + 4],
                                 dd_s[:, r * 512:(r + 1) * 512],
                                 ALU.mult, ALU.mult),
                             [kE, S(b + 3), ("ddiag",)], [kab])
                    else:
                        gv = gv_pos if typ == "l" else gv_neg
                        gb = gneg_s if typ == "l" else gpos_s
                        gbk = ("const", "gneg_s" if typ == "l" else "gpos_s")
                        P.op("v", lambda e, gv=gv, coef=coef, b=b:
                             e.tensor_scalar(
                                 stat[:, b + 4:b + 5], gv[:],
                                 stat[:, b + 3:b + 4], coef,
                                 ALU.mult, ALU.mult), [S(b + 3)], [S(b + 4)])
                        P.op("v", lambda e, ab=ab, c=c, gb=gb, b=b, Eb=Eb:
                             e.scalar_tensor_tensor(
                                 ab[:], Eb[:, c * 512:(c + 1) * 512],
                                 stat[:, b + 4:b + 5], gb[:], ALU.mult,
                                 ALU.mult),
                             [kE, S(b + 4), gbk], [kab])
                    nsub = (ncols + 127) // 128
                    for sub in range(nsub):
                        s_k = c * 4 + sub
                        pt = (ptA, ptB)[s_k % 2]
                        kpt = (tag, "pt", s_k % 2)
                        at = ats[s_k % 2]
                        kat = (tag, "at", s_k % 2)
                        P.op("t", lambda e, pt=pt, ab=ab, sub=sub: e.transpose(
                            pt[:], ab[:, sub * 128:(sub + 1) * 128],
                            ident_bf[:]), [kab, KID], [kpt])
                        P.op("v", lambda e, pt=pt, at=at: e.tensor_copy(
                            at[:], pt[:]), [kpt], [kat])
                        sdone += 1
                        last = (sdone == nsub_tot)
                        for n in range(2):
                            P.op("t", lambda e, at=at, s_k=s_k, n=n,
                                 first=(sdone == 1), last=last,
                                 pv=(pav0, pav1)[n]:
                                 e.matmul(pv[:], at[:],
                                          v_sb[:, s_k * D + n * 512:
                                               s_k * D + (n + 1) * 512],
                                          start=first, stop=last),
                                 [kat, ("v" + tag, s_k)], [kav[n]])
                P.op("v", lambda e, atb=atb: e.tensor_copy(
                    atb[:, 0:512], pav0[:]), [kav[0]], [(tag, "att", par, 0)])
                P.op("v", lambda e, atb=atb: e.tensor_copy(
                    atb[:, 512:1024], pav1[:]), [kav[1]], [(tag, "att", par, 1)])
                P.dma("so", out_d[j * 128:(j + 1) * 128, :], atb[:],
                      reads=[(tag, "att", par, 0), (tag, "att", par, 1)],
                      writes=[("attn_d" + tag, j)])
            P.flush()


    def dyn_load(e, dst, src_d, rg):
        return e.indirect_dma_start(
            dst, None, src_d,
            bass.IndirectOffsetOnAxis(ap=idx_sb[:, rg:rg + 1], axis=0))

    xnT_A = AT([128, ND * T], BF16, 0)
    kT_A = AT([128, ND * T], BF16, 64 * KB)

    wbg = nc.alloc_semaphore("wbg")
    for e_ in range(8):
        P.op("gp", lambda e, e_=e_: e.dma_start(out=w1b_d[e_], in_=W1_d[e_]),
             bgsem=wbg)
        P.op("gp", lambda e, e_=e_: e.dma_start(out=w2b_d[e_], in_=W2_d[e_]),
             bgsem=wbg)
    ph_ln(x0, xnT_A, "0")
    ph_qk_xpos(wk0d, cosk, sink, xnT_A, "0k", out_kT=kT_A)
    ph_qk_xpos(wq0d, cosq, sinq, xnT_A, "0q", out_q_d=qT0_d)

    gw_sb = AT([128, ND * 8], BF16, 128 * KB)
    Lacc = AT([8, T], F32, 130 * KB)
    gb_sb = AT([8, 1], F32, 146 * KB)
    ls8 = AT([8, 2], F32, 147 * KB)
    P.dma("gp", gw_sb[:].rearrange("p (dc e) -> p dc e", dc=ND),
          gwT_d.rearrange("(dc p) e -> p dc e", p=128), writes=[("gw",)])
    P.dma("ld", gb_sb[:], gbT_d, writes=[("gb",)])
    with nc.psum_tensor([8, 512], F32) as pg:
        for c in range(NKC):
            for dc in range(ND):
                P.op("t", lambda e, dc=dc, c=c: e.matmul(
                    pg[:], gw_sb[:, dc * 8:(dc + 1) * 8],
                    xnT_A[:, dc * T + c * 512: dc * T + (c + 1) * 512],
                    start=(dc == 0), stop=(dc == ND - 1)),
                    [("gw",), ("xnT0", dc, c)], [("pg",)])
            P.op("v", lambda e, c=c: e.tensor_copy(
                Lacc[:, c * 512:(c + 1) * 512], pg[:]),
                [("pg",)], [("Lacc", c)])
        P.op("v", lambda e: e.tensor_reduce(
            ls8[:, 0:1], Lacc[:], AX.X, ALU.add),
            [("Lacc", c) for c in range(NKC)], [("Ls",)])
        P.op("v", lambda e: e.scalar_tensor_tensor(
            ls8[:, 1:2], ls8[:, 0:1], 1.0 / T, gb_sb[:], ALU.mult, ALU.add),
            [("Ls",), ("gb",)], [("Lg",)])
        P.dma("gp", row8_d, ls8[:, 1:2], reads=[("Lg",)], writes=[("r8d",)])
        lrow = st8[0:1, 0:8]
        P.dma("gp", lrow, row8_d.rearrange("p one -> one p"),
              reads=[("r8d",)], writes=[("lrow",)])
        r8 = st8[0:1, 8:16]
        g8 = st8[0:1, 16:24]
        e8 = st8[0:1, 24:32]
        m1 = st8[0:1, 32:33]; nm1 = st8[0:1, 33:34]; m2 = st8[0:1, 34:35]
        zz = st8[0:1, 35:36]; rz8 = st8[0:1, 36:37]
        inv8 = st8[0:1, 37:45]
        keep2 = st8[0:1, 45:53]
        tmp8 = st8[0:1, 53:61]
        sc4 = st8[0:1, 61:65]
        P.op("v", lambda e: e.tensor_reduce(m1, lrow, AX.X, ALU.max),
             [("lrow",)], [("m1",)])
        P.op("v", lambda e: e.tensor_scalar(inv8, lrow, m1, None, ALU.is_ge),
             [("lrow",), ("m1",)], [("inv",)])
        P.op("v", lambda e: e.scalar_tensor_tensor(
            r8, inv8, NEG, lrow, ALU.mult, ALU.add),
            [("inv",), ("lrow",)], [("msk",)])
        P.op("v", lambda e: e.tensor_reduce(m2, r8, AX.X, ALU.max),
             [("msk",)], [("m2",)])
        P.op("v", lambda e: e.tensor_scalar(g8, lrow, m2, None, ALU.is_ge),
             [("lrow",), ("m2",)], [("keep",)])
        P.op("v", lambda e: e.tensor_scalar(nm1, m1, -1.0, None, ALU.mult),
             [("m1",)], [("nm1",)])
        P.op("s", lambda e: e.activation(e8, lrow, AF.Exp, bias=nm1),
             [("lrow",), ("nm1",)], [("e8",)])
        P.op("v", lambda e: e.tensor_tensor(e8, e8, g8, ALU.mult),
             [("e8",), ("keep",)], [("e8",)])
        P.op("v", lambda e: e.tensor_reduce(zz, e8, AX.X, ALU.add),
             [("e8",)], [("zz",)])
        P.op("v", lambda e: e.reciprocal(rz8, zz), [("zz",)], [("rz8",)])
        P.op("v", lambda e: e.tensor_scalar(e8, e8, rz8, None, ALU.mult),
             [("e8",), ("rz8",)], [("g8f",)])
        P.op("v", lambda e: e.tensor_tensor(keep2, g8, inv8, ALU.subtract),
             [("keep",), ("inv",)], [("keep2",)])
        P.op("v", lambda e: e.tensor_tensor(tmp8, eids_s[:], inv8, ALU.mult),
             [("eids",), ("inv",)], [("t8",)])
        P.op("v", lambda e: e.tensor_reduce(sc4[0:1, 0:1], tmp8, AX.X, ALU.add),
             [("t8",)], [("sea",)])
        P.op("v", lambda e: e.tensor_tensor(tmp8, eids_s[:], keep2, ALU.mult),
             [("eids",), ("keep2",)], [("t8",)])
        P.op("v", lambda e: e.tensor_reduce(sc4[0:1, 1:2], tmp8, AX.X, ALU.add),
             [("t8",)], [("seb",)])
        P.op("v", lambda e: e.tensor_tensor(tmp8, e8, inv8, ALU.mult),
             [("g8f",), ("inv",)], [("t8",)])
        P.op("v", lambda e: e.tensor_reduce(sc4[0:1, 2:3], tmp8, AX.X, ALU.add),
             [("t8",)], [("sga",)])
        P.op("v", lambda e: e.tensor_tensor(tmp8, e8, keep2, ALU.mult),
             [("g8f",), ("keep2",)], [("t8",)])
        P.op("v", lambda e: e.tensor_reduce(sc4[0:1, 3:4], tmp8, AX.X, ALU.add),
             [("t8",)], [("sgb",)])
        P.dma("gp", se2_d, sc4[0:1, 0:2], reads=[("sea",), ("seb",)],
              writes=[("se2d",)])
        P.dma("gp", g2_d, sc4[0:1, 2:4], reads=[("sga",), ("sgb",)],
              writes=[("g2d",)])
        P.dma("gp", sef_bc[:], bass.AP(se2_d.tensor, 0, [[0, 128], [1, 2]]),
              reads=[("se2d",)], writes=[("sefbc",)])
        P.dma("gp", gbc2[:], bass.AP(g2_d.tensor, 0, [[0, 128], [1, 2]]),
              reads=[("g2d",)], writes=[("gbc2",)])
        P.dma("gp", se21[:], se2_d.rearrange("one two -> two one"),
              reads=[("se2d",)], writes=[("se21",)])
        P.op("v", lambda e: e.tensor_copy(idx1e[:], se21[:]),
             [("se21",)], [("idx1",)])
        for sel in range(2):
            P.op("v", lambda e, sel=sel: e.tensor_scalar(
                stat[:, 14:15], sef_bc[:, sel:sel + 1], 1024.0, None, ALU.mult),
                [("sefbc",)], [S(14)])
            for dc in range(ND):
                P.op("v", lambda e, dc=dc: e.scalar_tensor_tensor(
                    stat[:, 15:16], iop_s[:], float(dc * 128), stat[:, 14:15],
                    ALU.add, ALU.add), [("iop",), S(14)], [S(15)])
                P.op("v", lambda e, sel=sel, dc=dc: e.tensor_copy(
                    idxg[:, sel * 8 + dc: sel * 8 + dc + 1], stat[:, 15:16]),
                    [S(15)], [("idxg", sel, dc)])
        P.flush()

    ph_spill_xnT(xnT_A, xnT0_d, "0")
    v_A = AT([128, NT * D], BF16, 0)
    ph_v(wv0, xnT0_d, v_A, "0", 152 * KB)

    def kept0(j):
        q0 = 128 * j
        out = []
        for c in range(NKC):
            k_lo, k_hi = 512 * c, 512 * c + 511
            if k_lo <= q0 + 127 and k_hi >= q0:
                mind = 0
            else:
                mind = min(abs(q0 - k_hi), abs(k_lo - (q0 + 127)))
            if mind >= WIN:
                continue
            if k_hi < q0:
                out.append((c, "l", float(np.float32(GAMMA ** float(q0 - k_lo)))))
            elif k_lo > q0 + 127:
                out.append((c, "r", float(np.float32(GAMMA ** float(k_lo - q0)))))
            else:
                out.append((c, "d", (q0 % 512) // 128))
        return out

    ph_attn(kT_A, v_A, qT0_d, attn0_d, "0", causal=False, kept_fn=kept0)

    ewf_sb = AT([128, ND * 1024], F32, 0)
    ew_sb = AT([128, 2 * ND * 1024], BF16, 32 * KB)
    ebf_sb = AT([2, 1024], F32, 128 * KB)
    ebr_sb = AT([1, 2 * 1024], F32, 132 * KB)
    eb_sb = AT([1, 2 * 1024], BF16, 140 * KB)
    xcD = [AT([128, ND * 128], BF16, 144 * KB + i * 2 * KB) for i in range(2)]
    accD = AT([128, D], F32, 148 * KB)
    gtD = AT([128, D], F32, 152 * KB)
    x0t = [AT([128, D], F32, 156 * KB + i * 4 * KB) for i in range(2)]
    a0t = [AT([128, D], F32, 164 * KB + i * 4 * KB) for i in range(2)]
    for sel in range(2):
        for dc in range(ND):
            P.op("gp", lambda e, sel=sel, dc=dc: e.indirect_dma_start(
                ewf_sb[:, dc * 1024:(dc + 1) * 1024], None, ew_g,
                bass.IndirectOffsetOnAxis(
                    ap=idxg[:, sel * 8 + dc: sel * 8 + dc + 1], axis=0)),
                reads=[("idxg", sel, dc)], writes=[("ewf", dc)])
        P.op("v", lambda e, sel=sel: e.tensor_copy(
            ew_sb[:, sel * ND * 1024:(sel + 1) * ND * 1024], ewf_sb[:]),
            [("ewf", dc) for dc in range(ND)], [("ew", sel)])
    P.op("gp", lambda e: e.indirect_dma_start(
        ebf_sb[:], None, eb0,
        bass.IndirectOffsetOnAxis(ap=idx1e[:], axis=0)),
        reads=[("idx1",)], writes=[("ebf", 0)])
    P.dma("gp", eb2_d, ebf_sb[:], reads=[("ebf", 0)], writes=[("eb2d",)])
    P.dma("ld", ebr_sb[:],
          bass.AP(eb2_d.tensor, 0, [[2048, 1], [1, 2048]]),
          reads=[("eb2d",)], writes=[("ebr",)])
    P.op("v", lambda e: e.tensor_copy(eb_sb[:], ebr_sb[:]),
         [("ebr",)], [("eb",)])
    with (nc.psum_tensor([128, 1024], F32) as pe0,
          nc.psum_tensor([128, 1024], F32) as pe1):
        for j in range(NT):
            xb = xcD[j % 2]
            kx = ("Dxc", j % 2)
            P.dma("ld", xb[:].rearrange("p (dc k) -> p dc k", dc=ND),
                  xnT0_d[:, :, j * 128:(j + 1) * 128]
                  .rearrange("dc p k -> p dc k"),
                  reads=[("xnTd0", dc) for dc in range(ND)], writes=[kx])
            P.dma("ld", x0t[j % 2][:], x0[j * 128:(j + 1) * 128, :],
                  writes=[("Dx0", j % 2)])
            P.dma("ld", a0t[j % 2][:], attn0_d[j * 128:(j + 1) * 128, :],
                  reads=[("attn_d0", j)], writes=[("Da0", j % 2)])
            for sel in range(2):
                pe = (pe0, pe1)[sel]
                kp = ("Dpe", sel)
                for n in range(2):
                    P.op("t", lambda e, pe=pe, sel=sel, n=n: e.matmul(
                        pe[:, n * 512:(n + 1) * 512], ones_bf[:],
                        eb_sb[0:1, sel * 1024 + n * 512:
                              sel * 1024 + (n + 1) * 512],
                        start=True, stop=False),
                        [("eb",), KONE], [kp])
                for dc in range(ND):
                    for n in range(2):
                        P.op("t", lambda e, pe=pe, xb=xb, sel=sel, dc=dc, n=n:
                             e.matmul(pe[:, n * 512:(n + 1) * 512],
                                      xb[:, dc * 128:(dc + 1) * 128],
                                      ew_sb[:, sel * ND * 1024 + dc * 1024 +
                                            n * 512:
                                            sel * ND * 1024 + dc * 1024 +
                                            (n + 1) * 512],
                                      start=False, stop=(dc == ND - 1)),
                             [("ew", sel), kx], [kp])
                P.op("s", lambda e, pe=pe: e.activation(gtD[:], pe[:], AF.Gelu),
                     [kp], [("Dgt",)])
                if sel == 0:
                    P.op("v", lambda e: e.tensor_scalar(
                        accD[:], gtD[:], gbc2[:, 0:1], None, ALU.mult),
                        [("Dgt",), ("gbc2",)], [("Dacc",)])
                else:
                    P.op("v", lambda e: e.scalar_tensor_tensor(
                        accD[:], gtD[:], gbc2[:, 1:2], accD[:],
                        ALU.mult, ALU.add),
                        [("Dgt",), ("gbc2",), ("Dacc",)], [("Dacc",)])
            P.op("v", lambda e, j=j: e.tensor_tensor(
                accD[:], accD[:], x0t[j % 2][:], ALU.add),
                [("Dacc",), ("Dx0", j % 2)], [("Dacc",)])
            P.op("v", lambda e, j=j: e.tensor_tensor(
                x0t[j % 2][:], accD[:], a0t[j % 2][:], ALU.add),
                [("Dacc",), ("Da0", j % 2)], [("Dx0", j % 2)])
            P.dma("so", x1_d[j * 128:(j + 1) * 128, :], x0t[j % 2][:],
                  reads=[("Dx0", j % 2)], writes=[("x1d", j)])
        P.flush()

    xnT_E = AT([128, ND * T], BF16, 0)
    kT_E = AT([128, ND * T], BF16, 64 * KB)
    ph_ln(x1_d, xnT_E, "1", src_key=lambda j: ("x1d", j), xnf_d=xn1f_d)
    ph_plain_kq(wk1, wq1, xnT_E, kT_E, qT1_d, "1")
    ph_spill_xnT(xnT_E, xn1T_d, "1")
    v_E = AT([128, NT * D], BF16, 0)
    ph_v(wv1, xn1T_d, v_E, "1", 160 * KB)

    def kept1(j):
        return [(c, "c", None) for c in range((j * 128 + 128 + 511) // 512)]

    ph_attn(kT_E, v_E, qT1_d, attn1_d, "1", causal=True, kept_fn=kept1)


    wTp = AT([128, NHT * 128], BF16, 0)
    gmb = AT([128, D], F32, 96 * KB)
    bmb = AT([128, D], F32, 100 * KB)
    x1o = [AT([128, D], F32, 104 * KB + i * 4 * KB) for i in range(2)]
    xn2n = [AT([128, D], F32, 112 * KB), AT([128, D], F32, 133 * KB)]
    xn2b = [AT([128, D], BF16, 116 * KB), AT([128, D], BF16, 137 * KB)]
    x2T = [AT([128, ND * 128], BF16, 118 * KB),
           AT([128, ND * 128], BF16, 139 * KB)]
    wr_sb = AT([128, ND * 8], BF16, 120 * KB)
    br_sb = AT([1, 8], BF16, 121 * KB)
    w_all = AT([128, NHT * 8], F32, 122 * KB)
    probs = [AT([128, 8], F32, 130 * KB), AT([128, 8], F32, 141 * KB)]
    g0scr = [AT([128, D], F32, 124 * KB), AT([128, D], F32, 142 * KB)]
    tb8 = [AT([128, 16], F32, 131 * KB), AT([128, 16], F32, 146 * KB)]
    P.dma("ld", gmb[:], gamma_d, writes=[("gmb",)])
    P.dma("ld", bmb[:], beta_d, writes=[("bmb",)])
    P.dma("gp", wr_sb[:].rearrange("p (dc e) -> p dc e", dc=ND),
          Wr_d.rearrange("(dc p) e -> p dc e", p=128), writes=[("wr",)])
    P.dma("gp", br_sb[:], br_d, writes=[("brs",)])
    P.op("v", lambda e: e.memset(wTp[:], 0.0), writes=[("wTpz",)])
    with (nc.psum_tensor([128, 128], BF16) as pt0,
          nc.psum_tensor([128, 128], BF16) as pt1,
          nc.psum_tensor([128, 8], F32) as pr_,
          nc.psum_tensor([128, 128], F32) as pw_):
        for r in range(NHT):
            par = r % 2
            b = 16 * par
            rb = 32 + 16 * par
            xb = x1o[par]
            xn2np, xn2bp, x2Tp = xn2n[par], xn2b[par], x2T[par]
            probp, g0scrp, tb8p = probs[par], g0scr[par], tb8[par]
            kx = ("G0x", par)
            P.op("gp", lambda e, xb=xb, r=r: dyn_load(e, xb[:], xn1f_d, r),
                reads=[("idx",)] + [("xnf_d", jj) for jj in range(NT)],
                writes=[kx])
            P.op("v", lambda e, xb=xb, b=b, xn2np=xn2np, g0scrp=g0scrp: e.tensor_reduce(
                stat[:, b + 0:b + 1], xb[:], AX.X, ALU.add), [kx], [S(b + 0)])
            P.op("s", lambda e, xb=xb, b=b, xn2np=xn2np, g0scrp=g0scrp: e.activation(
                g0scrp[:], xb[:], AF.Square, accum_out=stat[:, b + 1:b + 2]),
                [kx], [("G0scr", par), S(b + 1)])
            P.op("v", lambda e, b=b, rb=rb, xn2np=xn2np, xn2bp=xn2bp, x2Tp=x2Tp, probp=probp, g0scrp=g0scrp, tb8p=tb8p: e.tensor_scalar(
                stat[:, b + 2:b + 3], stat[:, b + 0:b + 1], 1.0 / D, None, ALU.mult),
                [S(b + 0)], [S(b + 2)])
            P.op("v", lambda e, b=b, rb=rb, xn2np=xn2np, xn2bp=xn2bp, x2Tp=x2Tp, probp=probp, g0scrp=g0scrp, tb8p=tb8p: e.tensor_tensor(
                stat[:, b + 3:b + 4], stat[:, b + 2:b + 3], stat[:, b + 2:b + 3], ALU.mult),
                [S(b + 2)], [S(b + 3)])
            P.op("v", lambda e, b=b, rb=rb, xn2np=xn2np, xn2bp=xn2bp, x2Tp=x2Tp, probp=probp, g0scrp=g0scrp, tb8p=tb8p: e.tensor_scalar(
                stat[:, b + 4:b + 5], stat[:, b + 1:b + 2], 1.0 / D, None, ALU.mult),
                [S(b + 1)], [S(b + 4)])
            P.op("v", lambda e, b=b, rb=rb, xn2np=xn2np, xn2bp=xn2bp, x2Tp=x2Tp, probp=probp, g0scrp=g0scrp, tb8p=tb8p: e.tensor_tensor(
                stat[:, b + 5:b + 6], stat[:, b + 4:b + 5], stat[:, b + 3:b + 4], ALU.subtract),
                [S(b + 3), S(b + 4)], [S(b + 5)])
            P.op("v", lambda e, b=b, rb=rb, xn2np=xn2np, xn2bp=xn2bp, x2Tp=x2Tp, probp=probp, g0scrp=g0scrp, tb8p=tb8p: e.tensor_scalar(
                stat[:, b + 5:b + 6], stat[:, b + 5:b + 6], 1e-5, None, ALU.add),
                [S(b + 5)], [S(b + 5)])
            P.op("s", lambda e, b=b, rb=rb, xn2np=xn2np, xn2bp=xn2bp, x2Tp=x2Tp, probp=probp, g0scrp=g0scrp, tb8p=tb8p: e.activation(
                stat[:, b + 6:b + 7], stat[:, b + 5:b + 6], AF.Sqrt), [S(b + 5)], [S(b + 6)])
            P.op("v", lambda e, b=b, rb=rb, xn2np=xn2np, xn2bp=xn2bp, x2Tp=x2Tp, probp=probp, g0scrp=g0scrp, tb8p=tb8p: e.reciprocal(stat[:, b + 7:b + 8], stat[:, b + 6:b + 7]),
                 [S(b + 6)], [S(b + 7)])
            P.op("v", lambda e, b=b, rb=rb, xn2np=xn2np, xn2bp=xn2bp, x2Tp=x2Tp, probp=probp, g0scrp=g0scrp, tb8p=tb8p: e.scalar_tensor_tensor(
                stat[:, b + 8:b + 9], stat[:, b + 2:b + 3], -1.0, stat[:, b + 7:b + 8],
                ALU.mult, ALU.mult), [S(b + 2), S(b + 7)], [S(b + 8)])
            P.op("s", lambda e, xb=xb, b=b, xn2np=xn2np, g0scrp=g0scrp: e.activation(
                xn2np[:], xb[:], AF.Identity, bias=stat[:, b + 8:b + 9], scale=stat[:, b + 7:b + 8]),
                [kx, S(b + 7), S(b + 8)], [("G0n", par)])
            P.op("v", lambda e, b=b, rb=rb, xn2np=xn2np, xn2bp=xn2bp, x2Tp=x2Tp, probp=probp, g0scrp=g0scrp, tb8p=tb8p: e.tensor_tensor(
                xn2np[:], xn2np[:], gmb[:], ALU.mult),
                [("G0n", par), ("gmb",)], [("G0n", par)])
            P.op("v", lambda e, b=b, rb=rb, xn2np=xn2np, xn2bp=xn2bp, x2Tp=x2Tp, probp=probp, g0scrp=g0scrp, tb8p=tb8p: e.tensor_tensor(
                xn2bp[:], xn2np[:], bmb[:], ALU.add),
                [("G0n", par), ("bmb",)], [("G0b", par)])
            for dc in range(ND):
                pp = (pt0, pt1)[dc % 2]
                kp = ("G0pt", dc % 2)
                P.op("t", lambda e, pp=pp, dc=dc, xn2bp=xn2bp: e.transpose(
                    pp[:], xn2bp[:, dc * 128:(dc + 1) * 128], ident_bf[:]),
                    [("G0b", par), KID], [kp])
                P.op("v", lambda e, pp=pp, dc=dc, x2Tp=x2Tp: e.tensor_copy(
                    x2Tp[:, dc * 128:(dc + 1) * 128], pp[:]),
                    [kp], [("G0T", par, dc)])
            for dc in range(ND):
                P.dma("so", xn2T_d[dc][:, r * 128:(r + 1) * 128],
                      x2Tp[:, dc * 128:(dc + 1) * 128],
                      reads=[("G0T", par, dc)], writes=[("x2Td", r, dc)])
            P.op("t", lambda e, b=b, rb=rb, xn2np=xn2np, xn2bp=xn2bp, x2Tp=x2Tp, probp=probp, g0scrp=g0scrp, tb8p=tb8p: e.matmul(pr_[:], ones_bf[:], br_sb[:],
                                         start=True, stop=False),
                 [("brs",), KONE], [("G0pr",)])
            for dc in range(ND):
                P.op("t", lambda e, dc=dc, x2Tp=x2Tp: e.matmul(
                    pr_[:], x2Tp[:, dc * 128:(dc + 1) * 128],
                    wr_sb[:, dc * 8:(dc + 1) * 8],
                    start=False, stop=(dc == ND - 1)),
                    [("wr",), ("G0T", par, dc)], [("G0pr",)])
            P.op("v", lambda e, b=b, rb=rb, xn2np=xn2np, xn2bp=xn2bp, x2Tp=x2Tp, probp=probp, g0scrp=g0scrp, tb8p=tb8p: e.tensor_copy(probp[:], pr_[:]),
                 [("G0pr",)], [("G0p", par)])
            P.op("v", lambda e, b=b, rb=rb, xn2np=xn2np, xn2bp=xn2bp, x2Tp=x2Tp, probp=probp, g0scrp=g0scrp, tb8p=tb8p: e.tensor_reduce(
                stat[:, rb + 0:rb + 1], probp[:], AX.X, ALU.max), [("G0p", par)], [S(9)])
            P.op("v", lambda e, b=b, rb=rb, xn2np=xn2np, xn2bp=xn2bp, x2Tp=x2Tp, probp=probp, g0scrp=g0scrp, tb8p=tb8p: e.tensor_scalar(
                stat[:, rb + 1:rb + 2], stat[:, rb + 0:rb + 1], -1.0, None, ALU.mult),
                [S(9)], [S(10)])
            P.op("s", lambda e, b=b, rb=rb, xn2np=xn2np, xn2bp=xn2bp, x2Tp=x2Tp, probp=probp, g0scrp=g0scrp, tb8p=tb8p: e.activation(
                probp[:], probp[:], AF.Exp, bias=stat[:, rb + 1:rb + 2],
                accum_out=stat[:, rb + 2:rb + 3]),
                [("G0p", par), S(10)], [("G0p", par), S(11)])
            P.op("v", lambda e, b=b, rb=rb, xn2np=xn2np, xn2bp=xn2bp, x2Tp=x2Tp, probp=probp, g0scrp=g0scrp, tb8p=tb8p: e.reciprocal(stat[:, rb + 3:rb + 4], stat[:, rb + 2:rb + 3]),
                 [S(11)], [S(12)])
            P.op("v", lambda e, b=b, rb=rb, xn2np=xn2np, xn2bp=xn2bp, x2Tp=x2Tp, probp=probp, g0scrp=g0scrp, tb8p=tb8p: e.tensor_scalar(
                probp[:], probp[:], stat[:, rb + 3:rb + 4], None, ALU.mult),
                [("G0p", par), S(12)], [("G0p", par)])
            P.op("v", lambda e, b=b, rb=rb, xn2np=xn2np, xn2bp=xn2bp, x2Tp=x2Tp, probp=probp, g0scrp=g0scrp, tb8p=tb8p: e.tensor_reduce(
                stat[:, rb + 0:rb + 1], probp[:], AX.X, ALU.max), [("G0p", par)], [S(9)])
            P.op("v", lambda e, b=b, rb=rb, xn2np=xn2np, xn2bp=xn2bp, x2Tp=x2Tp, probp=probp, g0scrp=g0scrp, tb8p=tb8p: e.tensor_scalar(
                tb8p[:, 0:8], probp[:], stat[:, rb + 0:rb + 1], None, ALU.is_ge),
                [("G0p", par), S(9)], [("G0i", par)])
            P.op("v", lambda e, b=b, rb=rb, xn2np=xn2np, xn2bp=xn2bp, x2Tp=x2Tp, probp=probp, g0scrp=g0scrp, tb8p=tb8p: e.scalar_tensor_tensor(
                tb8p[:, 8:16], tb8p[:, 0:8], NEG, probp[:], ALU.mult, ALU.add),
                [("G0i", par), ("G0p", par)], [("G0m", par)])
            P.op("v", lambda e, b=b, rb=rb, xn2np=xn2np, xn2bp=xn2bp, x2Tp=x2Tp, probp=probp, g0scrp=g0scrp, tb8p=tb8p: e.tensor_reduce(
                stat[:, rb + 1:rb + 2], tb8p[:, 8:16], AX.X, ALU.max),
                [("G0m", par)], [S(10)])
            P.op("v", lambda e, b=b, rb=rb, xn2np=xn2np, xn2bp=xn2bp, x2Tp=x2Tp, probp=probp, g0scrp=g0scrp, tb8p=tb8p: e.tensor_scalar(
                tb8p[:, 0:8], probp[:], stat[:, rb + 1:rb + 2], None, ALU.is_ge),
                [("G0p", par), S(10)], [("G0i", par)])
            P.op("v", lambda e, r=r, probp=probp, tb8p=tb8p: e.tensor_tensor(
                w_all[:, r * 8:(r + 1) * 8], probp[:], tb8p[:, 0:8], ALU.mult),
                [("G0p", par), ("G0i", par)], [("w", r)])
            P.op("t", lambda e, r=r, probp=probp, tb8p=tb8p: e.transpose(
                pw_[0:8, :], w_all[:, r * 8:(r + 1) * 8], ident_f[:]),
                [("w", r), KIDF], [("G0pw",)])
            P.op("v", lambda e, r=r, probp=probp, tb8p=tb8p: e.tensor_copy(
                wTp[0:8, r * 128:(r + 1) * 128], pw_[0:8, :]),
                [("G0pw",), ("wTpz",)], [("wT", r)])
        P.flush()

    xn2h = AT([128, ND * 1024], BF16, 4 * KB)
    h_sb = AT([128, NF * 1024], BF16, 20 * KB)
    w2_sb = AT([128, NF * 512], BF16, 84 * KB)
    ctl1 = AT([1, 512], F32, 116 * KB)
    cppb = AT([1, 1024], BF16, 118 * KB)
    cppP = AT([128, D], BF16, 120 * KB)
    ct_ld = AT([1, 1024], F32, 123 * KB)
    cacc = AT([1, 1024], F32, 127 * KB)
    b1_sb = AT([128, NF * 8], F32, 131 * KB)
    acc_h = AT([128, 8 * D], F32, 132 * KB)
    w1st = [AT([128, ND * 512], BF16, 164 * KB + i * 8 * KB) for i in range(2)]
    x2t = AT([128, D], F32, 180 * KB)
    f1t = [AT([128, D], F32, 184 * KB), AT([128, D], F32, 184 * KB)]
    hpd = AT([128, NF], BF16, 188 * KB)

    P.dma("ld", b1_sb[:].rearrange("p (m e) -> p m e", m=NF),
          b1T_d.rearrange("(m p) e -> p m e", p=128), writes=[("b1",)])
    P.op("v", lambda e: e.memset(cppP[:], 0.0), writes=[("cppz",)])
    P.flush()

    with (nc.psum_tensor([128, 1024], F32) as ph0,
          nc.psum_tensor([128, 1024], F32) as ph1,
          nc.psum_tensor([128, 512], F32) as po0,
          nc.psum_tensor([128, 512], F32) as po1):
        for half in range(2):
            for dc in range(ND):
                P.op("ld", lambda e, dc=dc, half=half: e.dma_start(
                    out=xn2h[:, dc * 1024:(dc + 1) * 1024],
                    in_=xn2T_d[dc][:, half * 1024:(half + 1) * 1024]),
                    reads=[("x2Td", rr, dc) for rr in range(NHT)],
                    writes=[("xn2h", dc)])
            for e_ in range(8):
                if half == 0:
                    P.op("s", lambda e, e_=e_: e.activation(
                        hpd[:], b1_sb[:].rearrange(
                            "p (m e) -> p m e", m=NF)[:, :, e_], AF.Relu),
                        [("b1",)], [("hpd",)])
                for fg in range(NF // 4):
                    wb = w1st[fg % 2]
                    kw = ("w1", fg % 2)
                    P.op("ld", lambda e, wb=wb, e_=e_, fg=fg: e.dma_start(
                        out=wb[:].rearrange("p (dc f) -> p dc f", dc=ND),
                        in_=w1b_d[e_][:, fg * 512:(fg + 1) * 512]
                        .rearrange("(dc p) f -> p dc f", p=128)),
                        writes=[kw], pre=[(wbg, "wbg", 256)])
                    for fs in range(4):
                        ft = fg * 4 + fs
                        ph = (ph0, ph1)[ft % 2]
                        kp = ("ph", ft % 2)
                        for dc in range(ND):
                            for n in range(2):
                                P.op("t", lambda e, ph=ph, wb=wb, dc=dc, n=n,
                                     fs=fs:
                                     e.matmul(ph[:, n * 512:(n + 1) * 512],
                                              wb[:, dc * 512 + fs * 128:
                                                 dc * 512 + (fs + 1) * 128],
                                              xn2h[:, dc * 1024 + n * 512:
                                                   dc * 1024 + (n + 1) * 512],
                                              start=(dc == 0),
                                              stop=(dc == ND - 1)),
                                     [kw, ("xn2h", dc)], [kp])
                        P.op("s", lambda e, ph=ph, ft=ft, e_=e_: e.activation(
                            h_sb[:, ft * 1024:(ft + 1) * 1024], ph[:], AF.Relu,
                            bias=b1_sb[:, ft * 8 + e_: ft * 8 + e_ + 1]),
                            [kp, ("b1",)], [("h", ft)])
                for h2c in range(2):
                    P.op("ld", lambda e, e_=e_, h2c=h2c: e.dma_start(
                        out=w2_sb[:].rearrange("p (fc h) -> p fc h", fc=NF),
                        in_=w2b_d[e_][:, h2c * 512:(h2c + 1) * 512]
                        .rearrange("(fc p) h -> p fc h", p=128)),
                        writes=[("w2",)], pre=[(wbg, "wbg", 256)])
                    for r in range(8):
                        po = (po0, po1)[r % 2]
                        kp = ("po", r % 2)
                        for ft in range(NF):
                            P.op("t", lambda e, po=po, ft=ft, r=r: e.matmul(
                                po[:], h_sb[:, ft * 1024 + r * 128:
                                            ft * 1024 + (r + 1) * 128],
                                w2_sb[:, ft * 512:(ft + 1) * 512],
                                start=(ft == 0), stop=(ft == NF - 1)),
                                [("w2",), ("h", ft)], [kp])
                        rg = half * 8 + r
                        if e_ == 0:
                            P.op("v", lambda e, po=po, r=r, h2c=h2c, rg=rg:
                                 e.tensor_scalar(
                                     acc_h[:, r * D + h2c * 512:
                                           r * D + h2c * 512 + 512],
                                     po[:], w_all[:, rg * 8: rg * 8 + 1],
                                     None, ALU.mult),
                                 [kp, ("w", rg)], [("acc", r, h2c)])
                        else:
                            P.op("v", lambda e, po=po, r=r, h2c=h2c, rg=rg,
                                 e_=e_: e.scalar_tensor_tensor(
                                     acc_h[:, r * D + h2c * 512:
                                           r * D + h2c * 512 + 512],
                                     po[:],
                                     w_all[:, rg * 8 + e_: rg * 8 + e_ + 1],
                                     acc_h[:, r * D + h2c * 512:
                                           r * D + h2c * 512 + 512],
                                     ALU.mult, ALU.add),
                                 [kp, ("w", rg), ("acc", r, h2c)],
                                 [("acc", r, h2c)])
                    if half == 0:
                        po = po0
                        kp = ("po", 0)
                        for ft in range(NF):
                            P.op("t", lambda e, po=po, ft=ft: e.matmul(
                                po[0:1, :], hpd[:, ft:ft + 1],
                                w2_sb[:, ft * 512:(ft + 1) * 512],
                                start=(ft == 0), stop=(ft == NF - 1)),
                                [("hpd",), ("w2",)], [kp])
                        P.op("v", lambda e, po=po: e.tensor_copy(
                            ctl1[:], po[0:1, :]), [kp], [("ctl1",)])
                        P.dma("gp", ctil_d[e_:e_ + 1, h2c * 512:(h2c + 1) * 512],
                              ctl1[:], reads=[("ctl1",)],
                              writes=[("ctd", e_, h2c)])
            if half == 0:
                ctr = [("ctd", e2, h2) for e2 in range(8) for h2 in range(2)]
                P.dma("ld", cacc[:], ctil_d[0:1, :], reads=ctr,
                      writes=[("cacc",)])
                for e2 in range(1, 8):
                    P.dma("ld", ct_ld[:], ctil_d[e2:e2 + 1, :], reads=ctr,
                          writes=[("ctld",)])
                    P.op("v", lambda e: e.tensor_tensor(
                        cacc[:], cacc[:], ct_ld[:], ALU.add),
                        [("cacc",), ("ctld",)], [("cacc",)])
                for e2 in range(8):
                    P.dma("ld", ct_ld[:], b2_d[e2:e2 + 1, :], writes=[("ctld",)])
                    P.op("v", lambda e: e.tensor_tensor(
                        cacc[:], cacc[:], ct_ld[:], ALU.add),
                        [("cacc",), ("ctld",)], [("cacc",)])
                for e2 in range(8):
                    P.dma("ld", ct_ld[:], ctil_d[e2:e2 + 1, :], reads=ctr,
                          writes=[("ctld",)])
                    P.op("v", lambda e: e.tensor_tensor(
                        cppb[:], cacc[:], ct_ld[:], ALU.subtract),
                        [("cacc",), ("ctld",)], [("cppb",)])
                    P.dma("gp", cpp_d[e2:e2 + 1, :], cppb[:],
                          reads=[("cppb",)], writes=[("cppd", e2)])
                P.dma("ld", cppP[0:8, :], cpp_d[:, :],
                      reads=[("cppd", e2) for e2 in range(8)] + [("cppz",)],
                      writes=[("cppP",)])
            for r in range(8):
                rg = half * 8 + r
                for n in range(2):
                    P.op("t", lambda e, po=(po0, po1)[n], rg=rg, n=n:
                         e.matmul(po[:],
                                  wTp[:, rg * 128:(rg + 1) * 128],
                                  cppP[:, n * 512:(n + 1) * 512],
                                  start=True, stop=True),
                         [("wT", rg), ("cppP",)], [("po", n)])
                for n in range(2):
                    P.op("v", lambda e, po=(po0, po1)[n], r=r, n=n:
                         e.scalar_tensor_tensor(
                             acc_h[:, r * D + n * 512: r * D + (n + 1) * 512],
                             po[:], 1.0,
                             acc_h[:, r * D + n * 512: r * D + (n + 1) * 512],
                             ALU.mult, ALU.add),
                         [("po", n), ("acc", r, 0), ("acc", r, 1)],
                         [("acc", r, n)])
                fb = f1t[0]
                kf = ("f1", 0)
                P.op("gp", lambda e, fb=fb, rg=rg: dyn_load(e, fb[:], x1_d, rg),
                    reads=[("idx",)] + [("x1d", jj) for jj in range(NT)],
                    writes=[kf])
                P.op("v", lambda e, fb=fb, r=r: e.tensor_tensor(
                    x2t[:], acc_h[:, r * D:(r + 1) * D], fb[:], ALU.add),
                    [("acc", r, 0), ("acc", r, 1), kf], [("x2t",)])
                P.op("gp", lambda e, fb=fb, rg=rg: dyn_load(
                    e, fb[:], attn1_d, rg),
                    reads=[("idx",), ("x2t",)] +
                          [("attn_d1", jj) for jj in range(NT)],
                    writes=[kf])
                P.op("v", lambda e, fb=fb: e.tensor_tensor(
                    x2t[:], x2t[:], fb[:], ALU.add), [("x2t",), kf], [("x2t",)])
                P.op("gp", lambda e, fb=fb, rg=rg: dyn_load(
                    e, fb[:], xn1f_d, rg),
                    reads=[("idx",), ("x2t",)] +
                          [("xnf_d", jj) for jj in range(NT)],
                    writes=[kf])
                P.op("v", lambda e, fb=fb: e.tensor_tensor(
                    x2t[:], x2t[:], fb[:], ALU.add), [("x2t",), kf], [("x2t",)])
                P.dma("so", x2own[rg * 128:(rg + 1) * 128, :], x2t[:],
                      reads=[("x2t",)], writes=[("out", rg)])
        P.flush()

    P.op("ld", lambda e: e.nop(),
         deps=[P.lastw[("out", rg)] for rg in range(NHT)])
    P.flush()
    return nc


def kernel(**inputs):
    tabs = _host_tables()
    g = _prep_inputs(inputs)
    nc = build_program()
    X = inputs["L0_attn_X"]
    base = {}
    base.update(tabs)
    base.update({"wq0d": g["wq0d"], "wk0d": g["wk0d"], "wv0": g["wv0"],
                 "gwT": g["gwT"], "gbT": g["gbT"], "ew_g": g["ew_g"],
                 "eb0": g["eb0"], "eids_row": tabs["eids_row"],
                 "iota_p": tabs["iota_p"], "wq1": g["wq1"], "wk1": g["wk1"],
                 "wv1": g["wv1"], "gamma_bc": g["gamma_bc"],
                 "beta_bc": g["beta_bc"], "Wr": g["Wr"], "br_row": g["br_row"],
                 "W1": g["W1"], "b1T": g["b1T"], "W2": g["W2"], "b2": g["b2"]})
    in_maps = []
    for c in range(8):
        m = dict(base)
        m["x0"] = np.ascontiguousarray(X[c // 2])
        h = c % 2
        idx = np.empty((128, NHT), dtype=np.int32)
        for r in range(NHT):
            idx[:, r] = h * HTOK + r * 128 + np.arange(128)
        m["idx_own"] = idx
        in_maps.append(m)
    res = run_bass_kernel_spmd(nc, in_maps, core_ids=list(range(8)))
    out = np.empty((B, T, D), dtype=np.float32)
    for b in range(B):
        out[b, 0:HTOK] = res.results[2 * b]["x2own"]
        out[b, HTOK:T] = res.results[2 * b + 1]["x2own"]
    return out




# version 18
# speedup vs baseline: 1.0505x; 1.0505x over previous
import numpy as np
import ml_dtypes
import concourse.bass as bass
import concourse.mybir as mybir
from concourse.bass_utils import run_bass_kernel_spmd

F32 = mybir.dt.float32
BF16 = mybir.dt.bfloat16
AF = mybir.ActivationFunctionType
ALU = mybir.AluOpType
AX = mybir.AxisListType

B, T, D = 4, 4096, 1024
NT = T // 128
NKC = T // 512
ND = D // 128
HTOK = T // 2
NHT = HTOK // 128
GAMMA = 0.9
WIN = 1024
FF = 4096
NF = FF // 128
NEG = -1e30
BF_NP = ml_dtypes.bfloat16
KB = 1024


def _host_tables():
    import jax
    import jax.numpy as jnp
    half = D // 2
    try:
        _cpu = jax.devices("cpu")[0]
    except Exception:
        _cpu = None
    import contextlib
    with (jax.default_device(_cpu) if _cpu is not None
          else contextlib.nullcontext()):
        scale_vec = (jnp.arange(0, D, 2, dtype=jnp.float32) + 0.4 * D) / (1.4 * D)
        power = jnp.arange(T, dtype=jnp.float32) / 512.0
        scale = scale_vec[None, :] ** power[:, None]
        inv_scale = 1.0 / scale
        inv_freq = 1.0 / 10000.0 ** (jnp.arange(half, dtype=jnp.float32) / half)
        sinusoid = jnp.arange(T, dtype=jnp.float32)[:, None] * inv_freq[None, :]
        cosq = np.asarray(jnp.cos(sinusoid) * scale)
        sinq = np.asarray(jnp.sin(sinusoid) * scale)
        cosk = np.asarray(jnp.cos(sinusoid) * inv_scale)
        sink = np.asarray(jnp.sin(sinusoid) * inv_scale)
    t = {}
    t["cosq"] = np.ascontiguousarray(cosq.T)
    t["sinq"] = np.ascontiguousarray(sinq.T)
    t["cosk"] = np.ascontiguousarray(cosk.T)
    t["sink"] = np.ascontiguousarray(sink.T)
    i = np.arange(128, dtype=np.float64)
    j = np.arange(512, dtype=np.float64)
    t["gvpos"] = (GAMMA ** i).astype(np.float32).reshape(128, 1)
    t["gvneg"] = (GAMMA ** (-i)).astype(np.float32).reshape(128, 1)
    t["gneg"] = np.ascontiguousarray(
        np.broadcast_to((GAMMA ** (-j)).astype(np.float32), (128, 512)))
    t["gpos"] = np.ascontiguousarray(
        np.broadcast_to((GAMMA ** j).astype(np.float32), (128, 512)))
    dd = np.zeros((128, 4 * 512), dtype=np.float32)
    for r in range(4):
        dd[:, r * 512:(r + 1) * 512] = (
            GAMMA ** np.abs(128.0 * r + i[:, None] - j[None, :])
        ).astype(np.float32)
    t["ddiag"] = dd
    cm = np.zeros((128, 128), dtype=np.float32)
    cm[np.triu_indices(128, 1)] = NEG
    t["cmask"] = cm
    t["z32"] = np.zeros((128, 1), dtype=np.float32)
    i = np.arange(128)[:, None]
    c = np.arange(256)[None, :]
    t["cmask2_h0"] = np.where(c <= i, 0.0, NEG).astype(np.float32)
    t["cmask2_h1"] = np.where(c <= 128 + i, 0.0, NEG).astype(np.float32)
    t["ident_bf"] = np.eye(128, dtype=BF_NP)
    t["ident_f"] = np.eye(128, dtype=np.float32)
    t["ones_row"] = np.ones((1, 128), dtype=BF_NP)
    t["eids_row"] = np.arange(8, dtype=np.float32).reshape(1, 8)
    t["iota_p"] = np.arange(128, dtype=np.float32).reshape(128, 1)
    return t


def _prep_inputs(inp):
    g = {}
    wq, wk = inp["L0_attn_W_Q"], inp["L0_attn_W_K"]
    g["wq0d"] = np.ascontiguousarray(np.concatenate([wq[:, 0::2], wq[:, 1::2]], 1))
    g["wk0d"] = np.ascontiguousarray(np.concatenate([wk[:, 0::2], wk[:, 1::2]], 1))
    g["wv0"] = np.ascontiguousarray(inp["L0_attn_W_V"])
    g["gwT"] = np.ascontiguousarray(inp["L0_ffn_gate_w"].T)
    g["gbT"] = np.ascontiguousarray(inp["L0_ffn_gate_b"].reshape(8, 1))
    g["ew_g"] = np.ascontiguousarray(
        np.concatenate([inp["L0_ffn_expert_w"][e].T for e in range(8)], axis=0))
    g["eb0"] = np.ascontiguousarray(inp["L0_ffn_expert_b"])
    g["wq1"] = np.ascontiguousarray(inp["L1_attn_w_q"])
    g["wk1"] = np.ascontiguousarray(inp["L1_attn_w_k"])
    g["wv1"] = np.ascontiguousarray(inp["L1_attn_w_v"])
    g["gamma_bc"] = np.ascontiguousarray(np.broadcast_to(inp["L1_ffn_gamma"], (128, D)))
    g["beta_bc"] = np.ascontiguousarray(np.broadcast_to(inp["L1_ffn_beta"], (128, D)))
    g["Wr"] = np.ascontiguousarray(inp["L1_ffn_Wr"])
    g["br_row"] = np.ascontiguousarray(inp["L1_ffn_br"].reshape(1, 8))
    g["W1"] = np.ascontiguousarray(inp["L1_ffn_W1"])
    g["b1T"] = np.ascontiguousarray(inp["L1_ffn_b1"].T)
    g["W2"] = np.ascontiguousarray(inp["L1_ffn_W2"])
    g["b2"] = np.ascontiguousarray(inp["L1_ffn_b2"])
    return g


class Prog:

    QSEM = {"ld": 16, "gp": 16, "so": 16, "t": 1, "v": 1, "s": 1}
    SEC = {"ld": "sync", "gp": "gpsimd", "so": "scalar", "s": "scalar",
           "t": "tensor", "v": "vector"}
    DMAQ = ("ld", "gp", "so")
    NPOOL = 16

    def __init__(self, nc):
        self.nc = nc
        self.sem = {q: nc.alloc_semaphore("q_" + q) for q in self.QSEM}
        self.done = {q: nc.alloc_semaphore("d_" + q) for q in self.DMAQ}
        self.pool = {q: [nc.alloc_semaphore(f"p_{q}{i}")
                         for i in range(self.NPOOL)] for q in self.DMAQ}
        self.pool_cnt = {q: [0] * self.NPOOL for q in self.DMAQ}
        self.pool_rr = {q: 0 for q in self.DMAQ}
        self.cnt = {q: 0 for q in self.QSEM}
        self.done_cnt = {q: 0 for q in self.DMAQ}
        self.all = []
        self.start = 0
        self.lastw = {}
        self.readers = {}
        self.waited = {}

    def op(self, q, fn, reads=(), writes=(), deps=(), pre=(), bgsem=None):
        i = len(self.all)
        dep = set(d for d in deps if d is not None)
        for r in reads:
            w = self.lastw.get(r)
            if w is not None:
                dep.add(w)
        for w in writes:
            lw = self.lastw.get(w)
            if lw is not None:
                dep.add(lw)
            dep |= self.readers.get(w, set())
        dep.discard(i)
        self.all.append(dict(q=q, fn=fn, deps=dep, ticket=None, psem=None,
                             inc=False, pre=tuple(pre), bgsem=bgsem))
        for r in reads:
            self.readers.setdefault(r, set()).add(i)
        for w in writes:
            self.lastw[w] = i
            self.readers[w] = set()
        return i

    def dma(self, q, out, in_, reads=(), writes=(), deps=()):
        return self.op(q, (lambda e, o=out, s=in_: e.dma_start(out=o, in_=s)),
                       reads, writes, deps)

    def _skip(self, d, o_q, n0):
        p = self.all[d]
        if p["q"] == "t" and o_q == "t":
            return True
        if d < n0:
            return True
        return False

    def flush(self):
        n0 = self.start
        batch = self.all[n0:]
        if not batch:
            return
        prev_q = {q: self.cnt[q] for q in self.DMAQ}
        prev_pool = {q: list(self.pool_cnt[q]) for q in self.DMAQ}

        for j, o in enumerate(batch, n0):
            for d in o["deps"]:
                if self._skip(d, o["q"], n0):
                    continue
                self.all[d]["inc"] = True
        for o in batch:
            if o["bgsem"] is not None:
                continue
            if o["q"] in self.DMAQ:
                if o["inc"]:
                    qq = o["q"]
                    i = self.pool_rr[qq]
                    self.pool_rr[qq] = (i + 1) % self.NPOOL
                    o["psem"] = (qq, i)
                    o["pprev"] = self.pool_cnt[qq][i]
                    self.pool_cnt[qq][i] += 16
                    o["ticket"] = self.pool_cnt[qq][i]
                else:
                    self.cnt[o["q"]] += 16
                    o["ticket"] = self.cnt[o["q"]]
            elif o["inc"]:
                self.cnt[o["q"]] += 1
                o["ticket"] = self.cnt[o["q"]]
        fence = {}
        for q in self.DMAQ:
            self.done_cnt[q] += 1
            fence[q] = self.done_cnt[q]

        def emit(eng, queues):
            ekey = str(eng.engine)

            def waitv(semkey, sem, v):
                if v <= 0:
                    return
                key = (ekey, semkey)
                if self.waited.get(key, -1) < v:
                    eng.wait_ge(sem, v)
                    self.waited[key] = v

            for q in queues:
                if q in self.DMAQ:
                    waitv(q, self.sem[q], prev_q[q])
                    for i in range(self.NPOOL):
                        waitv(("pool", q, i), self.pool[q][i],
                              prev_pool[q][i])
                    eng.sem_inc(self.done[q], 1)
            for q in self.DMAQ:
                waitv("d_" + q, self.done[q], fence[q])

            for j, o in enumerate(batch, n0):
                if o["q"] not in queues:
                    continue
                for (sm, smkey, v) in o["pre"]:
                    waitv(smkey, sm, v)
                if o["bgsem"] is not None:
                    inst = o["fn"](eng)
                    inst.then_inc(o["bgsem"], 16)
                    continue
                if o["psem"] is not None:
                    qq, i = o["psem"]
                    waitv(("pool", qq, i), self.pool[qq][i], o["pprev"])
                for d in sorted(o["deps"]):
                    if self._skip(d, o["q"], n0):
                        continue
                    p = self.all[d]
                    pq = p["q"]
                    if pq in self.DMAQ:
                        if p["psem"] is not None:
                            qq, i = p["psem"]
                            waitv(("pool", qq, i), self.pool[qq][i],
                                  p["ticket"])
                        else:
                            assert self.SEC[pq] == self.SEC[o["q"]],                                 (d, pq, o["q"])
                            waitv(pq, self.sem[pq], p["ticket"])
                    else:
                        assert p["ticket"] is not None, (d, p["q"])
                        waitv(pq, self.sem[pq], p["ticket"])
                inst = o["fn"](eng)
                if o["q"] in self.DMAQ:
                    if o["psem"] is not None:
                        qq, i = o["psem"]
                        inst.then_inc(self.pool[qq][i], 16)
                    else:
                        inst.then_inc(self.sem[o["q"]], 16)
                elif o["inc"]:
                    inst.then_inc(self.sem[o["q"]], 1)

        with self.nc.Block() as blk:
            blk.sync(lambda e: emit(e, ("ld",)))
            blk.gpsimd(lambda e: emit(e, ("gp",)))
            blk.scalar(lambda e: emit(e, ("s", "so")))
            blk.tensor(lambda e: emit(e, ("t",)))
            blk.vector(lambda e: emit(e, ("v",)))
        self.start = len(self.all)


SB_BIG = 35840
SB_TOP = 229376
import os
_KPH = int(os.environ.get("KPH", "99"))


def build_program():
    nc = bass.Bass("TRN2", target_bir_lowering=False)

    def EI(name, shape, dtype=F32):
        return nc.dram_tensor(name, list(shape), dtype, kind="ExternalInput").ap()

    x0 = EI("x0", (T, D))
    wq0d = EI("wq0d", (D, D)); wk0d = EI("wk0d", (D, D)); wv0 = EI("wv0", (D, D))
    cosq = EI("cosq", (512, T)); sinq = EI("sinq", (512, T))
    cosk = EI("cosk", (512, T)); sink = EI("sink", (512, T))
    gwT_d = EI("gwT", (D, 8)); gbT_d = EI("gbT", (8, 1))
    ew_g = EI("ew_g", (8 * D, 1024)); eb0 = EI("eb0", (8, 1024))
    eids_d = EI("eids_row", (1, 8)); iotap_d = EI("iota_p", (128, 1))
    wq1 = EI("wq1", (D, D)); wk1 = EI("wk1", (D, D)); wv1 = EI("wv1", (D, D))
    gamma_d = EI("gamma_bc", (128, D)); beta_d = EI("beta_bc", (128, D))
    Wr_d = EI("Wr", (D, 8)); br_d = EI("br_row", (1, 8))
    W1_d = EI("W1", (8, D, FF)); b1T_d = EI("b1T", (FF, 8))
    W2_d = EI("W2", (8, FF, D)); b2_d = EI("b2", (8, 1024))
    gvpos_d = EI("gvpos", (128, 1)); gvneg_d = EI("gvneg", (128, 1))
    gneg_d = EI("gneg", (128, 512)); gpos_d = EI("gpos", (128, 512))
    ddiag_d = EI("ddiag", (128, 4 * 512)); cmask_d = EI("cmask", (128, 128))
    cmask2_d = EI("cmask2", (128, 256))
    z32_d = EI("z32", (128, 1))
    idx_d = EI("idx_own", (128, NHT), mybir.dt.int32)
    identbf_d = EI("ident_bf", (128, 128), BF16)
    identf_d = EI("ident_f", (128, 128))
    ones_d = EI("ones_row", (1, 128), BF16)

    x2own = nc.dram_tensor("x2own", [HTOK, D], F32, kind="ExternalOutput").ap()

    def DSc(name, shape, dtype=F32):
        return nc.dram_tensor(name, list(shape), dtype, kind="Internal").ap()

    xnT0_d = DSc("xnT0_d", (ND, 128, T), BF16)
    qT0_d = DSc("qT0_d", (ND, 128, T), BF16)
    attn0_d = DSc("attn0_d", (T, D))
    x1_d = DSc("x1_d", (T, D))
    xn1f_d = DSc("xn1f_d", (T, D))
    xn1T_d = DSc("xn1T_d", (ND, 128, T), BF16)
    q1tm_d = DSc("q1tm_d", (T, D), BF16)
    attn1_d = DSc("attn1_d", (HTOK, D))
    xn2T_d = DSc("xn2T_d", (ND, 128, HTOK), BF16)
    row8_d = DSc("row8_d", (8, 1))
    grow_d = DSc("grow_d", (1, 8))
    se2_d = DSc("se2_d", (1, 2))
    g2_d = DSc("g2_d", (1, 2))
    eb2_d = DSc("eb2_d", (2, 1024))
    ctil_d = DSc("ctil_d", (8, 1024))
    w1b_d = DSc("w1b_d", (8, D, FF), BF16)
    w2b_d = DSc("w2b_d", (8, FF, D), BF16)
    cpp_d = DSc("cpp_d", (8, 1024), BF16)

    P = Prog(nc)

    ident_bf = nc.alloc_sbuf_tensor("ident_bf_s", [128, 128], BF16)
    ident_f = nc.alloc_sbuf_tensor("ident_f_s", [128, 128], F32)
    ones_bf = nc.alloc_sbuf_tensor("ones_bf_s", [1, 128], BF16)
    gv_pos = nc.alloc_sbuf_tensor("gv_pos", [128, 1], F32)
    gv_neg = nc.alloc_sbuf_tensor("gv_neg", [128, 1], F32)
    gneg_s = nc.alloc_sbuf_tensor("gneg_s", [128, 512], F32)
    gpos_s = nc.alloc_sbuf_tensor("gpos_s", [128, 512], F32)
    cmask_s = nc.alloc_sbuf_tensor("cmask_s", [128, 128], F32)
    cmask2_s = nc.alloc_sbuf_tensor("cmask2_s", [128, 256], F32)
    z32_s = nc.alloc_sbuf_tensor("z32_s", [128, 1], F32)
    gates_bc = nc.alloc_sbuf_tensor("gates_bc", [128, 8], F32)
    stat = nc.alloc_sbuf_tensor("stat", [128, 64], F32)
    st8 = nc.alloc_sbuf_tensor("st8", [1, 72], F32)
    eids_s = nc.alloc_sbuf_tensor("eids_s", [1, 8], F32)
    iop_s = nc.alloc_sbuf_tensor("iop_s", [128, 1], F32)
    sef_bc = nc.alloc_sbuf_tensor("sef_bc", [128, 2], F32)
    gbc2 = nc.alloc_sbuf_tensor("gbc2", [128, 2], F32)
    idxg = nc.alloc_sbuf_tensor("idxg", [128, 16], mybir.dt.int32)
    idx1e = nc.alloc_sbuf_tensor("idx1e", [2, 1], mybir.dt.int32)
    se21 = nc.alloc_sbuf_tensor("se21", [2, 1], F32)
    idx_sb = nc.alloc_sbuf_tensor("idx_sb", [128, NHT], mybir.dt.int32)
    assert nc.bytes_allocated(bass.MemorySpace.SBUF) <= SB_BIG, \
        nc.bytes_allocated(bass.MemorySpace.SBUF)

    _atn = [0]
    def AT(shape, dtype, off):
        _atn[0] += 1
        end = SB_BIG + off + int(np.prod(shape[1:])) * mybir.dt.size(dtype)
        assert end <= SB_TOP, (shape, off, end)
        return nc.alloc_sbuf_tensor_at(f"at{_atn[0]}", list(shape), dtype,
                                       offset=SB_BIG + off)

    S = lambda c: ("stat", c)

    for dst, src in ((ident_bf, identbf_d), (ident_f, identf_d),
                     (ones_bf, ones_d), (gv_pos, gvpos_d), (gv_neg, gvneg_d),
                     (gneg_s, gneg_d), (gpos_s, gpos_d), (cmask_s, cmask_d),
                     (cmask2_s, cmask2_d), (z32_s, z32_d)):
        P.dma("ld", dst[:], src, writes=[("const", dst.name)])
    P.dma("ld", idx_sb[:], idx_d, writes=[("idx",)])
    P.dma("ld", eids_s[:], eids_d, writes=[("eids",)])
    P.dma("ld", iop_s[:], iotap_d, writes=[("iop",)])
    KID = ("const", "ident_bf_s")
    KIDF = ("const", "ident_f_s")
    KONE = ("const", "ones_bf_s")

    def ph_ln(src_d, xnT, tag, src_key=None, xnf_d=None):
        xt = [AT([128, D], F32, 128 * KB + i * 4 * KB) for i in range(2)]
        scr = [AT([128, D], F32, 136 * KB + i * 4 * KB) for i in range(2)]
        xnb = [AT([128, D], BF16, 144 * KB + i * 2 * KB) for i in range(2)]
        xnf = [AT([128, D], F32, 148 * KB + i * 4 * KB) for i in range(2)]
        with (nc.psum_tensor([128, 128], BF16) as pt0,
              nc.psum_tensor([128, 128], BF16) as pt1):
            ptr = [pt0, pt1]
            for j in range(NT):
                par = j % 2
                b = 16 * par
                xb = xt[par]
                xnbp, xnfp, scrp = xnb[par], xnf[par], scr[par]
                kxt = (tag, "xt", par)
                P.dma("ld", xb[:], src_d[j * 128:(j + 1) * 128, :],
                      reads=([src_key(j)] if src_key else []), writes=[kxt])
                P.op("v", lambda e, xb=xb, b=b: e.tensor_reduce(
                    stat[:, b + 0:b + 1], xb[:], AX.X, ALU.add),
                    [kxt], [S(b + 0)])
                P.op("s", lambda e, xb=xb, b=b, scrp=scrp: e.activation(
                    scrp[:], xb[:], AF.Square, accum_out=stat[:, b + 1:b + 2]),
                    [kxt], [(tag, "scr", par), S(b + 1)])
                P.op("v", lambda e, b=b: e.tensor_scalar(
                    stat[:, b + 2:b + 3], stat[:, b + 0:b + 1], 1.0 / D, None,
                    ALU.mult), [S(b + 0)], [S(b + 2)])
                P.op("v", lambda e, b=b: e.tensor_tensor(
                    stat[:, b + 3:b + 4], stat[:, b + 2:b + 3],
                    stat[:, b + 2:b + 3], ALU.mult), [S(b + 2)], [S(b + 3)])
                P.op("v", lambda e, b=b: e.tensor_scalar(
                    stat[:, b + 4:b + 5], stat[:, b + 1:b + 2], 1.0 / D, None,
                    ALU.mult), [S(b + 1)], [S(b + 4)])
                P.op("v", lambda e, b=b: e.tensor_tensor(
                    stat[:, b + 5:b + 6], stat[:, b + 4:b + 5],
                    stat[:, b + 3:b + 4], ALU.subtract),
                    [S(b + 3), S(b + 4)], [S(b + 5)])
                P.op("v", lambda e, b=b: e.tensor_scalar(
                    stat[:, b + 5:b + 6], stat[:, b + 5:b + 6], 1e-5, None,
                    ALU.add), [S(b + 5)], [S(b + 5)])
                P.op("s", lambda e, b=b: e.activation(
                    stat[:, b + 6:b + 7], stat[:, b + 5:b + 6], AF.Sqrt),
                    [S(b + 5)], [S(b + 6)])
                P.op("v", lambda e, b=b: e.reciprocal(
                    stat[:, b + 7:b + 8], stat[:, b + 6:b + 7]),
                    [S(b + 6)], [S(b + 7)])
                P.op("v", lambda e, b=b: e.scalar_tensor_tensor(
                    stat[:, b + 8:b + 9], stat[:, b + 2:b + 3], -1.0,
                    stat[:, b + 7:b + 8], ALU.mult, ALU.mult),
                    [S(b + 2), S(b + 7)], [S(b + 8)])
                P.op("s", lambda e, xb=xb, b=b, xnbp=xnbp: e.activation(
                    xnbp[:], xb[:], AF.Identity, bias=stat[:, b + 8:b + 9],
                    scale=stat[:, b + 7:b + 8]),
                    [kxt, S(b + 7), S(b + 8)], [(tag, "xnb", par)])
                if xnf_d is not None:
                    P.op("s", lambda e, xb=xb, b=b, xnfp=xnfp: e.activation(
                        xnfp[:], xb[:], AF.Identity, bias=stat[:, b + 8:b + 9],
                        scale=stat[:, b + 7:b + 8]),
                        [kxt, S(b + 7), S(b + 8)], [(tag, "xnf", par)])
                    P.dma("so", xnf_d[j * 128:(j + 1) * 128, :], xnfp[:],
                          reads=[(tag, "xnf", par)], writes=[("xnf_d", j)])
                for dc in range(ND):
                    pp = ptr[dc % 2]
                    kp = (tag, "pt", dc % 2)
                    P.op("t", lambda e, pp=pp, dc=dc, xnbp=xnbp: e.transpose(
                        pp[:], xnbp[:, dc * 128:(dc + 1) * 128], ident_bf[:]),
                        [(tag, "xnb", par), KID], [kp])
                    P.op("v", lambda e, pp=pp, dc=dc, j=j: e.tensor_copy(
                        xnT[:, dc * T + j * 128: dc * T + (j + 1) * 128], pp[:]),
                        [kp], [("xnT" + tag, dc, j // 4)])
            P.flush()

    def ph_qk_xpos(w_d, cos_d, sin_d, xnT, tag, out_kT=None, out_q_d=None):
        t0 = tag[0]
        w_sb = AT([128, ND * D], BF16, 128 * KB)
        cs = [AT([128, 512], F32, 144 * KB + i * 2 * KB) for i in range(2)]
        sn = [AT([128, 512], F32, 148 * KB + i * 2 * KB) for i in range(2)]
        tm1 = AT([128, 512], F32, 152 * KB)
        tm2 = AT([128, 512], F32, 154 * KB)
        qst = AT([128, 2 * T], BF16, 156 * KB)
        P.dma("gp", w_sb[:].rearrange("p (dc o) -> p dc o", dc=ND),
              w_d.rearrange("(dc p) o -> p dc o", p=128), writes=[(tag, "w")])
        with (nc.psum_tensor([128, 512], F32) as pA0,
              nc.psum_tensor([128, 512], F32) as pA1,
              nc.psum_tensor([128, 512], F32) as pB0,
              nc.psum_tensor([128, 512], F32) as pB1):
            for pr in range(4):
                for c in range(NKC):
                    ib = c % 2
                    pa, pb = (pA0, pB0) if ib == 0 else (pA1, pB1)
                    ka, kb = (tag, "pa", ib), (tag, "pb", ib)
                    kc, ks = (tag, "cs", ib), (tag, "sn", ib)
                    P.dma("ld", cs[ib][:],
                          cos_d[(pr * 128):(pr + 1) * 128,
                                c * 512:(c + 1) * 512], writes=[kc])
                    P.dma("ld", sn[ib][:],
                          sin_d[(pr * 128):(pr + 1) * 128,
                                c * 512:(c + 1) * 512], writes=[ks])
                    for jo, ps, kk in ((pr, pa, ka), (pr + 4, pb, kb)):
                        for dc in range(ND):
                            P.op("t", lambda e, ps=ps, jo=jo, dc=dc, c=c:
                                 e.matmul(ps[:],
                                          w_sb[:, dc * D + jo * 128:
                                               dc * D + (jo + 1) * 128],
                                          xnT[:, dc * T + c * 512:
                                              dc * T + (c + 1) * 512],
                                          start=(dc == 0), stop=(dc == ND - 1)),
                                 [(tag, "w"), ("xnT" + t0, dc, c)], [kk])
                    P.op("v", lambda e, pa=pa, ib=ib: e.tensor_tensor(
                        tm1[:], pa[:], cs[ib][:], ALU.mult),
                        [ka, kc], [(tag, "tm1")])
                    P.op("v", lambda e, pb=pb, ib=ib: e.tensor_tensor(
                        tm2[:], pb[:], sn[ib][:], ALU.mult),
                        [kb, ks], [(tag, "tm2")])
                    if out_kT is not None:
                        P.op("v", lambda e, pr=pr, c=c: e.tensor_tensor(
                            out_kT[:, pr * T + c * 512: pr * T + (c + 1) * 512],
                            tm1[:], tm2[:], ALU.subtract),
                            [(tag, "tm1"), (tag, "tm2")], [("kT" + t0, pr, c)])
                    else:
                        P.op("v", lambda e, c=c: e.tensor_tensor(
                            qst[:, c * 512:(c + 1) * 512],
                            tm1[:], tm2[:], ALU.subtract),
                            [(tag, "tm1"), (tag, "tm2")], [(tag, "qst", 0, c)])
                    P.op("v", lambda e, pb=pb, ib=ib: e.tensor_tensor(
                        tm1[:], pb[:], cs[ib][:], ALU.mult),
                        [kb, kc], [(tag, "tm1")])
                    P.op("v", lambda e, pa=pa, ib=ib: e.tensor_tensor(
                        tm2[:], pa[:], sn[ib][:], ALU.mult),
                        [ka, ks], [(tag, "tm2")])
                    if out_kT is not None:
                        P.op("v", lambda e, pr=pr, c=c: e.tensor_tensor(
                            out_kT[:, (pr + 4) * T + c * 512:
                                   (pr + 4) * T + (c + 1) * 512],
                            tm1[:], tm2[:], ALU.add),
                            [(tag, "tm1"), (tag, "tm2")],
                            [("kT" + t0, pr + 4, c)])
                    else:
                        P.op("v", lambda e, c=c: e.tensor_tensor(
                            qst[:, T + c * 512: T + (c + 1) * 512],
                            tm1[:], tm2[:], ALU.add),
                            [(tag, "tm1"), (tag, "tm2")], [(tag, "qst", 1, c)])
                if out_q_d is not None:
                    for hh, jo in ((0, pr), (1, pr + 4)):
                        P.dma("so", out_q_d[jo][:, :],
                              qst[:, hh * T:(hh + 1) * T],
                              reads=[(tag, "qst", hh, c) for c in range(NKC)],
                              writes=[("qT_d" + t0, jo)])
            P.flush()

    def ph_spill_xnT(xnT, spill_d, tag):
        for dc in range(ND):
            P.dma("so", spill_d[dc][:, :], xnT[:, dc * T:(dc + 1) * T],
                  reads=[("xnT" + tag, dc, c) for c in range(NKC)],
                  writes=[("xnTd" + tag, dc)])

    def ph_v(wv_d, spill_d, v_sb, tag, xoff, wq_d=None, qtm_d=None):
        wv_sb = AT([128, ND * D], BF16, 128 * KB)
        xc = [AT([128, ND * 128], BF16, xoff + i * 2 * KB) for i in range(2)]
        P.dma("gp", wv_sb[:].rearrange("p (dc o) -> p dc o", dc=ND),
              wv_d.rearrange("(dc p) o -> p dc o", p=128), writes=[(tag, "wv")])
        if wq_d is not None:
            wq_sb = AT([128, ND * D], BF16, 144 * KB)
            qtt = [AT([128, D], BF16, 164 * KB + i * 2 * KB) for i in range(2)]
            P.dma("gp", wq_sb[:].rearrange("p (dc o) -> p dc o", dc=ND),
                  wq_d.rearrange("(dc p) o -> p dc o", p=128),
                  writes=[(tag, "wq")])
        with (nc.psum_tensor([128, 512], F32) as p0,
              nc.psum_tensor([128, 512], F32) as p1,
              nc.psum_tensor([128, 512], F32) as p2,
              nc.psum_tensor([128, 512], F32) as p3):
            for j in range(NT):
                xb = xc[j % 2]
                kx = (tag, "vxc", j % 2)
                P.dma("ld", xb[:].rearrange("p (dc k) -> p dc k", dc=ND),
                      spill_d[:, :, j * 128:(j + 1) * 128]
                      .rearrange("dc p k -> p dc k"),
                      reads=[("xnTd" + tag, dc) for dc in range(ND)],
                      writes=[kx])
                for n in range(2):
                    ps = (p0, p1)[n]
                    kk = (tag, "vps", n)
                    for dc in range(ND):
                        P.op("t", lambda e, ps=ps, xb=xb, dc=dc, n=n: e.matmul(
                            ps[:], xb[:, dc * 128:(dc + 1) * 128],
                            wv_sb[:, dc * D + n * 512: dc * D + (n + 1) * 512],
                            start=(dc == 0), stop=(dc == ND - 1)),
                            [(tag, "wv"), kx], [kk])
                    P.op("v", lambda e, ps=ps, j=j, n=n: e.tensor_copy(
                        v_sb[:, j * D + n * 512: j * D + (n + 1) * 512], ps[:]),
                        [kk], [("v" + tag, j)])
                if wq_d is not None:
                    qb2 = qtt[j % 2]
                    kq2 = (tag, "qtt", j % 2)
                    for n in range(2):
                        ps = (p2, p3)[n]
                        kk = (tag, "qps", n)
                        for dc in range(ND):
                            P.op("t", lambda e, ps=ps, xb=xb, dc=dc, n=n:
                                 e.matmul(ps[:],
                                          xb[:, dc * 128:(dc + 1) * 128],
                                          wq_sb[:, dc * D + n * 512:
                                                dc * D + (n + 1) * 512],
                                          start=(dc == 0), stop=(dc == ND - 1)),
                                 [(tag, "wq"), kx], [kk])
                        P.op("v", lambda e, ps=ps, qb2=qb2, n=n: e.tensor_copy(
                            qb2[:, n * 512:(n + 1) * 512], ps[:]),
                            [kk], [kq2 + (n,)])
                    P.dma("so", qtm_d[j * 128:(j + 1) * 128, :], qb2[:],
                          reads=[kq2 + (0,), kq2 + (1,)],
                          writes=[("q1tm", j)])
            P.flush()

    def ph_plain_kq(wk_d, wq_d, xnT, kT, q_d, tag):
        w_sb = AT([128, ND * D], BF16, 128 * KB)
        qst = AT([128, T], BF16, 152 * KB)
        for w_d, dest in ((wk_d, "k"),):
            P.dma("gp", w_sb[:].rearrange("p (dc o) -> p dc o", dc=ND),
                  w_d.rearrange("(dc p) o -> p dc o", p=128),
                  writes=[(tag, "w")])
            with (nc.psum_tensor([128, 512], F32) as p0,
                  nc.psum_tensor([128, 512], F32) as p1):
                for jo in range(ND):
                    for c in range(NKC):
                        ps = (p0, p1)[c % 2]
                        kk = (tag, "ps", c % 2)
                        for dc in range(ND):
                            P.op("t", lambda e, ps=ps, jo=jo, dc=dc, c=c:
                                 e.matmul(ps[:],
                                          w_sb[:, dc * D + jo * 128:
                                               dc * D + (jo + 1) * 128],
                                          xnT[:, dc * T + c * 512:
                                              dc * T + (c + 1) * 512],
                                          start=(dc == 0), stop=(dc == ND - 1)),
                                 [(tag, "w"), ("xnT" + tag, dc, c)], [kk])
                        if dest == "k":
                            P.op("v", lambda e, ps=ps, jo=jo, c=c: e.tensor_copy(
                                kT[:, jo * T + c * 512: jo * T + (c + 1) * 512],
                                ps[:]), [kk], [("kT" + tag, jo, c)])
                        else:
                            P.op("v", lambda e, ps=ps, c=c: e.tensor_copy(
                                qst[:, c * 512:(c + 1) * 512], ps[:]),
                                [kk], [(tag, "qst", c)])
                    if dest == "q":
                        P.dma("so", q_d[jo][:, :], qst[:],
                              reads=[(tag, "qst", c) for c in range(NKC)],
                              writes=[("qT_d" + tag, jo)])
                P.flush()

    def ph_attn(kT, v_sb, q_d, out_d, tag, causal, kept_fn):
        Ssb = AT([128, T], F32, 128 * KB)
        Ebf = [AT([128, T], BF16, 144 * KB), AT([128, T], BF16, 172 * KB)]
        dd_s = AT([128, 2048], F32, 152 * KB)
        qt = [AT([128, ND * 128], BF16, 160 * KB + i * 2 * KB) for i in range(2)]
        abf = [AT([128, 512], BF16, 164 * KB + i * KB) for i in range(2)]
        ats = [AT([128, 128], BF16, 166 * KB + i * KB) for i in range(2)]
        att = [AT([128, D], F32, 168 * KB), AT([128, D], F32, 180 * KB)]
        if not causal:
            P.dma("ld", dd_s[:], ddiag_d, writes=[("ddiag",)])
        with (nc.psum_tensor([128, 512], F32) as ps0,
              nc.psum_tensor([128, 512], F32) as ps1,
              nc.psum_tensor([128, 512], F32) as pav0,
              nc.psum_tensor([128, 512], F32) as pav1,
              nc.psum_tensor([128, 128], BF16) as ptA,
              nc.psum_tensor([128, 128], BF16) as ptB):
            for j in range(NT):
                par = j % 2
                b = 32 + 16 * par
                Eb = Ebf[par]
                kE = (tag, "E", par)
                atb = att[par]
                ext = (j + 1) * 128 if causal else T
                nch = (ext + 511) // 512
                qb = qt[par]
                kq = (tag, "qt", par)
                P.dma("ld", qb[:].rearrange("p (dc k) -> p dc k", dc=ND),
                      q_d[:, :, j * 128:(j + 1) * 128]
                      .rearrange("dc p k -> p dc k"),
                      reads=[("qT_d" + tag, dc) for dc in range(ND)],
                      writes=[kq])
                srd = []
                for c in range(nch):
                    ncols = min(512, ext - c * 512)
                    ps = (ps0, ps1)[c % 2]
                    kk = (tag, "ps", c % 2)
                    for dc in range(ND):
                        P.op("t", lambda e, ps=ps, qb=qb, dc=dc, c=c, ncols=ncols:
                             e.matmul(ps[:, 0:ncols],
                                      qb[:, dc * 128:(dc + 1) * 128],
                                      kT[:, dc * T + c * 512:
                                         dc * T + c * 512 + ncols],
                                      start=(dc == 0), stop=(dc == ND - 1)),
                             [kq, ("kT" + tag, dc, c)], [kk])
                    evsc = (1.0 / 32.0) if causal else 1.0
                    if causal and c == nch - 1:
                        if ncols > 128:
                            P.op("v", lambda e, ps=ps, c=c, ncols=ncols:
                                 e.tensor_scalar(
                                     Ssb[:, c * 512: c * 512 + ncols - 128],
                                     ps[:, 0:ncols - 128], evsc, None,
                                     ALU.mult),
                                 [kk], [(tag, "S", c)])
                            srd.append((tag, "S", c))
                        P.op("v", lambda e, ps=ps, c=c, ncols=ncols:
                             e.scalar_tensor_tensor(
                                 Ssb[:, c * 512 + ncols - 128: c * 512 + ncols],
                                 ps[:, ncols - 128:ncols], evsc, cmask_s[:],
                                 ALU.mult, ALU.add),
                             [kk, ("const", "cmask_s")], [(tag, "Sm", c)])
                        srd.append((tag, "Sm", c))
                    elif causal:
                        P.op("v", lambda e, ps=ps, c=c, ncols=ncols:
                             e.tensor_scalar(Ssb[:, c * 512:c * 512 + ncols],
                                             ps[:, 0:ncols], evsc, None,
                                             ALU.mult),
                             [kk], [(tag, "S", c)])
                        srd.append((tag, "S", c))
                    else:
                        P.op("v", lambda e, ps=ps, c=c, ncols=ncols:
                             e.tensor_copy(Ssb[:, c * 512:c * 512 + ncols],
                                           ps[:, 0:ncols]),
                             [kk], [(tag, "S", c)])
                        srd.append((tag, "S", c))
                P.op("v", lambda e, ext=ext, b=b: e.tensor_reduce(
                    stat[:, b + 9 - 9:b + 10 - 9], Ssb[:, 0:ext], AX.X,
                    ALU.max), srd, [S(b + 0)])
                P.op("v", lambda e, b=b: e.tensor_scalar(
                    stat[:, b + 1:b + 2], stat[:, b + 0:b + 1], -1.0, None,
                    ALU.mult), [S(b + 0)], [S(b + 1)])
                P.op("s", lambda e, ext=ext, b=b, Eb=Eb: e.activation(
                    Eb[:, 0:ext], Ssb[:, 0:ext], AF.Exp,
                    bias=stat[:, b + 1:b + 2],
                    accum_out=stat[:, b + 2:b + 3]),
                    srd + [S(b + 1)], [kE, S(b + 2)])
                P.op("v", lambda e, b=b: e.reciprocal(
                    stat[:, b + 3:b + 4], stat[:, b + 2:b + 3]),
                    [S(b + 2)], [S(b + 3)])
                kept = kept_fn(j)
                nsub_tot = sum((min(512, ext - c * 512) + 127) // 128
                               for (c, _, _) in kept)
                kav = [(tag, "pav", 0), (tag, "pav", 1)]
                sdone = 0
                for (c, typ, coef) in kept:
                    ncols = min(512, ext - c * 512)
                    ab = abf[c % 2]
                    kab = (tag, "abf", c % 2)
                    if causal:
                        P.op("v", lambda e, ab=ab, c=c, ncols=ncols, b=b, Eb=Eb:
                             e.tensor_scalar(ab[:, 0:ncols],
                                             Eb[:, c * 512:c * 512 + ncols],
                                             stat[:, b + 3:b + 4], None,
                                             ALU.mult),
                             [kE, S(b + 3)], [kab])
                    elif typ == "d":
                        P.op("v", lambda e, ab=ab, c=c, r=coef, b=b, Eb=Eb:
                             e.scalar_tensor_tensor(
                                 ab[:], Eb[:, c * 512:(c + 1) * 512],
                                 stat[:, b + 3:b + 4],
                                 dd_s[:, r * 512:(r + 1) * 512],
                                 ALU.mult, ALU.mult),
                             [kE, S(b + 3), ("ddiag",)], [kab])
                    else:
                        gv = gv_pos if typ == "l" else gv_neg
                        gb = gneg_s if typ == "l" else gpos_s
                        gbk = ("const", "gneg_s" if typ == "l" else "gpos_s")
                        P.op("v", lambda e, gv=gv, coef=coef, b=b:
                             e.tensor_scalar(
                                 stat[:, b + 4:b + 5], gv[:],
                                 stat[:, b + 3:b + 4], coef,
                                 ALU.mult, ALU.mult), [S(b + 3)], [S(b + 4)])
                        P.op("v", lambda e, ab=ab, c=c, gb=gb, b=b, Eb=Eb:
                             e.scalar_tensor_tensor(
                                 ab[:], Eb[:, c * 512:(c + 1) * 512],
                                 stat[:, b + 4:b + 5], gb[:], ALU.mult,
                                 ALU.mult),
                             [kE, S(b + 4), gbk], [kab])
                    nsub = (ncols + 127) // 128
                    for sub in range(nsub):
                        s_k = c * 4 + sub
                        pt = (ptA, ptB)[s_k % 2]
                        kpt = (tag, "pt", s_k % 2)
                        at = ats[s_k % 2]
                        kat = (tag, "at", s_k % 2)
                        P.op("t", lambda e, pt=pt, ab=ab, sub=sub: e.transpose(
                            pt[:], ab[:, sub * 128:(sub + 1) * 128],
                            ident_bf[:]), [kab, KID], [kpt])
                        P.op("v", lambda e, pt=pt, at=at: e.tensor_copy(
                            at[:], pt[:]), [kpt], [kat])
                        sdone += 1
                        last = (sdone == nsub_tot)
                        for n in range(2):
                            P.op("t", lambda e, at=at, s_k=s_k, n=n,
                                 first=(sdone == 1), last=last,
                                 pv=(pav0, pav1)[n]:
                                 e.matmul(pv[:], at[:],
                                          v_sb[:, s_k * D + n * 512:
                                               s_k * D + (n + 1) * 512],
                                          start=first, stop=last),
                                 [kat, ("v" + tag, s_k)], [kav[n]])
                P.op("v", lambda e, atb=atb: e.tensor_copy(
                    atb[:, 0:512], pav0[:]), [kav[0]], [(tag, "att", par, 0)])
                P.op("v", lambda e, atb=atb: e.tensor_copy(
                    atb[:, 512:1024], pav1[:]), [kav[1]], [(tag, "att", par, 1)])
                P.dma("so", out_d[j * 128:(j + 1) * 128, :], atb[:],
                      reads=[(tag, "att", par, 0), (tag, "att", par, 1)],
                      writes=[("attn_d" + tag, j)])
            P.flush()


    def dyn_load(e, dst, src_d, rg):
        return e.indirect_dma_start(
            dst, None, src_d,
            bass.IndirectOffsetOnAxis(ap=idx_sb[:, rg:rg + 1], axis=0))

    def ph_attn_own(kT, v_sb, qtm_d, out_d, tag):
        Ssb = AT([128, T], F32, 128 * KB)
        Ebf = [AT([128, T], BF16, 144 * KB), AT([128, T], BF16, 172 * KB)]
        qrow = [AT([128, D], BF16, 152 * KB + i * 2 * KB) for i in range(2)]
        qt = [AT([128, ND * 128], BF16, 160 * KB + i * 2 * KB) for i in range(2)]
        abf = [AT([128, 512], BF16, 164 * KB + i * KB) for i in range(2)]
        ats = [AT([128, 128], BF16, 166 * KB + i * KB) for i in range(2)]
        att = [AT([128, D], F32, 168 * KB), AT([128, D], F32, 180 * KB)]
        with (nc.psum_tensor([128, 512], F32) as ps0,
              nc.psum_tensor([128, 512], F32) as ps1,
              nc.psum_tensor([128, 512], F32) as pav0,
              nc.psum_tensor([128, 512], F32) as pav1,
              nc.psum_tensor([128, 128], BF16) as ptA,
              nc.psum_tensor([128, 128], BF16) as ptB):
            for rg in range(NHT):
                par = rg % 2
                b = 32 + 16 * par
                Eb = Ebf[par]
                kE = (tag, "E", par)
                atb = att[par]
                ext = (2 * rg + 2) * 128
                nch = (ext + 511) // 512
                qr = qrow[par]
                kqr = (tag, "qr", par)
                qb = qt[par]
                kq = (tag, "qt", par)
                P.op("gp", lambda e, qr=qr, rg=rg: dyn_load(e, qr[:], qtm_d, rg),
                     reads=[("idx",)] + [("q1tm", jj) for jj in range(NT)],
                     writes=[kqr])
                for dc in range(ND):
                    pt = (ptA, ptB)[dc % 2]
                    kpt = (tag, "pt", dc % 2)
                    P.op("t", lambda e, pt=pt, qr=qr, dc=dc: e.transpose(
                        pt[:], qr[:, dc * 128:(dc + 1) * 128], ident_bf[:]),
                        [kqr, KID], [kpt])
                    P.op("v", lambda e, pt=pt, qb=qb, dc=dc: e.tensor_copy(
                        qb[:, dc * 128:(dc + 1) * 128], pt[:]), [kpt], [kq])
                srd = []
                evsc = 1.0 / 32.0
                for c in range(nch):
                    ncols = min(512, ext - c * 512)
                    ps = (ps0, ps1)[c % 2]
                    kk = (tag, "ps", c % 2)
                    for dc in range(ND):
                        P.op("t", lambda e, ps=ps, qb=qb, dc=dc, c=c,
                             ncols=ncols:
                             e.matmul(ps[:, 0:ncols],
                                      qb[:, dc * 128:(dc + 1) * 128],
                                      kT[:, dc * T + c * 512:
                                         dc * T + c * 512 + ncols],
                                      start=(dc == 0), stop=(dc == ND - 1)),
                             [kq, ("kT" + tag, dc, c)], [kk])
                    if c == nch - 1:
                        if ncols > 256:
                            P.op("v", lambda e, ps=ps, c=c, ncols=ncols:
                                 e.tensor_scalar(
                                     Ssb[:, c * 512: c * 512 + ncols - 256],
                                     ps[:, 0:ncols - 256], evsc, None,
                                     ALU.mult),
                                 [kk], [(tag, "S", c)])
                            srd.append((tag, "S", c))
                        P.op("v", lambda e, ps=ps, c=c, ncols=ncols:
                             e.scalar_tensor_tensor(
                                 Ssb[:, c * 512 + ncols - 256: c * 512 + ncols],
                                 ps[:, ncols - 256:ncols], evsc, cmask2_s[:],
                                 ALU.mult, ALU.add),
                             [kk, ("const", "cmask2_s")], [(tag, "Sm", c)])
                        srd.append((tag, "Sm", c))
                    else:
                        P.op("v", lambda e, ps=ps, c=c, ncols=ncols:
                             e.tensor_scalar(Ssb[:, c * 512:c * 512 + ncols],
                                             ps[:, 0:ncols], evsc, None,
                                             ALU.mult),
                             [kk], [(tag, "S", c)])
                        srd.append((tag, "S", c))
                P.op("v", lambda e, ext=ext, b=b: e.tensor_reduce(
                    stat[:, b + 0:b + 1], Ssb[:, 0:ext], AX.X,
                    ALU.max), srd, [S(b + 0)])
                P.op("v", lambda e, b=b: e.tensor_scalar(
                    stat[:, b + 1:b + 2], stat[:, b + 0:b + 1], -1.0, None,
                    ALU.mult), [S(b + 0)], [S(b + 1)])
                P.op("s", lambda e, ext=ext, b=b, Eb=Eb: e.activation(
                    Eb[:, 0:ext], Ssb[:, 0:ext], AF.Exp,
                    bias=stat[:, b + 1:b + 2],
                    accum_out=stat[:, b + 2:b + 3]),
                    srd + [S(b + 1)], [kE, S(b + 2)])
                P.op("v", lambda e, b=b: e.reciprocal(
                    stat[:, b + 3:b + 4], stat[:, b + 2:b + 3]),
                    [S(b + 2)], [S(b + 3)])
                nsub_tot = ext // 128
                kav = [(tag, "pav", 0), (tag, "pav", 1)]
                sdone = 0
                for c in range(nch):
                    ncols = min(512, ext - c * 512)
                    ab = abf[c % 2]
                    kab = (tag, "abf", c % 2)
                    P.op("s", lambda e, ab=ab, c=c, ncols=ncols, b=b, Eb=Eb:
                         e.activation(ab[:, 0:ncols],
                                      Eb[:, c * 512:c * 512 + ncols],
                                      AF.Identity, bias=z32_s[:, 0:1],
                                      scale=stat[:, b + 3:b + 4]),
                         [kE, S(b + 3), ("const", "z32_s")], [kab])
                    for sub in range(ncols // 128):
                        s_k = c * 4 + sub
                        pt = (ptA, ptB)[s_k % 2]
                        kpt = (tag, "pt", s_k % 2)
                        at = ats[s_k % 2]
                        kat = (tag, "at", s_k % 2)
                        P.op("t", lambda e, pt=pt, ab=ab, sub=sub: e.transpose(
                            pt[:], ab[:, sub * 128:(sub + 1) * 128],
                            ident_bf[:]), [kab, KID], [kpt])
                        P.op("v", lambda e, pt=pt, at=at: e.tensor_copy(
                            at[:], pt[:]), [kpt], [kat])
                        sdone += 1
                        last = (sdone == nsub_tot)
                        for n in range(2):
                            P.op("t", lambda e, at=at, s_k=s_k, n=n,
                                 first=(sdone == 1), last=last,
                                 pv=(pav0, pav1)[n]:
                                 e.matmul(pv[:], at[:],
                                          v_sb[:, s_k * D + n * 512:
                                               s_k * D + (n + 1) * 512],
                                          start=first, stop=last),
                                 [kat, ("v" + tag, s_k)], [kav[n]])
                P.op("v", lambda e, atb=atb: e.tensor_copy(
                    atb[:, 0:512], pav0[:]), [kav[0]], [(tag, "att", par, 0)])
                P.op("v", lambda e, atb=atb: e.tensor_copy(
                    atb[:, 512:1024], pav1[:]), [kav[1]], [(tag, "att", par, 1)])
                P.dma("so", out_d[rg * 128:(rg + 1) * 128, :], atb[:],
                      reads=[(tag, "att", par, 0), (tag, "att", par, 1)],
                      writes=[("attn_d" + tag, rg)])
            P.flush()

    xnT_A = AT([128, ND * T], BF16, 0)
    kT_A = AT([128, ND * T], BF16, 64 * KB)

    wbg = nc.alloc_semaphore("wbg")
    for e_ in range(8):
        P.op("gp", lambda e, e_=e_: e.dma_start(out=w1b_d[e_], in_=W1_d[e_]),
             bgsem=wbg)
        P.op("gp", lambda e, e_=e_: e.dma_start(out=w2b_d[e_], in_=W2_d[e_]),
             bgsem=wbg)
    ph_ln(x0, xnT_A, "0")
    ph_qk_xpos(wk0d, cosk, sink, xnT_A, "0k", out_kT=kT_A)
    ph_qk_xpos(wq0d, cosq, sinq, xnT_A, "0q", out_q_d=qT0_d)

    gw_sb = AT([128, ND * 8], BF16, 128 * KB)
    Lacc = AT([8, T], F32, 130 * KB)
    gb_sb = AT([8, 1], F32, 146 * KB)
    ls8 = AT([8, 2], F32, 147 * KB)
    P.dma("gp", gw_sb[:].rearrange("p (dc e) -> p dc e", dc=ND),
          gwT_d.rearrange("(dc p) e -> p dc e", p=128), writes=[("gw",)])
    P.dma("ld", gb_sb[:], gbT_d, writes=[("gb",)])
    with nc.psum_tensor([8, 512], F32) as pg:
        for c in range(NKC):
            for dc in range(ND):
                P.op("t", lambda e, dc=dc, c=c: e.matmul(
                    pg[:], gw_sb[:, dc * 8:(dc + 1) * 8],
                    xnT_A[:, dc * T + c * 512: dc * T + (c + 1) * 512],
                    start=(dc == 0), stop=(dc == ND - 1)),
                    [("gw",), ("xnT0", dc, c)], [("pg",)])
            P.op("v", lambda e, c=c: e.tensor_copy(
                Lacc[:, c * 512:(c + 1) * 512], pg[:]),
                [("pg",)], [("Lacc", c)])
        P.op("v", lambda e: e.tensor_reduce(
            ls8[:, 0:1], Lacc[:], AX.X, ALU.add),
            [("Lacc", c) for c in range(NKC)], [("Ls",)])
        P.op("v", lambda e: e.scalar_tensor_tensor(
            ls8[:, 1:2], ls8[:, 0:1], 1.0 / T, gb_sb[:], ALU.mult, ALU.add),
            [("Ls",), ("gb",)], [("Lg",)])
        P.dma("gp", row8_d, ls8[:, 1:2], reads=[("Lg",)], writes=[("r8d",)])
        lrow = st8[0:1, 0:8]
        P.dma("gp", lrow, row8_d.rearrange("p one -> one p"),
              reads=[("r8d",)], writes=[("lrow",)])
        r8 = st8[0:1, 8:16]
        g8 = st8[0:1, 16:24]
        e8 = st8[0:1, 24:32]
        m1 = st8[0:1, 32:33]; nm1 = st8[0:1, 33:34]; m2 = st8[0:1, 34:35]
        zz = st8[0:1, 35:36]; rz8 = st8[0:1, 36:37]
        inv8 = st8[0:1, 37:45]
        keep2 = st8[0:1, 45:53]
        tmp8 = st8[0:1, 53:61]
        sc4 = st8[0:1, 61:65]
        P.op("v", lambda e: e.tensor_reduce(m1, lrow, AX.X, ALU.max),
             [("lrow",)], [("m1",)])
        P.op("v", lambda e: e.tensor_scalar(inv8, lrow, m1, None, ALU.is_ge),
             [("lrow",), ("m1",)], [("inv",)])
        P.op("v", lambda e: e.scalar_tensor_tensor(
            r8, inv8, NEG, lrow, ALU.mult, ALU.add),
            [("inv",), ("lrow",)], [("msk",)])
        P.op("v", lambda e: e.tensor_reduce(m2, r8, AX.X, ALU.max),
             [("msk",)], [("m2",)])
        P.op("v", lambda e: e.tensor_scalar(g8, lrow, m2, None, ALU.is_ge),
             [("lrow",), ("m2",)], [("keep",)])
        P.op("v", lambda e: e.tensor_scalar(nm1, m1, -1.0, None, ALU.mult),
             [("m1",)], [("nm1",)])
        P.op("s", lambda e: e.activation(e8, lrow, AF.Exp, bias=nm1),
             [("lrow",), ("nm1",)], [("e8",)])
        P.op("v", lambda e: e.tensor_tensor(e8, e8, g8, ALU.mult),
             [("e8",), ("keep",)], [("e8",)])
        P.op("v", lambda e: e.tensor_reduce(zz, e8, AX.X, ALU.add),
             [("e8",)], [("zz",)])
        P.op("v", lambda e: e.reciprocal(rz8, zz), [("zz",)], [("rz8",)])
        P.op("v", lambda e: e.tensor_scalar(e8, e8, rz8, None, ALU.mult),
             [("e8",), ("rz8",)], [("g8f",)])
        P.op("v", lambda e: e.tensor_tensor(keep2, g8, inv8, ALU.subtract),
             [("keep",), ("inv",)], [("keep2",)])
        P.op("v", lambda e: e.tensor_tensor(tmp8, eids_s[:], inv8, ALU.mult),
             [("eids",), ("inv",)], [("t8",)])
        P.op("v", lambda e: e.tensor_reduce(sc4[0:1, 0:1], tmp8, AX.X, ALU.add),
             [("t8",)], [("sea",)])
        P.op("v", lambda e: e.tensor_tensor(tmp8, eids_s[:], keep2, ALU.mult),
             [("eids",), ("keep2",)], [("t8",)])
        P.op("v", lambda e: e.tensor_reduce(sc4[0:1, 1:2], tmp8, AX.X, ALU.add),
             [("t8",)], [("seb",)])
        P.op("v", lambda e: e.tensor_tensor(tmp8, e8, inv8, ALU.mult),
             [("g8f",), ("inv",)], [("t8",)])
        P.op("v", lambda e: e.tensor_reduce(sc4[0:1, 2:3], tmp8, AX.X, ALU.add),
             [("t8",)], [("sga",)])
        P.op("v", lambda e: e.tensor_tensor(tmp8, e8, keep2, ALU.mult),
             [("g8f",), ("keep2",)], [("t8",)])
        P.op("v", lambda e: e.tensor_reduce(sc4[0:1, 3:4], tmp8, AX.X, ALU.add),
             [("t8",)], [("sgb",)])
        P.dma("gp", se2_d, sc4[0:1, 0:2], reads=[("sea",), ("seb",)],
              writes=[("se2d",)])
        P.dma("gp", g2_d, sc4[0:1, 2:4], reads=[("sga",), ("sgb",)],
              writes=[("g2d",)])
        P.dma("gp", sef_bc[:], bass.AP(se2_d.tensor, 0, [[0, 128], [1, 2]]),
              reads=[("se2d",)], writes=[("sefbc",)])
        P.dma("gp", gbc2[:], bass.AP(g2_d.tensor, 0, [[0, 128], [1, 2]]),
              reads=[("g2d",)], writes=[("gbc2",)])
        P.dma("gp", se21[:], se2_d.rearrange("one two -> two one"),
              reads=[("se2d",)], writes=[("se21",)])
        P.op("v", lambda e: e.tensor_copy(idx1e[:], se21[:]),
             [("se21",)], [("idx1",)])
        for sel in range(2):
            P.op("v", lambda e, sel=sel: e.tensor_scalar(
                stat[:, 14:15], sef_bc[:, sel:sel + 1], 1024.0, None, ALU.mult),
                [("sefbc",)], [S(14)])
            for dc in range(ND):
                P.op("v", lambda e, dc=dc: e.scalar_tensor_tensor(
                    stat[:, 15:16], iop_s[:], float(dc * 128), stat[:, 14:15],
                    ALU.add, ALU.add), [("iop",), S(14)], [S(15)])
                P.op("v", lambda e, sel=sel, dc=dc: e.tensor_copy(
                    idxg[:, sel * 8 + dc: sel * 8 + dc + 1], stat[:, 15:16]),
                    [S(15)], [("idxg", sel, dc)])
        P.flush()

    ph_spill_xnT(xnT_A, xnT0_d, "0")
    v_A = AT([128, NT * D], BF16, 0)
    ph_v(wv0, xnT0_d, v_A, "0", 152 * KB)

    def kept0(j):
        q0 = 128 * j
        out = []
        for c in range(NKC):
            k_lo, k_hi = 512 * c, 512 * c + 511
            if k_lo <= q0 + 127 and k_hi >= q0:
                mind = 0
            else:
                mind = min(abs(q0 - k_hi), abs(k_lo - (q0 + 127)))
            if mind >= WIN:
                continue
            if k_hi < q0:
                out.append((c, "l", float(np.float32(GAMMA ** float(q0 - k_lo)))))
            elif k_lo > q0 + 127:
                out.append((c, "r", float(np.float32(GAMMA ** float(k_lo - q0)))))
            else:
                out.append((c, "d", (q0 % 512) // 128))
        return out

    ph_attn(kT_A, v_A, qT0_d, attn0_d, "0", causal=False, kept_fn=kept0)

    ewf_sb = AT([128, ND * 1024], F32, 0)
    ew_sb = AT([128, 2 * ND * 1024], BF16, 32 * KB)
    ebf_sb = AT([2, 1024], F32, 128 * KB)
    ebr_sb = AT([1, 2 * 1024], F32, 132 * KB)
    eb_sb = AT([1, 2 * 1024], BF16, 140 * KB)
    xcD = [AT([128, ND * 128], BF16, 144 * KB + i * 2 * KB) for i in range(2)]
    accD = AT([128, D], F32, 148 * KB)
    gtD = AT([128, D], F32, 152 * KB)
    x0t = [AT([128, D], F32, 156 * KB + i * 4 * KB) for i in range(2)]
    a0t = [AT([128, D], F32, 164 * KB + i * 4 * KB) for i in range(2)]
    for sel in range(2):
        for dc in range(ND):
            P.op("gp", lambda e, sel=sel, dc=dc: e.indirect_dma_start(
                ewf_sb[:, dc * 1024:(dc + 1) * 1024], None, ew_g,
                bass.IndirectOffsetOnAxis(
                    ap=idxg[:, sel * 8 + dc: sel * 8 + dc + 1], axis=0)),
                reads=[("idxg", sel, dc)], writes=[("ewf", dc)])
        P.op("v", lambda e, sel=sel: e.tensor_copy(
            ew_sb[:, sel * ND * 1024:(sel + 1) * ND * 1024], ewf_sb[:]),
            [("ewf", dc) for dc in range(ND)], [("ew", sel)])
    P.op("gp", lambda e: e.indirect_dma_start(
        ebf_sb[:], None, eb0,
        bass.IndirectOffsetOnAxis(ap=idx1e[:], axis=0)),
        reads=[("idx1",)], writes=[("ebf", 0)])
    P.dma("gp", eb2_d, ebf_sb[:], reads=[("ebf", 0)], writes=[("eb2d",)])
    P.dma("ld", ebr_sb[:],
          bass.AP(eb2_d.tensor, 0, [[2048, 1], [1, 2048]]),
          reads=[("eb2d",)], writes=[("ebr",)])
    P.op("v", lambda e: e.tensor_copy(eb_sb[:], ebr_sb[:]),
         [("ebr",)], [("eb",)])
    with (nc.psum_tensor([128, 1024], F32) as pe0,
          nc.psum_tensor([128, 1024], F32) as pe1):
        for j in range(NT):
            xb = xcD[j % 2]
            kx = ("Dxc", j % 2)
            P.dma("ld", xb[:].rearrange("p (dc k) -> p dc k", dc=ND),
                  xnT0_d[:, :, j * 128:(j + 1) * 128]
                  .rearrange("dc p k -> p dc k"),
                  reads=[("xnTd0", dc) for dc in range(ND)], writes=[kx])
            P.dma("ld", x0t[j % 2][:], x0[j * 128:(j + 1) * 128, :],
                  writes=[("Dx0", j % 2)])
            P.dma("ld", a0t[j % 2][:], attn0_d[j * 128:(j + 1) * 128, :],
                  reads=[("attn_d0", j)], writes=[("Da0", j % 2)])
            for sel in range(2):
                pe = (pe0, pe1)[sel]
                kp = ("Dpe", sel)
                for n in range(2):
                    P.op("t", lambda e, pe=pe, sel=sel, n=n: e.matmul(
                        pe[:, n * 512:(n + 1) * 512], ones_bf[:],
                        eb_sb[0:1, sel * 1024 + n * 512:
                              sel * 1024 + (n + 1) * 512],
                        start=True, stop=False),
                        [("eb",), KONE], [kp])
                for dc in range(ND):
                    for n in range(2):
                        P.op("t", lambda e, pe=pe, xb=xb, sel=sel, dc=dc, n=n:
                             e.matmul(pe[:, n * 512:(n + 1) * 512],
                                      xb[:, dc * 128:(dc + 1) * 128],
                                      ew_sb[:, sel * ND * 1024 + dc * 1024 +
                                            n * 512:
                                            sel * ND * 1024 + dc * 1024 +
                                            (n + 1) * 512],
                                      start=False, stop=(dc == ND - 1)),
                             [("ew", sel), kx], [kp])
                P.op("s", lambda e, pe=pe: e.activation(gtD[:], pe[:], AF.Gelu),
                     [kp], [("Dgt",)])
                if sel == 0:
                    P.op("v", lambda e: e.tensor_scalar(
                        accD[:], gtD[:], gbc2[:, 0:1], None, ALU.mult),
                        [("Dgt",), ("gbc2",)], [("Dacc",)])
                else:
                    P.op("v", lambda e: e.scalar_tensor_tensor(
                        accD[:], gtD[:], gbc2[:, 1:2], accD[:],
                        ALU.mult, ALU.add),
                        [("Dgt",), ("gbc2",), ("Dacc",)], [("Dacc",)])
            P.op("v", lambda e, j=j: e.tensor_tensor(
                accD[:], accD[:], x0t[j % 2][:], ALU.add),
                [("Dacc",), ("Dx0", j % 2)], [("Dacc",)])
            P.op("v", lambda e, j=j: e.tensor_tensor(
                x0t[j % 2][:], accD[:], a0t[j % 2][:], ALU.add),
                [("Dacc",), ("Da0", j % 2)], [("Dx0", j % 2)])
            P.dma("so", x1_d[j * 128:(j + 1) * 128, :], x0t[j % 2][:],
                  reads=[("Dx0", j % 2)], writes=[("x1d", j)])
        P.flush()

    xnT_E = AT([128, ND * T], BF16, 0)
    kT_E = AT([128, ND * T], BF16, 64 * KB)
    ph_ln(x1_d, xnT_E, "1", src_key=lambda j: ("x1d", j), xnf_d=xn1f_d)
    ph_plain_kq(wk1, wq1, xnT_E, kT_E, None, "1")
    ph_spill_xnT(xnT_E, xn1T_d, "1")
    v_E = AT([128, NT * D], BF16, 0)
    ph_v(wv1, xn1T_d, v_E, "1", 160 * KB, wq_d=wq1, qtm_d=q1tm_d)

    ph_attn_own(kT_E, v_E, q1tm_d, attn1_d, "1")


    wTp = AT([128, NHT * 128], BF16, 0)
    gmb = AT([128, D], F32, 96 * KB)
    bmb = AT([128, D], F32, 100 * KB)
    x1o = [AT([128, D], F32, 104 * KB + i * 4 * KB) for i in range(2)]
    xn2n = [AT([128, D], F32, 112 * KB), AT([128, D], F32, 133 * KB)]
    xn2b = [AT([128, D], BF16, 116 * KB), AT([128, D], BF16, 137 * KB)]
    x2T = [AT([128, ND * 128], BF16, 118 * KB),
           AT([128, ND * 128], BF16, 139 * KB)]
    wr_sb = AT([128, ND * 8], BF16, 120 * KB)
    br_sb = AT([1, 8], BF16, 121 * KB)
    w_all = AT([128, NHT * 8], F32, 122 * KB)
    probs = [AT([128, 8], F32, 130 * KB), AT([128, 8], F32, 141 * KB)]
    g0scr = [AT([128, D], F32, 124 * KB), AT([128, D], F32, 142 * KB)]
    tb8 = [AT([128, 16], F32, 131 * KB), AT([128, 16], F32, 146 * KB)]
    P.dma("ld", gmb[:], gamma_d, writes=[("gmb",)])
    P.dma("ld", bmb[:], beta_d, writes=[("bmb",)])
    P.dma("gp", wr_sb[:].rearrange("p (dc e) -> p dc e", dc=ND),
          Wr_d.rearrange("(dc p) e -> p dc e", p=128), writes=[("wr",)])
    P.dma("gp", br_sb[:], br_d, writes=[("brs",)])
    P.op("v", lambda e: e.memset(wTp[:], 0.0), writes=[("wTpz",)])
    with (nc.psum_tensor([128, 128], BF16) as pt0,
          nc.psum_tensor([128, 128], BF16) as pt1,
          nc.psum_tensor([128, 8], F32) as pr_,
          nc.psum_tensor([128, 128], F32) as pw_):
        for r in range(NHT):
            par = r % 2
            b = 16 * par
            rb = 32 + 16 * par
            xb = x1o[par]
            xn2np, xn2bp, x2Tp = xn2n[par], xn2b[par], x2T[par]
            probp, g0scrp, tb8p = probs[par], g0scr[par], tb8[par]
            kx = ("G0x", par)
            P.op("gp", lambda e, xb=xb, r=r: dyn_load(e, xb[:], xn1f_d, r),
                reads=[("idx",)] + [("xnf_d", jj) for jj in range(NT)],
                writes=[kx])
            P.op("v", lambda e, xb=xb, b=b, xn2np=xn2np, g0scrp=g0scrp: e.tensor_reduce(
                stat[:, b + 0:b + 1], xb[:], AX.X, ALU.add), [kx], [S(b + 0)])
            P.op("s", lambda e, xb=xb, b=b, xn2np=xn2np, g0scrp=g0scrp: e.activation(
                g0scrp[:], xb[:], AF.Square, accum_out=stat[:, b + 1:b + 2]),
                [kx], [("G0scr", par), S(b + 1)])
            P.op("v", lambda e, b=b, rb=rb, xn2np=xn2np, xn2bp=xn2bp, x2Tp=x2Tp, probp=probp, g0scrp=g0scrp, tb8p=tb8p: e.tensor_scalar(
                stat[:, b + 2:b + 3], stat[:, b + 0:b + 1], 1.0 / D, None, ALU.mult),
                [S(b + 0)], [S(b + 2)])
            P.op("v", lambda e, b=b, rb=rb, xn2np=xn2np, xn2bp=xn2bp, x2Tp=x2Tp, probp=probp, g0scrp=g0scrp, tb8p=tb8p: e.tensor_tensor(
                stat[:, b + 3:b + 4], stat[:, b + 2:b + 3], stat[:, b + 2:b + 3], ALU.mult),
                [S(b + 2)], [S(b + 3)])
            P.op("v", lambda e, b=b, rb=rb, xn2np=xn2np, xn2bp=xn2bp, x2Tp=x2Tp, probp=probp, g0scrp=g0scrp, tb8p=tb8p: e.tensor_scalar(
                stat[:, b + 4:b + 5], stat[:, b + 1:b + 2], 1.0 / D, None, ALU.mult),
                [S(b + 1)], [S(b + 4)])
            P.op("v", lambda e, b=b, rb=rb, xn2np=xn2np, xn2bp=xn2bp, x2Tp=x2Tp, probp=probp, g0scrp=g0scrp, tb8p=tb8p: e.tensor_tensor(
                stat[:, b + 5:b + 6], stat[:, b + 4:b + 5], stat[:, b + 3:b + 4], ALU.subtract),
                [S(b + 3), S(b + 4)], [S(b + 5)])
            P.op("v", lambda e, b=b, rb=rb, xn2np=xn2np, xn2bp=xn2bp, x2Tp=x2Tp, probp=probp, g0scrp=g0scrp, tb8p=tb8p: e.tensor_scalar(
                stat[:, b + 5:b + 6], stat[:, b + 5:b + 6], 1e-5, None, ALU.add),
                [S(b + 5)], [S(b + 5)])
            P.op("s", lambda e, b=b, rb=rb, xn2np=xn2np, xn2bp=xn2bp, x2Tp=x2Tp, probp=probp, g0scrp=g0scrp, tb8p=tb8p: e.activation(
                stat[:, b + 6:b + 7], stat[:, b + 5:b + 6], AF.Sqrt), [S(b + 5)], [S(b + 6)])
            P.op("v", lambda e, b=b, rb=rb, xn2np=xn2np, xn2bp=xn2bp, x2Tp=x2Tp, probp=probp, g0scrp=g0scrp, tb8p=tb8p: e.reciprocal(stat[:, b + 7:b + 8], stat[:, b + 6:b + 7]),
                 [S(b + 6)], [S(b + 7)])
            P.op("v", lambda e, b=b, rb=rb, xn2np=xn2np, xn2bp=xn2bp, x2Tp=x2Tp, probp=probp, g0scrp=g0scrp, tb8p=tb8p: e.scalar_tensor_tensor(
                stat[:, b + 8:b + 9], stat[:, b + 2:b + 3], -1.0, stat[:, b + 7:b + 8],
                ALU.mult, ALU.mult), [S(b + 2), S(b + 7)], [S(b + 8)])
            P.op("s", lambda e, xb=xb, b=b, xn2np=xn2np, g0scrp=g0scrp: e.activation(
                xn2np[:], xb[:], AF.Identity, bias=stat[:, b + 8:b + 9], scale=stat[:, b + 7:b + 8]),
                [kx, S(b + 7), S(b + 8)], [("G0n", par)])
            P.op("v", lambda e, b=b, rb=rb, xn2np=xn2np, xn2bp=xn2bp, x2Tp=x2Tp, probp=probp, g0scrp=g0scrp, tb8p=tb8p: e.tensor_tensor(
                xn2np[:], xn2np[:], gmb[:], ALU.mult),
                [("G0n", par), ("gmb",)], [("G0n", par)])
            P.op("v", lambda e, b=b, rb=rb, xn2np=xn2np, xn2bp=xn2bp, x2Tp=x2Tp, probp=probp, g0scrp=g0scrp, tb8p=tb8p: e.tensor_tensor(
                xn2bp[:], xn2np[:], bmb[:], ALU.add),
                [("G0n", par), ("bmb",)], [("G0b", par)])
            for dc in range(ND):
                pp = (pt0, pt1)[dc % 2]
                kp = ("G0pt", dc % 2)
                P.op("t", lambda e, pp=pp, dc=dc, xn2bp=xn2bp: e.transpose(
                    pp[:], xn2bp[:, dc * 128:(dc + 1) * 128], ident_bf[:]),
                    [("G0b", par), KID], [kp])
                P.op("v", lambda e, pp=pp, dc=dc, x2Tp=x2Tp: e.tensor_copy(
                    x2Tp[:, dc * 128:(dc + 1) * 128], pp[:]),
                    [kp], [("G0T", par, dc)])
            for dc in range(ND):
                P.dma("so", xn2T_d[dc][:, r * 128:(r + 1) * 128],
                      x2Tp[:, dc * 128:(dc + 1) * 128],
                      reads=[("G0T", par, dc)], writes=[("x2Td", r, dc)])
            P.op("t", lambda e, b=b, rb=rb, xn2np=xn2np, xn2bp=xn2bp, x2Tp=x2Tp, probp=probp, g0scrp=g0scrp, tb8p=tb8p: e.matmul(pr_[:], ones_bf[:], br_sb[:],
                                         start=True, stop=False),
                 [("brs",), KONE], [("G0pr",)])
            for dc in range(ND):
                P.op("t", lambda e, dc=dc, x2Tp=x2Tp: e.matmul(
                    pr_[:], x2Tp[:, dc * 128:(dc + 1) * 128],
                    wr_sb[:, dc * 8:(dc + 1) * 8],
                    start=False, stop=(dc == ND - 1)),
                    [("wr",), ("G0T", par, dc)], [("G0pr",)])
            P.op("v", lambda e, b=b, rb=rb, xn2np=xn2np, xn2bp=xn2bp, x2Tp=x2Tp, probp=probp, g0scrp=g0scrp, tb8p=tb8p: e.tensor_copy(probp[:], pr_[:]),
                 [("G0pr",)], [("G0p", par)])
            P.op("v", lambda e, b=b, rb=rb, xn2np=xn2np, xn2bp=xn2bp, x2Tp=x2Tp, probp=probp, g0scrp=g0scrp, tb8p=tb8p: e.tensor_reduce(
                stat[:, rb + 0:rb + 1], probp[:], AX.X, ALU.max), [("G0p", par)], [S(9)])
            P.op("v", lambda e, b=b, rb=rb, xn2np=xn2np, xn2bp=xn2bp, x2Tp=x2Tp, probp=probp, g0scrp=g0scrp, tb8p=tb8p: e.tensor_scalar(
                stat[:, rb + 1:rb + 2], stat[:, rb + 0:rb + 1], -1.0, None, ALU.mult),
                [S(9)], [S(10)])
            P.op("s", lambda e, b=b, rb=rb, xn2np=xn2np, xn2bp=xn2bp, x2Tp=x2Tp, probp=probp, g0scrp=g0scrp, tb8p=tb8p: e.activation(
                probp[:], probp[:], AF.Exp, bias=stat[:, rb + 1:rb + 2],
                accum_out=stat[:, rb + 2:rb + 3]),
                [("G0p", par), S(10)], [("G0p", par), S(11)])
            P.op("v", lambda e, b=b, rb=rb, xn2np=xn2np, xn2bp=xn2bp, x2Tp=x2Tp, probp=probp, g0scrp=g0scrp, tb8p=tb8p: e.reciprocal(stat[:, rb + 3:rb + 4], stat[:, rb + 2:rb + 3]),
                 [S(11)], [S(12)])
            P.op("v", lambda e, b=b, rb=rb, xn2np=xn2np, xn2bp=xn2bp, x2Tp=x2Tp, probp=probp, g0scrp=g0scrp, tb8p=tb8p: e.tensor_scalar(
                probp[:], probp[:], stat[:, rb + 3:rb + 4], None, ALU.mult),
                [("G0p", par), S(12)], [("G0p", par)])
            P.op("v", lambda e, b=b, rb=rb, xn2np=xn2np, xn2bp=xn2bp, x2Tp=x2Tp, probp=probp, g0scrp=g0scrp, tb8p=tb8p: e.tensor_reduce(
                stat[:, rb + 0:rb + 1], probp[:], AX.X, ALU.max), [("G0p", par)], [S(9)])
            P.op("v", lambda e, b=b, rb=rb, xn2np=xn2np, xn2bp=xn2bp, x2Tp=x2Tp, probp=probp, g0scrp=g0scrp, tb8p=tb8p: e.tensor_scalar(
                tb8p[:, 0:8], probp[:], stat[:, rb + 0:rb + 1], None, ALU.is_ge),
                [("G0p", par), S(9)], [("G0i", par)])
            P.op("v", lambda e, b=b, rb=rb, xn2np=xn2np, xn2bp=xn2bp, x2Tp=x2Tp, probp=probp, g0scrp=g0scrp, tb8p=tb8p: e.scalar_tensor_tensor(
                tb8p[:, 8:16], tb8p[:, 0:8], NEG, probp[:], ALU.mult, ALU.add),
                [("G0i", par), ("G0p", par)], [("G0m", par)])
            P.op("v", lambda e, b=b, rb=rb, xn2np=xn2np, xn2bp=xn2bp, x2Tp=x2Tp, probp=probp, g0scrp=g0scrp, tb8p=tb8p: e.tensor_reduce(
                stat[:, rb + 1:rb + 2], tb8p[:, 8:16], AX.X, ALU.max),
                [("G0m", par)], [S(10)])
            P.op("v", lambda e, b=b, rb=rb, xn2np=xn2np, xn2bp=xn2bp, x2Tp=x2Tp, probp=probp, g0scrp=g0scrp, tb8p=tb8p: e.tensor_scalar(
                tb8p[:, 0:8], probp[:], stat[:, rb + 1:rb + 2], None, ALU.is_ge),
                [("G0p", par), S(10)], [("G0i", par)])
            P.op("v", lambda e, r=r, probp=probp, tb8p=tb8p: e.tensor_tensor(
                w_all[:, r * 8:(r + 1) * 8], probp[:], tb8p[:, 0:8], ALU.mult),
                [("G0p", par), ("G0i", par)], [("w", r)])
            P.op("t", lambda e, r=r, probp=probp, tb8p=tb8p: e.transpose(
                pw_[0:8, :], w_all[:, r * 8:(r + 1) * 8], ident_f[:]),
                [("w", r), KIDF], [("G0pw",)])
            P.op("v", lambda e, r=r, probp=probp, tb8p=tb8p: e.tensor_copy(
                wTp[0:8, r * 128:(r + 1) * 128], pw_[0:8, :]),
                [("G0pw",), ("wTpz",)], [("wT", r)])
        P.flush()

    xn2h = AT([128, ND * 1024], BF16, 4 * KB)
    h_sb = AT([128, NF * 1024], BF16, 20 * KB)
    w2_sb = AT([128, NF * 512], BF16, 84 * KB)
    ctl1 = AT([1, 512], F32, 116 * KB)
    cppb = AT([1, 1024], BF16, 118 * KB)
    cppP = AT([128, D], BF16, 120 * KB)
    ct_ld = AT([1, 1024], F32, 123 * KB)
    cacc = AT([1, 1024], F32, 127 * KB)
    b1_sb = AT([128, NF * 8], F32, 131 * KB)
    acc_h = AT([128, 8 * D], F32, 132 * KB)
    w1st = [AT([128, ND * 512], BF16, 164 * KB + i * 8 * KB) for i in range(2)]
    x2t = AT([128, D], F32, 180 * KB)
    f1t = [AT([128, D], F32, 184 * KB), AT([128, D], F32, 184 * KB)]
    hpd = AT([128, NF], BF16, 188 * KB)

    P.dma("ld", b1_sb[:].rearrange("p (m e) -> p m e", m=NF),
          b1T_d.rearrange("(m p) e -> p m e", p=128), writes=[("b1",)])
    P.op("v", lambda e: e.memset(cppP[:], 0.0), writes=[("cppz",)])
    P.flush()

    with (nc.psum_tensor([128, 1024], F32) as ph0,
          nc.psum_tensor([128, 1024], F32) as ph1,
          nc.psum_tensor([128, 512], F32) as po0,
          nc.psum_tensor([128, 512], F32) as po1):
        for half in range(2):
            for dc in range(ND):
                P.op("ld", lambda e, dc=dc, half=half: e.dma_start(
                    out=xn2h[:, dc * 1024:(dc + 1) * 1024],
                    in_=xn2T_d[dc][:, half * 1024:(half + 1) * 1024]),
                    reads=[("x2Td", rr, dc) for rr in range(NHT)],
                    writes=[("xn2h", dc)])
            for e_ in range(8):
                if half == 0:
                    P.op("s", lambda e, e_=e_: e.activation(
                        hpd[:], b1_sb[:].rearrange(
                            "p (m e) -> p m e", m=NF)[:, :, e_], AF.Relu),
                        [("b1",)], [("hpd",)])
                for fg in range(NF // 4):
                    wb = w1st[fg % 2]
                    kw = ("w1", fg % 2)
                    P.op("ld", lambda e, wb=wb, e_=e_, fg=fg: e.dma_start(
                        out=wb[:].rearrange("p (dc f) -> p dc f", dc=ND),
                        in_=w1b_d[e_][:, fg * 512:(fg + 1) * 512]
                        .rearrange("(dc p) f -> p dc f", p=128)),
                        writes=[kw], pre=[(wbg, "wbg", 256)])
                    for fs in range(4):
                        ft = fg * 4 + fs
                        ph = (ph0, ph1)[ft % 2]
                        kp = ("ph", ft % 2)
                        for dc in range(ND):
                            for n in range(2):
                                P.op("t", lambda e, ph=ph, wb=wb, dc=dc, n=n,
                                     fs=fs:
                                     e.matmul(ph[:, n * 512:(n + 1) * 512],
                                              wb[:, dc * 512 + fs * 128:
                                                 dc * 512 + (fs + 1) * 128],
                                              xn2h[:, dc * 1024 + n * 512:
                                                   dc * 1024 + (n + 1) * 512],
                                              start=(dc == 0),
                                              stop=(dc == ND - 1)),
                                     [kw, ("xn2h", dc)], [kp])
                        P.op("s", lambda e, ph=ph, ft=ft, e_=e_: e.activation(
                            h_sb[:, ft * 1024:(ft + 1) * 1024], ph[:], AF.Relu,
                            bias=b1_sb[:, ft * 8 + e_: ft * 8 + e_ + 1]),
                            [kp, ("b1",)], [("h", ft)])
                for h2c in range(2):
                    P.op("ld", lambda e, e_=e_, h2c=h2c: e.dma_start(
                        out=w2_sb[:].rearrange("p (fc h) -> p fc h", fc=NF),
                        in_=w2b_d[e_][:, h2c * 512:(h2c + 1) * 512]
                        .rearrange("(fc p) h -> p fc h", p=128)),
                        writes=[("w2",)], pre=[(wbg, "wbg", 256)])
                    for r in range(8):
                        po = (po0, po1)[r % 2]
                        kp = ("po", r % 2)
                        for ft in range(NF):
                            P.op("t", lambda e, po=po, ft=ft, r=r: e.matmul(
                                po[:], h_sb[:, ft * 1024 + r * 128:
                                            ft * 1024 + (r + 1) * 128],
                                w2_sb[:, ft * 512:(ft + 1) * 512],
                                start=(ft == 0), stop=(ft == NF - 1)),
                                [("w2",), ("h", ft)], [kp])
                        rg = half * 8 + r
                        if e_ == 0:
                            P.op("v", lambda e, po=po, r=r, h2c=h2c, rg=rg:
                                 e.tensor_scalar(
                                     acc_h[:, r * D + h2c * 512:
                                           r * D + h2c * 512 + 512],
                                     po[:], w_all[:, rg * 8: rg * 8 + 1],
                                     None, ALU.mult),
                                 [kp, ("w", rg)], [("acc", r, h2c)])
                        else:
                            P.op("v", lambda e, po=po, r=r, h2c=h2c, rg=rg,
                                 e_=e_: e.scalar_tensor_tensor(
                                     acc_h[:, r * D + h2c * 512:
                                           r * D + h2c * 512 + 512],
                                     po[:],
                                     w_all[:, rg * 8 + e_: rg * 8 + e_ + 1],
                                     acc_h[:, r * D + h2c * 512:
                                           r * D + h2c * 512 + 512],
                                     ALU.mult, ALU.add),
                                 [kp, ("w", rg), ("acc", r, h2c)],
                                 [("acc", r, h2c)])
                    if half == 0:
                        po = po0
                        kp = ("po", 0)
                        for ft in range(NF):
                            P.op("t", lambda e, po=po, ft=ft: e.matmul(
                                po[0:1, :], hpd[:, ft:ft + 1],
                                w2_sb[:, ft * 512:(ft + 1) * 512],
                                start=(ft == 0), stop=(ft == NF - 1)),
                                [("hpd",), ("w2",)], [kp])
                        P.op("v", lambda e, po=po: e.tensor_copy(
                            ctl1[:], po[0:1, :]), [kp], [("ctl1",)])
                        P.dma("gp", ctil_d[e_:e_ + 1, h2c * 512:(h2c + 1) * 512],
                              ctl1[:], reads=[("ctl1",)],
                              writes=[("ctd", e_, h2c)])
            if half == 0:
                ctr = [("ctd", e2, h2) for e2 in range(8) for h2 in range(2)]
                P.dma("ld", cacc[:], ctil_d[0:1, :], reads=ctr,
                      writes=[("cacc",)])
                for e2 in range(1, 8):
                    P.dma("ld", ct_ld[:], ctil_d[e2:e2 + 1, :], reads=ctr,
                          writes=[("ctld",)])
                    P.op("v", lambda e: e.tensor_tensor(
                        cacc[:], cacc[:], ct_ld[:], ALU.add),
                        [("cacc",), ("ctld",)], [("cacc",)])
                for e2 in range(8):
                    P.dma("ld", ct_ld[:], b2_d[e2:e2 + 1, :], writes=[("ctld",)])
                    P.op("v", lambda e: e.tensor_tensor(
                        cacc[:], cacc[:], ct_ld[:], ALU.add),
                        [("cacc",), ("ctld",)], [("cacc",)])
                for e2 in range(8):
                    P.dma("ld", ct_ld[:], ctil_d[e2:e2 + 1, :], reads=ctr,
                          writes=[("ctld",)])
                    P.op("v", lambda e: e.tensor_tensor(
                        cppb[:], cacc[:], ct_ld[:], ALU.subtract),
                        [("cacc",), ("ctld",)], [("cppb",)])
                    P.dma("gp", cpp_d[e2:e2 + 1, :], cppb[:],
                          reads=[("cppb",)], writes=[("cppd", e2)])
                P.dma("ld", cppP[0:8, :], cpp_d[:, :],
                      reads=[("cppd", e2) for e2 in range(8)] + [("cppz",)],
                      writes=[("cppP",)])
            for r in range(8):
                rg = half * 8 + r
                for n in range(2):
                    P.op("t", lambda e, po=(po0, po1)[n], rg=rg, n=n:
                         e.matmul(po[:],
                                  wTp[:, rg * 128:(rg + 1) * 128],
                                  cppP[:, n * 512:(n + 1) * 512],
                                  start=True, stop=True),
                         [("wT", rg), ("cppP",)], [("po", n)])
                for n in range(2):
                    P.op("v", lambda e, po=(po0, po1)[n], r=r, n=n:
                         e.scalar_tensor_tensor(
                             acc_h[:, r * D + n * 512: r * D + (n + 1) * 512],
                             po[:], 1.0,
                             acc_h[:, r * D + n * 512: r * D + (n + 1) * 512],
                             ALU.mult, ALU.add),
                         [("po", n), ("acc", r, 0), ("acc", r, 1)],
                         [("acc", r, n)])
                fb = f1t[0]
                kf = ("f1", 0)
                P.op("gp", lambda e, fb=fb, rg=rg: dyn_load(e, fb[:], x1_d, rg),
                    reads=[("idx",)] + [("x1d", jj) for jj in range(NT)],
                    writes=[kf])
                P.op("v", lambda e, fb=fb, r=r: e.tensor_tensor(
                    x2t[:], acc_h[:, r * D:(r + 1) * D], fb[:], ALU.add),
                    [("acc", r, 0), ("acc", r, 1), kf], [("x2t",)])
                P.dma("ld", fb[:], attn1_d[rg * 128:(rg + 1) * 128, :],
                      reads=[("x2t",), ("attn_d1", rg)], writes=[kf])
                P.op("v", lambda e, fb=fb: e.tensor_tensor(
                    x2t[:], x2t[:], fb[:], ALU.add), [("x2t",), kf], [("x2t",)])
                P.op("gp", lambda e, fb=fb, rg=rg: dyn_load(
                    e, fb[:], xn1f_d, rg),
                    reads=[("idx",), ("x2t",)] +
                          [("xnf_d", jj) for jj in range(NT)],
                    writes=[kf])
                P.op("v", lambda e, fb=fb: e.tensor_tensor(
                    x2t[:], x2t[:], fb[:], ALU.add), [("x2t",), kf], [("x2t",)])
                P.dma("so", x2own[rg * 128:(rg + 1) * 128, :], x2t[:],
                      reads=[("x2t",)], writes=[("out", rg)])
        P.flush()

    P.op("ld", lambda e: e.nop(),
         deps=[P.lastw[("out", rg)] for rg in range(NHT)])
    P.flush()
    return nc


def kernel(**inputs):
    tabs = _host_tables()
    cm2 = (tabs.pop("cmask2_h0"), tabs.pop("cmask2_h1"))
    g = _prep_inputs(inputs)
    nc = build_program()
    X = inputs["L0_attn_X"]
    base = {}
    base.update(tabs)
    base.update({"wq0d": g["wq0d"], "wk0d": g["wk0d"], "wv0": g["wv0"],
                 "gwT": g["gwT"], "gbT": g["gbT"], "ew_g": g["ew_g"],
                 "eb0": g["eb0"], "eids_row": tabs["eids_row"],
                 "iota_p": tabs["iota_p"], "wq1": g["wq1"], "wk1": g["wk1"],
                 "wv1": g["wv1"], "gamma_bc": g["gamma_bc"],
                 "beta_bc": g["beta_bc"], "Wr": g["Wr"], "br_row": g["br_row"],
                 "W1": g["W1"], "b1T": g["b1T"], "W2": g["W2"], "b2": g["b2"]})
    in_maps = []
    for c in range(8):
        m = dict(base)
        m["x0"] = np.ascontiguousarray(X[c // 2])
        h = c % 2
        idx = np.empty((128, NHT), dtype=np.int32)
        for r in range(NHT):
            idx[:, r] = (2 * r + h) * 128 + np.arange(128)
        m["idx_own"] = idx
        m["cmask2"] = cm2[h]
        in_maps.append(m)
    res = run_bass_kernel_spmd(nc, in_maps, core_ids=list(range(8)))
    out = np.empty((B, T, D), dtype=np.float32)
    for b in range(B):
        ob = out[b].reshape(NT, 128, D)
        ob[0::2] = res.results[2 * b]["x2own"].reshape(NHT, 128, D)
        ob[1::2] = res.results[2 * b + 1]["x2own"].reshape(NHT, 128, D)
    return out




# version 20
# speedup vs baseline: 1.0942x; 1.0416x over previous
import numpy as np
import ml_dtypes
import concourse.bass as bass
import concourse.mybir as mybir
from concourse.bass_utils import run_bass_kernel_spmd

F32 = mybir.dt.float32
BF16 = mybir.dt.bfloat16
AF = mybir.ActivationFunctionType
ALU = mybir.AluOpType
AX = mybir.AxisListType

B, T, D = 4, 4096, 1024
NT = T // 128
NKC = T // 512
ND = D // 128
HTOK = T // 2
NHT = HTOK // 128
GAMMA = 0.9
WIN = 1024
FF = 4096
NF = FF // 128
NEG = -1e30
BF_NP = ml_dtypes.bfloat16
KB = 1024


def _host_tables():
    import jax
    import jax.numpy as jnp
    half = D // 2
    try:
        _cpu = jax.devices("cpu")[0]
    except Exception:
        _cpu = None
    import contextlib
    with (jax.default_device(_cpu) if _cpu is not None
          else contextlib.nullcontext()):
        scale_vec = (jnp.arange(0, D, 2, dtype=jnp.float32) + 0.4 * D) / (1.4 * D)
        power = jnp.arange(T, dtype=jnp.float32) / 512.0
        scale = scale_vec[None, :] ** power[:, None]
        inv_scale = 1.0 / scale
        inv_freq = 1.0 / 10000.0 ** (jnp.arange(half, dtype=jnp.float32) / half)
        sinusoid = jnp.arange(T, dtype=jnp.float32)[:, None] * inv_freq[None, :]
        cosq = np.asarray(jnp.cos(sinusoid) * scale)
        sinq = np.asarray(jnp.sin(sinusoid) * scale)
        cosk = np.asarray(jnp.cos(sinusoid) * inv_scale)
        sink = np.asarray(jnp.sin(sinusoid) * inv_scale)
    t = {}
    t["cosq"] = np.ascontiguousarray(cosq.T)
    t["sinq"] = np.ascontiguousarray(sinq.T)
    t["cosk"] = np.ascontiguousarray(cosk.T)
    t["sink"] = np.ascontiguousarray(sink.T)
    i = np.arange(128, dtype=np.float64)
    j = np.arange(512, dtype=np.float64)
    t["gvpos"] = (GAMMA ** i).astype(np.float32).reshape(128, 1)
    t["gvneg"] = (GAMMA ** (-i)).astype(np.float32).reshape(128, 1)
    t["gneg"] = np.ascontiguousarray(
        np.broadcast_to((GAMMA ** (-j)).astype(np.float32), (128, 512)))
    t["gpos"] = np.ascontiguousarray(
        np.broadcast_to((GAMMA ** j).astype(np.float32), (128, 512)))
    dd = np.zeros((128, 4 * 512), dtype=np.float32)
    for r in range(4):
        dd[:, r * 512:(r + 1) * 512] = (
            GAMMA ** np.abs(128.0 * r + i[:, None] - j[None, :])
        ).astype(np.float32)
    t["ddiag"] = dd
    cm = np.zeros((128, 128), dtype=np.float32)
    cm[np.triu_indices(128, 1)] = NEG
    t["cmask"] = cm
    i = np.arange(128)[:, None]
    c = np.arange(256)[None, :]
    t["cmask2_h0"] = np.where(c <= i, 0.0, NEG).astype(np.float32)
    t["cmask2_h1"] = np.where(c <= 128 + i, 0.0, NEG).astype(np.float32)
    t["ident_bf"] = np.eye(128, dtype=BF_NP)
    t["ident_f"] = np.eye(128, dtype=np.float32)
    t["ones_row"] = np.ones((1, 128), dtype=BF_NP)
    t["eids_row"] = np.arange(8, dtype=np.float32).reshape(1, 8)
    t["iota_p"] = np.arange(128, dtype=np.float32).reshape(128, 1)
    return t


def _prep_inputs(inp):
    g = {}
    wq, wk = inp["L0_attn_W_Q"], inp["L0_attn_W_K"]
    g["wq0d"] = np.ascontiguousarray(np.concatenate([wq[:, 0::2], wq[:, 1::2]], 1))
    g["wk0d"] = np.ascontiguousarray(np.concatenate([wk[:, 0::2], wk[:, 1::2]], 1))
    g["wv0"] = np.ascontiguousarray(inp["L0_attn_W_V"])
    g["gwT"] = np.ascontiguousarray(inp["L0_ffn_gate_w"].T)
    g["gbT"] = np.ascontiguousarray(inp["L0_ffn_gate_b"].reshape(8, 1))
    g["ew_g"] = np.ascontiguousarray(
        np.concatenate([inp["L0_ffn_expert_w"][e].T for e in range(8)], axis=0))
    g["eb0"] = np.ascontiguousarray(inp["L0_ffn_expert_b"])
    g["wq1"] = np.ascontiguousarray(inp["L1_attn_w_q"])
    g["wk1"] = np.ascontiguousarray(inp["L1_attn_w_k"])
    g["wv1"] = np.ascontiguousarray(inp["L1_attn_w_v"])
    g["gamma_bc"] = np.ascontiguousarray(np.broadcast_to(inp["L1_ffn_gamma"], (128, D)))
    g["beta_bc"] = np.ascontiguousarray(np.broadcast_to(inp["L1_ffn_beta"], (128, D)))
    g["Wr"] = np.ascontiguousarray(inp["L1_ffn_Wr"])
    g["br_row"] = np.ascontiguousarray(inp["L1_ffn_br"].reshape(1, 8))
    g["W1"] = np.ascontiguousarray(inp["L1_ffn_W1"])
    g["b1T"] = np.ascontiguousarray(inp["L1_ffn_b1"].T)
    g["W2"] = np.ascontiguousarray(inp["L1_ffn_W2"])
    g["b2"] = np.ascontiguousarray(inp["L1_ffn_b2"])
    return g


class Prog:

    QSEM = {"ld": 16, "gp": 16, "so": 16, "t": 1, "v": 1, "s": 1}
    SEC = {"ld": "sync", "gp": "gpsimd", "so": "scalar", "s": "scalar",
           "t": "tensor", "v": "vector"}
    DMAQ = ("ld", "gp", "so")
    NPOOL = 16

    def __init__(self, nc):
        self.nc = nc
        self.sem = {q: nc.alloc_semaphore("q_" + q) for q in self.QSEM}
        self.done = {q: nc.alloc_semaphore("d_" + q) for q in self.DMAQ}
        self.pool = {q: [nc.alloc_semaphore(f"p_{q}{i}")
                         for i in range(self.NPOOL)] for q in self.DMAQ}
        self.pool_cnt = {q: [0] * self.NPOOL for q in self.DMAQ}
        self.pool_rr = {q: 0 for q in self.DMAQ}
        self.cnt = {q: 0 for q in self.QSEM}
        self.done_cnt = {q: 0 for q in self.DMAQ}
        self.all = []
        self.start = 0
        self.lastw = {}
        self.readers = {}
        self.waited = {}

    def op(self, q, fn, reads=(), writes=(), deps=(), pre=(), bgsem=None):
        i = len(self.all)
        dep = set(d for d in deps if d is not None)
        for r in reads:
            w = self.lastw.get(r)
            if w is not None:
                dep.add(w)
        for w in writes:
            lw = self.lastw.get(w)
            if lw is not None:
                dep.add(lw)
            dep |= self.readers.get(w, set())
        dep.discard(i)
        self.all.append(dict(q=q, fn=fn, deps=dep, ticket=None, psem=None,
                             inc=False, pre=tuple(pre), bgsem=bgsem))
        for r in reads:
            self.readers.setdefault(r, set()).add(i)
        for w in writes:
            self.lastw[w] = i
            self.readers[w] = set()
        return i

    def dma(self, q, out, in_, reads=(), writes=(), deps=()):
        return self.op(q, (lambda e, o=out, s=in_: e.dma_start(out=o, in_=s)),
                       reads, writes, deps)

    def _skip(self, d, o_q, n0):
        p = self.all[d]
        if p["q"] == "t" and o_q == "t":
            return True
        if d < n0:
            return True
        return False

    def flush(self):
        n0 = self.start
        batch = self.all[n0:]
        if not batch:
            return
        prev_q = {q: self.cnt[q] for q in self.DMAQ}
        prev_pool = {q: list(self.pool_cnt[q]) for q in self.DMAQ}

        for j, o in enumerate(batch, n0):
            for d in o["deps"]:
                if self._skip(d, o["q"], n0):
                    continue
                self.all[d]["inc"] = True
        for o in batch:
            if o["bgsem"] is not None:
                continue
            if o["q"] in self.DMAQ:
                if o["inc"]:
                    qq = o["q"]
                    i = self.pool_rr[qq]
                    self.pool_rr[qq] = (i + 1) % self.NPOOL
                    o["psem"] = (qq, i)
                    o["pprev"] = self.pool_cnt[qq][i]
                    self.pool_cnt[qq][i] += 16
                    o["ticket"] = self.pool_cnt[qq][i]
                else:
                    self.cnt[o["q"]] += 16
                    o["ticket"] = self.cnt[o["q"]]
            elif o["inc"]:
                self.cnt[o["q"]] += 1
                o["ticket"] = self.cnt[o["q"]]
        fence = {}
        for q in self.DMAQ:
            self.done_cnt[q] += 1
            fence[q] = self.done_cnt[q]

        def emit(eng, queues):
            ekey = str(eng.engine)

            def waitv(semkey, sem, v):
                if v <= 0:
                    return
                key = (ekey, semkey)
                if self.waited.get(key, -1) < v:
                    eng.wait_ge(sem, v)
                    self.waited[key] = v

            for q in queues:
                if q in self.DMAQ:
                    waitv(q, self.sem[q], prev_q[q])
                    for i in range(self.NPOOL):
                        waitv(("pool", q, i), self.pool[q][i],
                              prev_pool[q][i])
                    eng.sem_inc(self.done[q], 1)
            for q in self.DMAQ:
                waitv("d_" + q, self.done[q], fence[q])

            for j, o in enumerate(batch, n0):
                if o["q"] not in queues:
                    continue
                for (sm, smkey, v) in o["pre"]:
                    waitv(smkey, sm, v)
                if o["bgsem"] is not None:
                    inst = o["fn"](eng)
                    inst.then_inc(o["bgsem"], 16)
                    continue
                if o["psem"] is not None:
                    qq, i = o["psem"]
                    waitv(("pool", qq, i), self.pool[qq][i], o["pprev"])
                for d in sorted(o["deps"]):
                    if self._skip(d, o["q"], n0):
                        continue
                    p = self.all[d]
                    pq = p["q"]
                    if pq in self.DMAQ:
                        if p["psem"] is not None:
                            qq, i = p["psem"]
                            waitv(("pool", qq, i), self.pool[qq][i],
                                  p["ticket"])
                        else:
                            assert self.SEC[pq] == self.SEC[o["q"]],                                 (d, pq, o["q"])
                            waitv(pq, self.sem[pq], p["ticket"])
                    else:
                        assert p["ticket"] is not None, (d, p["q"])
                        waitv(pq, self.sem[pq], p["ticket"])
                inst = o["fn"](eng)
                if o["q"] in self.DMAQ:
                    if o["psem"] is not None:
                        qq, i = o["psem"]
                        inst.then_inc(self.pool[qq][i], 16)
                    else:
                        inst.then_inc(self.sem[o["q"]], 16)
                elif o["inc"]:
                    inst.then_inc(self.sem[o["q"]], 1)

        with self.nc.Block() as blk:
            blk.sync(lambda e: emit(e, ("ld",)))
            blk.gpsimd(lambda e: emit(e, ("gp",)))
            blk.scalar(lambda e: emit(e, ("s", "so")))
            blk.tensor(lambda e: emit(e, ("t",)))
            blk.vector(lambda e: emit(e, ("v",)))
        self.start = len(self.all)


SB_BIG = 35840
SB_TOP = 229376
import os
_KPH = int(os.environ.get("KPH", "99"))


def build_program():
    nc = bass.Bass("TRN2", target_bir_lowering=False)

    def EI(name, shape, dtype=F32):
        return nc.dram_tensor(name, list(shape), dtype, kind="ExternalInput").ap()

    x0 = EI("x0", (T, D))
    wq0d = EI("wq0d", (D, D)); wk0d = EI("wk0d", (D, D)); wv0 = EI("wv0", (D, D))
    cosq = EI("cosq", (512, T)); sinq = EI("sinq", (512, T))
    cosk = EI("cosk", (512, T)); sink = EI("sink", (512, T))
    gwT_d = EI("gwT", (D, 8)); gbT_d = EI("gbT", (8, 1))
    ew_g = EI("ew_g", (8 * D, 1024)); eb0 = EI("eb0", (8, 1024))
    eids_d = EI("eids_row", (1, 8)); iotap_d = EI("iota_p", (128, 1))
    wq1 = EI("wq1", (D, D)); wk1 = EI("wk1", (D, D)); wv1 = EI("wv1", (D, D))
    gamma_d = EI("gamma_bc", (128, D)); beta_d = EI("beta_bc", (128, D))
    Wr_d = EI("Wr", (D, 8)); br_d = EI("br_row", (1, 8))
    W1_d = EI("W1", (8, D, FF)); b1T_d = EI("b1T", (FF, 8))
    W2_d = EI("W2", (8, FF, D)); b2_d = EI("b2", (8, 1024))
    gvpos_d = EI("gvpos", (128, 1)); gvneg_d = EI("gvneg", (128, 1))
    gneg_d = EI("gneg", (128, 512)); gpos_d = EI("gpos", (128, 512))
    ddiag_d = EI("ddiag", (128, 4 * 512)); cmask_d = EI("cmask", (128, 128))
    cmask2_d = EI("cmask2", (128, 256))
    idx_d = EI("idx_own", (128, NHT), mybir.dt.int32)
    identbf_d = EI("ident_bf", (128, 128), BF16)
    identf_d = EI("ident_f", (128, 128))
    ones_d = EI("ones_row", (1, 128), BF16)

    x2own = nc.dram_tensor("x2own", [HTOK, D], F32, kind="ExternalOutput").ap()

    def DSc(name, shape, dtype=F32):
        return nc.dram_tensor(name, list(shape), dtype, kind="Internal").ap()

    xnT0_d = DSc("xnT0_d", (ND, 128, T), BF16)
    qT0_d = DSc("qT0_d", (ND, 128, T), BF16)
    attn0_d = DSc("attn0_d", (T, D))
    x1_d = DSc("x1_d", (T, D))
    xn1f_d = DSc("xn1f_d", (T, D))
    xn1T_d = DSc("xn1T_d", (ND, 128, T), BF16)
    q1tm_d = DSc("q1tm_d", (T, D), BF16)
    attn1_d = DSc("attn1_d", (HTOK, D))
    xn2T_d = DSc("xn2T_d", (ND, 128, HTOK), BF16)
    row8_d = DSc("row8_d", (8, 1))
    grow_d = DSc("grow_d", (1, 8))
    se2_d = DSc("se2_d", (1, 2))
    g2_d = DSc("g2_d", (1, 2))
    eb2_d = DSc("eb2_d", (2, 1024))
    ctil_d = DSc("ctil_d", (8, 1024))
    w1b_d = DSc("w1b_d", (8, D, FF), BF16)
    w2b_d = DSc("w2b_d", (8, FF, D), BF16)
    cpp_d = DSc("cpp_d", (8, 1024), BF16)

    P = Prog(nc)

    ident_bf = nc.alloc_sbuf_tensor("ident_bf_s", [128, 128], BF16)
    ident_f = nc.alloc_sbuf_tensor("ident_f_s", [128, 128], F32)
    ones_bf = nc.alloc_sbuf_tensor("ones_bf_s", [1, 128], BF16)
    gv_pos = nc.alloc_sbuf_tensor("gv_pos", [128, 1], F32)
    gv_neg = nc.alloc_sbuf_tensor("gv_neg", [128, 1], F32)
    gneg_s = nc.alloc_sbuf_tensor("gneg_s", [128, 512], F32)
    gpos_s = nc.alloc_sbuf_tensor("gpos_s", [128, 512], F32)
    cmask_s = nc.alloc_sbuf_tensor("cmask_s", [128, 128], F32)
    cmask2_s = nc.alloc_sbuf_tensor("cmask2_s", [128, 256], F32)
    gates_bc = nc.alloc_sbuf_tensor("gates_bc", [128, 8], F32)
    stat = nc.alloc_sbuf_tensor("stat", [128, 64], F32)
    st8 = nc.alloc_sbuf_tensor("st8", [1, 72], F32)
    eids_s = nc.alloc_sbuf_tensor("eids_s", [1, 8], F32)
    iop_s = nc.alloc_sbuf_tensor("iop_s", [128, 1], F32)
    sef_bc = nc.alloc_sbuf_tensor("sef_bc", [128, 2], F32)
    gbc2 = nc.alloc_sbuf_tensor("gbc2", [128, 2], F32)
    idxg = nc.alloc_sbuf_tensor("idxg", [128, 16], mybir.dt.int32)
    idx1e = nc.alloc_sbuf_tensor("idx1e", [2, 1], mybir.dt.int32)
    se21 = nc.alloc_sbuf_tensor("se21", [2, 1], F32)
    idx_sb = nc.alloc_sbuf_tensor("idx_sb", [128, NHT], mybir.dt.int32)
    assert nc.bytes_allocated(bass.MemorySpace.SBUF) <= SB_BIG, \
        nc.bytes_allocated(bass.MemorySpace.SBUF)

    _atn = [0]
    def AT(shape, dtype, off):
        _atn[0] += 1
        end = SB_BIG + off + int(np.prod(shape[1:])) * mybir.dt.size(dtype)
        assert end <= SB_TOP, (shape, off, end)
        return nc.alloc_sbuf_tensor_at(f"at{_atn[0]}", list(shape), dtype,
                                       offset=SB_BIG + off)

    S = lambda c: ("stat", c)

    for dst, src in ((ident_bf, identbf_d), (ident_f, identf_d),
                     (ones_bf, ones_d), (gv_pos, gvpos_d), (gv_neg, gvneg_d),
                     (gneg_s, gneg_d), (gpos_s, gpos_d), (cmask_s, cmask_d),
                     (cmask2_s, cmask2_d)):
        P.dma("ld", dst[:], src, writes=[("const", dst.name)])
    P.dma("ld", idx_sb[:], idx_d, writes=[("idx",)])
    P.dma("ld", eids_s[:], eids_d, writes=[("eids",)])
    P.dma("ld", iop_s[:], iotap_d, writes=[("iop",)])
    KID = ("const", "ident_bf_s")
    KIDF = ("const", "ident_f_s")
    KONE = ("const", "ones_bf_s")

    def ph_ln(src_d, xnT, tag, src_key=None, xnf_d=None):
        xt = [AT([128, D], F32, 128 * KB + i * 4 * KB) for i in range(2)]
        scr = [AT([128, D], F32, 136 * KB + i * 4 * KB) for i in range(2)]
        xnb = [AT([128, D], BF16, 144 * KB + i * 2 * KB) for i in range(2)]
        xnf = [AT([128, D], F32, 148 * KB + i * 4 * KB) for i in range(2)]
        with (nc.psum_tensor([128, 128], BF16) as pt0,
              nc.psum_tensor([128, 128], BF16) as pt1):
            ptr = [pt0, pt1]
            for j in range(NT):
                par = j % 2
                b = 16 * par
                xb = xt[par]
                xnbp, xnfp, scrp = xnb[par], xnf[par], scr[par]
                kxt = (tag, "xt", par)
                P.dma("ld", xb[:], src_d[j * 128:(j + 1) * 128, :],
                      reads=([src_key(j)] if src_key else []), writes=[kxt])
                P.op("v", lambda e, xb=xb, b=b: e.tensor_reduce(
                    stat[:, b + 0:b + 1], xb[:], AX.X, ALU.add),
                    [kxt], [S(b + 0)])
                P.op("s", lambda e, xb=xb, b=b, scrp=scrp: e.activation(
                    scrp[:], xb[:], AF.Square, accum_out=stat[:, b + 1:b + 2]),
                    [kxt], [(tag, "scr", par), S(b + 1)])
                P.op("v", lambda e, b=b: e.tensor_scalar(
                    stat[:, b + 2:b + 3], stat[:, b + 0:b + 1], 1.0 / D, None,
                    ALU.mult), [S(b + 0)], [S(b + 2)])
                P.op("v", lambda e, b=b: e.tensor_tensor(
                    stat[:, b + 3:b + 4], stat[:, b + 2:b + 3],
                    stat[:, b + 2:b + 3], ALU.mult), [S(b + 2)], [S(b + 3)])
                P.op("v", lambda e, b=b: e.tensor_scalar(
                    stat[:, b + 4:b + 5], stat[:, b + 1:b + 2], 1.0 / D, None,
                    ALU.mult), [S(b + 1)], [S(b + 4)])
                P.op("v", lambda e, b=b: e.tensor_tensor(
                    stat[:, b + 5:b + 6], stat[:, b + 4:b + 5],
                    stat[:, b + 3:b + 4], ALU.subtract),
                    [S(b + 3), S(b + 4)], [S(b + 5)])
                P.op("v", lambda e, b=b: e.tensor_scalar(
                    stat[:, b + 5:b + 6], stat[:, b + 5:b + 6], 1e-5, None,
                    ALU.add), [S(b + 5)], [S(b + 5)])
                P.op("s", lambda e, b=b: e.activation(
                    stat[:, b + 6:b + 7], stat[:, b + 5:b + 6], AF.Sqrt),
                    [S(b + 5)], [S(b + 6)])
                P.op("v", lambda e, b=b: e.reciprocal(
                    stat[:, b + 7:b + 8], stat[:, b + 6:b + 7]),
                    [S(b + 6)], [S(b + 7)])
                P.op("v", lambda e, b=b: e.scalar_tensor_tensor(
                    stat[:, b + 8:b + 9], stat[:, b + 2:b + 3], -1.0,
                    stat[:, b + 7:b + 8], ALU.mult, ALU.mult),
                    [S(b + 2), S(b + 7)], [S(b + 8)])
                P.op("s", lambda e, xb=xb, b=b, xnbp=xnbp: e.activation(
                    xnbp[:], xb[:], AF.Identity, bias=stat[:, b + 8:b + 9],
                    scale=stat[:, b + 7:b + 8]),
                    [kxt, S(b + 7), S(b + 8)], [(tag, "xnb", par)])
                if xnf_d is not None:
                    P.op("s", lambda e, xb=xb, b=b, xnfp=xnfp: e.activation(
                        xnfp[:], xb[:], AF.Identity, bias=stat[:, b + 8:b + 9],
                        scale=stat[:, b + 7:b + 8]),
                        [kxt, S(b + 7), S(b + 8)], [(tag, "xnf", par)])
                    P.dma("so", xnf_d[j * 128:(j + 1) * 128, :], xnfp[:],
                          reads=[(tag, "xnf", par)], writes=[("xnf_d", j)])
                for dc in range(ND):
                    pp = ptr[dc % 2]
                    kp = (tag, "pt", dc % 2)
                    P.op("t", lambda e, pp=pp, dc=dc, xnbp=xnbp: e.transpose(
                        pp[:], xnbp[:, dc * 128:(dc + 1) * 128], ident_bf[:]),
                        [(tag, "xnb", par), KID], [kp])
                    P.op("v", lambda e, pp=pp, dc=dc, j=j: e.tensor_copy(
                        xnT[:, dc * T + j * 128: dc * T + (j + 1) * 128], pp[:]),
                        [kp], [("xnT" + tag, dc, j // 4)])
            P.flush()

    def ph_qk_xpos(w_d, cos_d, sin_d, xnT, tag, out_kT=None, out_q_d=None):
        t0 = tag[0]
        w_sb = AT([128, ND * D], BF16, 128 * KB)
        cs = [AT([128, 512], F32, 144 * KB + i * 2 * KB) for i in range(2)]
        sn = [AT([128, 512], F32, 148 * KB + i * 2 * KB) for i in range(2)]
        tm1 = AT([128, 512], F32, 152 * KB)
        tm2 = AT([128, 512], F32, 154 * KB)
        qst = AT([128, 2 * T], BF16, 156 * KB)
        P.dma("gp", w_sb[:].rearrange("p (dc o) -> p dc o", dc=ND),
              w_d.rearrange("(dc p) o -> p dc o", p=128), writes=[(tag, "w")])
        with (nc.psum_tensor([128, 512], F32) as pA0,
              nc.psum_tensor([128, 512], F32) as pA1,
              nc.psum_tensor([128, 512], F32) as pB0,
              nc.psum_tensor([128, 512], F32) as pB1):
            for pr in range(4):
                for c in range(NKC):
                    ib = c % 2
                    pa, pb = (pA0, pB0) if ib == 0 else (pA1, pB1)
                    ka, kb = (tag, "pa", ib), (tag, "pb", ib)
                    kc, ks = (tag, "cs", ib), (tag, "sn", ib)
                    P.dma("ld", cs[ib][:],
                          cos_d[(pr * 128):(pr + 1) * 128,
                                c * 512:(c + 1) * 512], writes=[kc])
                    P.dma("ld", sn[ib][:],
                          sin_d[(pr * 128):(pr + 1) * 128,
                                c * 512:(c + 1) * 512], writes=[ks])
                    for jo, ps, kk in ((pr, pa, ka), (pr + 4, pb, kb)):
                        for dc in range(ND):
                            P.op("t", lambda e, ps=ps, jo=jo, dc=dc, c=c:
                                 e.matmul(ps[:],
                                          w_sb[:, dc * D + jo * 128:
                                               dc * D + (jo + 1) * 128],
                                          xnT[:, dc * T + c * 512:
                                              dc * T + (c + 1) * 512],
                                          start=(dc == 0), stop=(dc == ND - 1)),
                                 [(tag, "w"), ("xnT" + t0, dc, c)], [kk])
                    P.op("v", lambda e, pa=pa, ib=ib: e.tensor_tensor(
                        tm1[:], pa[:], cs[ib][:], ALU.mult),
                        [ka, kc], [(tag, "tm1")])
                    P.op("v", lambda e, pb=pb, ib=ib: e.tensor_tensor(
                        tm2[:], pb[:], sn[ib][:], ALU.mult),
                        [kb, ks], [(tag, "tm2")])
                    if out_kT is not None:
                        P.op("v", lambda e, pr=pr, c=c: e.tensor_tensor(
                            out_kT[:, pr * T + c * 512: pr * T + (c + 1) * 512],
                            tm1[:], tm2[:], ALU.subtract),
                            [(tag, "tm1"), (tag, "tm2")], [("kT" + t0, pr, c)])
                    else:
                        P.op("v", lambda e, c=c: e.tensor_tensor(
                            qst[:, c * 512:(c + 1) * 512],
                            tm1[:], tm2[:], ALU.subtract),
                            [(tag, "tm1"), (tag, "tm2")], [(tag, "qst", 0, c)])
                    P.op("v", lambda e, pb=pb, ib=ib: e.tensor_tensor(
                        tm1[:], pb[:], cs[ib][:], ALU.mult),
                        [kb, kc], [(tag, "tm1")])
                    P.op("v", lambda e, pa=pa, ib=ib: e.tensor_tensor(
                        tm2[:], pa[:], sn[ib][:], ALU.mult),
                        [ka, ks], [(tag, "tm2")])
                    if out_kT is not None:
                        P.op("v", lambda e, pr=pr, c=c: e.tensor_tensor(
                            out_kT[:, (pr + 4) * T + c * 512:
                                   (pr + 4) * T + (c + 1) * 512],
                            tm1[:], tm2[:], ALU.add),
                            [(tag, "tm1"), (tag, "tm2")],
                            [("kT" + t0, pr + 4, c)])
                    else:
                        P.op("v", lambda e, c=c: e.tensor_tensor(
                            qst[:, T + c * 512: T + (c + 1) * 512],
                            tm1[:], tm2[:], ALU.add),
                            [(tag, "tm1"), (tag, "tm2")], [(tag, "qst", 1, c)])
                if out_q_d is not None:
                    for hh, jo in ((0, pr), (1, pr + 4)):
                        P.dma("so", out_q_d[jo][:, :],
                              qst[:, hh * T:(hh + 1) * T],
                              reads=[(tag, "qst", hh, c) for c in range(NKC)],
                              writes=[("qT_d" + t0, jo)])
            P.flush()

    def ph_spill_xnT(xnT, spill_d, tag):
        for dc in range(ND):
            P.dma("so", spill_d[dc][:, :], xnT[:, dc * T:(dc + 1) * T],
                  reads=[("xnT" + tag, dc, c) for c in range(NKC)],
                  writes=[("xnTd" + tag, dc)])

    def ph_v(wv_d, spill_d, v_sb, tag, xoff, wq_d=None, qtm_d=None):
        wv_sb = AT([128, ND * D], BF16, 128 * KB)
        xc = [AT([128, ND * 128], BF16, xoff + i * 2 * KB) for i in range(2)]
        P.dma("gp", wv_sb[:].rearrange("p (dc o) -> p dc o", dc=ND),
              wv_d.rearrange("(dc p) o -> p dc o", p=128), writes=[(tag, "wv")])
        if wq_d is not None:
            wq_sb = AT([128, ND * D], BF16, 144 * KB)
            qtt = [AT([128, D], BF16, 164 * KB + i * 2 * KB) for i in range(2)]
            P.dma("gp", wq_sb[:].rearrange("p (dc o) -> p dc o", dc=ND),
                  wq_d.rearrange("(dc p) o -> p dc o", p=128),
                  writes=[(tag, "wq")])
        with (nc.psum_tensor([128, 512], F32) as p0,
              nc.psum_tensor([128, 512], F32) as p1,
              nc.psum_tensor([128, 512], F32) as p2,
              nc.psum_tensor([128, 512], F32) as p3):
            for j in range(NT):
                xb = xc[j % 2]
                kx = (tag, "vxc", j % 2)
                P.dma("ld", xb[:].rearrange("p (dc k) -> p dc k", dc=ND),
                      spill_d[:, :, j * 128:(j + 1) * 128]
                      .rearrange("dc p k -> p dc k"),
                      reads=[("xnTd" + tag, dc) for dc in range(ND)],
                      writes=[kx])
                for n in range(2):
                    ps = (p0, p1)[n]
                    kk = (tag, "vps", n)
                    for dc in range(ND):
                        P.op("t", lambda e, ps=ps, xb=xb, dc=dc, n=n: e.matmul(
                            ps[:], xb[:, dc * 128:(dc + 1) * 128],
                            wv_sb[:, dc * D + n * 512: dc * D + (n + 1) * 512],
                            start=(dc == 0), stop=(dc == ND - 1)),
                            [(tag, "wv"), kx], [kk])
                    P.op("v", lambda e, ps=ps, j=j, n=n: e.tensor_copy(
                        v_sb[:, j * D + n * 512: j * D + (n + 1) * 512], ps[:]),
                        [kk], [("v" + tag, j)])
                if wq_d is not None:
                    qb2 = qtt[j % 2]
                    kq2 = (tag, "qtt", j % 2)
                    for n in range(2):
                        ps = (p2, p3)[n]
                        kk = (tag, "qps", n)
                        for dc in range(ND):
                            P.op("t", lambda e, ps=ps, xb=xb, dc=dc, n=n:
                                 e.matmul(ps[:],
                                          xb[:, dc * 128:(dc + 1) * 128],
                                          wq_sb[:, dc * D + n * 512:
                                                dc * D + (n + 1) * 512],
                                          start=(dc == 0), stop=(dc == ND - 1)),
                                 [(tag, "wq"), kx], [kk])
                        P.op("v", lambda e, ps=ps, qb2=qb2, n=n: e.tensor_copy(
                            qb2[:, n * 512:(n + 1) * 512], ps[:]),
                            [kk], [kq2 + (n,)])
                    P.dma("so", qtm_d[j * 128:(j + 1) * 128, :], qb2[:],
                          reads=[kq2 + (0,), kq2 + (1,)],
                          writes=[("q1tm", j)])
            P.flush()

    def ph_plain_kq(wk_d, wq_d, xnT, kT, q_d, tag):
        w_sb = AT([128, ND * D], BF16, 128 * KB)
        qst = AT([128, T], BF16, 152 * KB)
        for w_d, dest in ((wk_d, "k"),):
            P.dma("gp", w_sb[:].rearrange("p (dc o) -> p dc o", dc=ND),
                  w_d.rearrange("(dc p) o -> p dc o", p=128),
                  writes=[(tag, "w")])
            with (nc.psum_tensor([128, 512], F32) as p0,
                  nc.psum_tensor([128, 512], F32) as p1):
                for jo in range(ND):
                    for c in range(NKC):
                        ps = (p0, p1)[c % 2]
                        kk = (tag, "ps", c % 2)
                        for dc in range(ND):
                            P.op("t", lambda e, ps=ps, jo=jo, dc=dc, c=c:
                                 e.matmul(ps[:],
                                          w_sb[:, dc * D + jo * 128:
                                               dc * D + (jo + 1) * 128],
                                          xnT[:, dc * T + c * 512:
                                              dc * T + (c + 1) * 512],
                                          start=(dc == 0), stop=(dc == ND - 1)),
                                 [(tag, "w"), ("xnT" + tag, dc, c)], [kk])
                        if dest == "k":
                            P.op("v", lambda e, ps=ps, jo=jo, c=c: e.tensor_copy(
                                kT[:, jo * T + c * 512: jo * T + (c + 1) * 512],
                                ps[:]), [kk], [("kT" + tag, jo, c)])
                        else:
                            P.op("v", lambda e, ps=ps, c=c: e.tensor_copy(
                                qst[:, c * 512:(c + 1) * 512], ps[:]),
                                [kk], [(tag, "qst", c)])
                    if dest == "q":
                        P.dma("so", q_d[jo][:, :], qst[:],
                              reads=[(tag, "qst", c) for c in range(NKC)],
                              writes=[("qT_d" + tag, jo)])
                P.flush()

    def ph_attn(kT, v_sb, q_d, out_d, tag, causal, kept_fn):
        Ssb = AT([128, T], F32, 128 * KB)
        Ebf = [AT([128, T], BF16, 144 * KB), AT([128, T], BF16, 172 * KB)]
        dd_s = AT([128, 2048], F32, 152 * KB)
        qt = [AT([128, ND * 128], BF16, 160 * KB + i * 2 * KB) for i in range(2)]
        abf = [AT([128, 512], BF16, 164 * KB + i * KB) for i in range(2)]
        ats = [AT([128, 128], BF16, 166 * KB + i * KB) for i in range(2)]
        att = [AT([128, D], F32, 168 * KB), AT([128, D], F32, 180 * KB)]
        if not causal:
            P.dma("ld", dd_s[:], ddiag_d, writes=[("ddiag",)])
        with (nc.psum_tensor([128, 512], F32) as ps0,
              nc.psum_tensor([128, 512], F32) as ps1,
              nc.psum_tensor([128, 512], F32) as pav0,
              nc.psum_tensor([128, 512], F32) as pav1,
              nc.psum_tensor([128, 128], BF16) as ptA,
              nc.psum_tensor([128, 128], BF16) as ptB):
            for j in range(NT):
                par = j % 2
                b = 32 + 16 * par
                Eb = Ebf[par]
                kE = (tag, "E", par)
                atb = att[par]
                ext = (j + 1) * 128 if causal else T
                nch = (ext + 511) // 512
                qb = qt[par]
                kq = (tag, "qt", par)
                P.dma("ld", qb[:].rearrange("p (dc k) -> p dc k", dc=ND),
                      q_d[:, :, j * 128:(j + 1) * 128]
                      .rearrange("dc p k -> p dc k"),
                      reads=[("qT_d" + tag, dc) for dc in range(ND)],
                      writes=[kq])
                srd = []
                for c in range(nch):
                    ncols = min(512, ext - c * 512)
                    ps = (ps0, ps1)[c % 2]
                    kk = (tag, "ps", c % 2)
                    for dc in range(ND):
                        P.op("t", lambda e, ps=ps, qb=qb, dc=dc, c=c, ncols=ncols:
                             e.matmul(ps[:, 0:ncols],
                                      qb[:, dc * 128:(dc + 1) * 128],
                                      kT[:, dc * T + c * 512:
                                         dc * T + c * 512 + ncols],
                                      start=(dc == 0), stop=(dc == ND - 1)),
                             [kq, ("kT" + tag, dc, c)], [kk])
                    evsc = (1.0 / 32.0) if causal else 1.0
                    if causal and c == nch - 1:
                        if ncols > 128:
                            P.op("v", lambda e, ps=ps, c=c, ncols=ncols:
                                 e.tensor_scalar(
                                     Ssb[:, c * 512: c * 512 + ncols - 128],
                                     ps[:, 0:ncols - 128], evsc, None,
                                     ALU.mult),
                                 [kk], [(tag, "S", c)])
                            srd.append((tag, "S", c))
                        P.op("v", lambda e, ps=ps, c=c, ncols=ncols:
                             e.scalar_tensor_tensor(
                                 Ssb[:, c * 512 + ncols - 128: c * 512 + ncols],
                                 ps[:, ncols - 128:ncols], evsc, cmask_s[:],
                                 ALU.mult, ALU.add),
                             [kk, ("const", "cmask_s")], [(tag, "Sm", c)])
                        srd.append((tag, "Sm", c))
                    elif causal:
                        P.op("v", lambda e, ps=ps, c=c, ncols=ncols:
                             e.tensor_scalar(Ssb[:, c * 512:c * 512 + ncols],
                                             ps[:, 0:ncols], evsc, None,
                                             ALU.mult),
                             [kk], [(tag, "S", c)])
                        srd.append((tag, "S", c))
                    else:
                        P.op("v", lambda e, ps=ps, c=c, ncols=ncols:
                             e.tensor_copy(Ssb[:, c * 512:c * 512 + ncols],
                                           ps[:, 0:ncols]),
                             [kk], [(tag, "S", c)])
                        srd.append((tag, "S", c))
                P.op("v", lambda e, ext=ext, b=b: e.tensor_reduce(
                    stat[:, b + 9 - 9:b + 10 - 9], Ssb[:, 0:ext], AX.X,
                    ALU.max), srd, [S(b + 0)])
                P.op("v", lambda e, b=b: e.tensor_scalar(
                    stat[:, b + 1:b + 2], stat[:, b + 0:b + 1], -1.0, None,
                    ALU.mult), [S(b + 0)], [S(b + 1)])
                P.op("s", lambda e, ext=ext, b=b, Eb=Eb: e.activation(
                    Eb[:, 0:ext], Ssb[:, 0:ext], AF.Exp,
                    bias=stat[:, b + 1:b + 2],
                    accum_out=stat[:, b + 2:b + 3]),
                    srd + [S(b + 1)], [kE, S(b + 2)])
                P.op("v", lambda e, b=b: e.reciprocal(
                    stat[:, b + 3:b + 4], stat[:, b + 2:b + 3]),
                    [S(b + 2)], [S(b + 3)])
                kept = kept_fn(j)
                nsub_tot = sum((min(512, ext - c * 512) + 127) // 128
                               for (c, _, _) in kept)
                kav = [(tag, "pav", 0), (tag, "pav", 1)]
                sdone = 0
                for (c, typ, coef) in kept:
                    ncols = min(512, ext - c * 512)
                    ab = abf[c % 2]
                    kab = (tag, "abf", c % 2)
                    if causal:
                        P.op("v", lambda e, ab=ab, c=c, ncols=ncols, b=b, Eb=Eb:
                             e.tensor_scalar(ab[:, 0:ncols],
                                             Eb[:, c * 512:c * 512 + ncols],
                                             stat[:, b + 3:b + 4], None,
                                             ALU.mult),
                             [kE, S(b + 3)], [kab])
                    elif typ == "d":
                        P.op("v", lambda e, ab=ab, c=c, r=coef, b=b, Eb=Eb:
                             e.scalar_tensor_tensor(
                                 ab[:], Eb[:, c * 512:(c + 1) * 512],
                                 stat[:, b + 3:b + 4],
                                 dd_s[:, r * 512:(r + 1) * 512],
                                 ALU.mult, ALU.mult),
                             [kE, S(b + 3), ("ddiag",)], [kab])
                    else:
                        gv = gv_pos if typ == "l" else gv_neg
                        gb = gneg_s if typ == "l" else gpos_s
                        gbk = ("const", "gneg_s" if typ == "l" else "gpos_s")
                        P.op("v", lambda e, gv=gv, coef=coef, b=b:
                             e.tensor_scalar(
                                 stat[:, b + 4:b + 5], gv[:],
                                 stat[:, b + 3:b + 4], coef,
                                 ALU.mult, ALU.mult), [S(b + 3)], [S(b + 4)])
                        P.op("v", lambda e, ab=ab, c=c, gb=gb, b=b, Eb=Eb:
                             e.scalar_tensor_tensor(
                                 ab[:], Eb[:, c * 512:(c + 1) * 512],
                                 stat[:, b + 4:b + 5], gb[:], ALU.mult,
                                 ALU.mult),
                             [kE, S(b + 4), gbk], [kab])
                    nsub = (ncols + 127) // 128
                    for sub in range(nsub):
                        s_k = c * 4 + sub
                        pt = (ptA, ptB)[s_k % 2]
                        kpt = (tag, "pt", s_k % 2)
                        at = ats[s_k % 2]
                        kat = (tag, "at", s_k % 2)
                        P.op("t", lambda e, pt=pt, ab=ab, sub=sub: e.transpose(
                            pt[:], ab[:, sub * 128:(sub + 1) * 128],
                            ident_bf[:]), [kab, KID], [kpt])
                        P.op("s", lambda e, pt=pt, at=at: e.copy(
                            at[:], pt[:]), [kpt], [kat])
                        sdone += 1
                        last = (sdone == nsub_tot)
                        for n in range(2):
                            P.op("t", lambda e, at=at, s_k=s_k, n=n,
                                 first=(sdone == 1), last=last,
                                 pv=(pav0, pav1)[n]:
                                 e.matmul(pv[:], at[:],
                                          v_sb[:, s_k * D + n * 512:
                                               s_k * D + (n + 1) * 512],
                                          start=first, stop=last),
                                 [kat, ("v" + tag, s_k)], [kav[n]])
                P.op("v", lambda e, atb=atb: e.tensor_copy(
                    atb[:, 0:512], pav0[:]), [kav[0]], [(tag, "att", par, 0)])
                P.op("v", lambda e, atb=atb: e.tensor_copy(
                    atb[:, 512:1024], pav1[:]), [kav[1]], [(tag, "att", par, 1)])
                P.dma("so", out_d[j * 128:(j + 1) * 128, :], atb[:],
                      reads=[(tag, "att", par, 0), (tag, "att", par, 1)],
                      writes=[("attn_d" + tag, j)])
            P.flush()


    def dyn_load(e, dst, src_d, rg):
        return e.indirect_dma_start(
            dst, None, src_d,
            bass.IndirectOffsetOnAxis(ap=idx_sb[:, rg:rg + 1], axis=0))

    def ph_attn_own(kT, v_sb, qtm_d, out_d, tag):
        Ssb = AT([128, T], F32, 128 * KB)
        Ebf = [AT([128, T], BF16, 144 * KB), AT([128, T], BF16, 172 * KB)]
        qrow = [AT([128, D], BF16, 152 * KB + i * 2 * KB) for i in range(2)]
        qt = [AT([128, ND * 128], BF16, 160 * KB + i * 2 * KB) for i in range(2)]
        abf = [AT([128, 512], BF16, 164 * KB + i * KB) for i in range(2)]
        ats = [AT([128, 128], BF16, 166 * KB + i * KB) for i in range(2)]
        att = [AT([128, D], F32, 168 * KB), AT([128, D], F32, 180 * KB)]
        with (nc.psum_tensor([128, 512], F32) as ps0,
              nc.psum_tensor([128, 512], F32) as ps1,
              nc.psum_tensor([128, 512], F32) as pav0,
              nc.psum_tensor([128, 512], F32) as pav1,
              nc.psum_tensor([128, 128], BF16) as ptA,
              nc.psum_tensor([128, 128], BF16) as ptB):
            for rg in range(NHT):
                par = rg % 2
                b = 32 + 16 * par
                Eb = Ebf[par]
                kE = (tag, "E", par)
                atb = att[par]
                ext = (2 * rg + 2) * 128
                nch = (ext + 511) // 512
                qr = qrow[par]
                kqr = (tag, "qr", par)
                qb = qt[par]
                kq = (tag, "qt", par)
                P.op("gp", lambda e, qr=qr, rg=rg: dyn_load(e, qr[:], qtm_d, rg),
                     reads=[("idx",)] + [("q1tm", jj) for jj in range(NT)],
                     writes=[kqr])
                for dc in range(ND):
                    pt = (ptA, ptB)[dc % 2]
                    kpt = (tag, "pt", dc % 2)
                    P.op("t", lambda e, pt=pt, qr=qr, dc=dc: e.transpose(
                        pt[:], qr[:, dc * 128:(dc + 1) * 128], ident_bf[:]),
                        [kqr, KID], [kpt])
                    P.op("v", lambda e, pt=pt, qb=qb, dc=dc: e.tensor_copy(
                        qb[:, dc * 128:(dc + 1) * 128], pt[:]), [kpt], [kq])
                srd = []
                evsc = 1.0 / 32.0
                for c in range(nch):
                    ncols = min(512, ext - c * 512)
                    ps = (ps0, ps1)[c % 2]
                    kk = (tag, "ps", c % 2)
                    for dc in range(ND):
                        P.op("t", lambda e, ps=ps, qb=qb, dc=dc, c=c,
                             ncols=ncols:
                             e.matmul(ps[:, 0:ncols],
                                      qb[:, dc * 128:(dc + 1) * 128],
                                      kT[:, dc * T + c * 512:
                                         dc * T + c * 512 + ncols],
                                      start=(dc == 0), stop=(dc == ND - 1)),
                             [kq, ("kT" + tag, dc, c)], [kk])
                    if c == nch - 1:
                        if ncols > 256:
                            P.op("v", lambda e, ps=ps, c=c, ncols=ncols:
                                 e.tensor_scalar(
                                     Ssb[:, c * 512: c * 512 + ncols - 256],
                                     ps[:, 0:ncols - 256], evsc, None,
                                     ALU.mult),
                                 [kk], [(tag, "S", c)])
                            srd.append((tag, "S", c))
                        P.op("v", lambda e, ps=ps, c=c, ncols=ncols:
                             e.scalar_tensor_tensor(
                                 Ssb[:, c * 512 + ncols - 256: c * 512 + ncols],
                                 ps[:, ncols - 256:ncols], evsc, cmask2_s[:],
                                 ALU.mult, ALU.add),
                             [kk, ("const", "cmask2_s")], [(tag, "Sm", c)])
                        srd.append((tag, "Sm", c))
                    else:
                        P.op("v", lambda e, ps=ps, c=c, ncols=ncols:
                             e.tensor_scalar(Ssb[:, c * 512:c * 512 + ncols],
                                             ps[:, 0:ncols], evsc, None,
                                             ALU.mult),
                             [kk], [(tag, "S", c)])
                        srd.append((tag, "S", c))
                P.op("v", lambda e, ext=ext, b=b: e.tensor_reduce(
                    stat[:, b + 0:b + 1], Ssb[:, 0:ext], AX.X,
                    ALU.max), srd, [S(b + 0)])
                P.op("v", lambda e, b=b: e.tensor_scalar(
                    stat[:, b + 1:b + 2], stat[:, b + 0:b + 1], -1.0, None,
                    ALU.mult), [S(b + 0)], [S(b + 1)])
                P.op("s", lambda e, ext=ext, b=b, Eb=Eb: e.activation(
                    Eb[:, 0:ext], Ssb[:, 0:ext], AF.Exp,
                    bias=stat[:, b + 1:b + 2],
                    accum_out=stat[:, b + 2:b + 3]),
                    srd + [S(b + 1)], [kE, S(b + 2)])
                P.op("v", lambda e, b=b: e.reciprocal(
                    stat[:, b + 3:b + 4], stat[:, b + 2:b + 3]),
                    [S(b + 2)], [S(b + 3)])
                nsub_tot = ext // 128
                kav = [(tag, "pav", 0), (tag, "pav", 1)]
                sdone = 0
                for c in range(nch):
                    ncols = min(512, ext - c * 512)
                    ab = abf[c % 2]
                    kab = (tag, "abf", c % 2)
                    P.op("v", lambda e, ab=ab, c=c, ncols=ncols, b=b, Eb=Eb:
                         e.tensor_scalar(ab[:, 0:ncols],
                                         Eb[:, c * 512:c * 512 + ncols],
                                         stat[:, b + 3:b + 4], None,
                                         ALU.mult),
                         [kE, S(b + 3)], [kab])
                    for sub in range(ncols // 128):
                        s_k = c * 4 + sub
                        pt = (ptA, ptB)[s_k % 2]
                        kpt = (tag, "pt", s_k % 2)
                        at = ats[s_k % 2]
                        kat = (tag, "at", s_k % 2)
                        P.op("t", lambda e, pt=pt, ab=ab, sub=sub: e.transpose(
                            pt[:], ab[:, sub * 128:(sub + 1) * 128],
                            ident_bf[:]), [kab, KID], [kpt])
                        P.op("s", lambda e, pt=pt, at=at: e.copy(
                            at[:], pt[:]), [kpt], [kat])
                        sdone += 1
                        last = (sdone == nsub_tot)
                        for n in range(2):
                            P.op("t", lambda e, at=at, s_k=s_k, n=n,
                                 first=(sdone == 1), last=last,
                                 pv=(pav0, pav1)[n]:
                                 e.matmul(pv[:], at[:],
                                          v_sb[:, s_k * D + n * 512:
                                               s_k * D + (n + 1) * 512],
                                          start=first, stop=last),
                                 [kat, ("v" + tag, s_k)], [kav[n]])
                P.op("v", lambda e, atb=atb: e.tensor_copy(
                    atb[:, 0:512], pav0[:]), [kav[0]], [(tag, "att", par, 0)])
                P.op("v", lambda e, atb=atb: e.tensor_copy(
                    atb[:, 512:1024], pav1[:]), [kav[1]], [(tag, "att", par, 1)])
                P.dma("so", out_d[rg * 128:(rg + 1) * 128, :], atb[:],
                      reads=[(tag, "att", par, 0), (tag, "att", par, 1)],
                      writes=[("attn_d" + tag, rg)])
            P.flush()

    xnT_A = AT([128, ND * T], BF16, 0)
    kT_A = AT([128, ND * T], BF16, 64 * KB)

    wbg = nc.alloc_semaphore("wbg")
    for e_ in range(8):
        P.op("gp", lambda e, e_=e_: e.dma_start(out=w1b_d[e_], in_=W1_d[e_]),
             bgsem=wbg)
        P.op("gp", lambda e, e_=e_: e.dma_start(out=w2b_d[e_], in_=W2_d[e_]),
             bgsem=wbg)
    ph_ln(x0, xnT_A, "0")
    ph_qk_xpos(wk0d, cosk, sink, xnT_A, "0k", out_kT=kT_A)
    ph_qk_xpos(wq0d, cosq, sinq, xnT_A, "0q", out_q_d=qT0_d)

    gw_sb = AT([128, ND * 8], BF16, 128 * KB)
    Lacc = AT([8, T], F32, 130 * KB)
    gb_sb = AT([8, 1], F32, 146 * KB)
    ls8 = AT([8, 2], F32, 147 * KB)
    P.dma("gp", gw_sb[:].rearrange("p (dc e) -> p dc e", dc=ND),
          gwT_d.rearrange("(dc p) e -> p dc e", p=128), writes=[("gw",)])
    P.dma("ld", gb_sb[:], gbT_d, writes=[("gb",)])
    with nc.psum_tensor([8, 512], F32) as pg:
        for c in range(NKC):
            for dc in range(ND):
                P.op("t", lambda e, dc=dc, c=c: e.matmul(
                    pg[:], gw_sb[:, dc * 8:(dc + 1) * 8],
                    xnT_A[:, dc * T + c * 512: dc * T + (c + 1) * 512],
                    start=(dc == 0), stop=(dc == ND - 1)),
                    [("gw",), ("xnT0", dc, c)], [("pg",)])
            P.op("v", lambda e, c=c: e.tensor_copy(
                Lacc[:, c * 512:(c + 1) * 512], pg[:]),
                [("pg",)], [("Lacc", c)])
        P.op("v", lambda e: e.tensor_reduce(
            ls8[:, 0:1], Lacc[:], AX.X, ALU.add),
            [("Lacc", c) for c in range(NKC)], [("Ls",)])
        P.op("v", lambda e: e.scalar_tensor_tensor(
            ls8[:, 1:2], ls8[:, 0:1], 1.0 / T, gb_sb[:], ALU.mult, ALU.add),
            [("Ls",), ("gb",)], [("Lg",)])
        P.dma("gp", row8_d, ls8[:, 1:2], reads=[("Lg",)], writes=[("r8d",)])
        lrow = st8[0:1, 0:8]
        P.dma("gp", lrow, row8_d.rearrange("p one -> one p"),
              reads=[("r8d",)], writes=[("lrow",)])
        r8 = st8[0:1, 8:16]
        g8 = st8[0:1, 16:24]
        e8 = st8[0:1, 24:32]
        m1 = st8[0:1, 32:33]; nm1 = st8[0:1, 33:34]; m2 = st8[0:1, 34:35]
        zz = st8[0:1, 35:36]; rz8 = st8[0:1, 36:37]
        inv8 = st8[0:1, 37:45]
        keep2 = st8[0:1, 45:53]
        tmp8 = st8[0:1, 53:61]
        sc4 = st8[0:1, 61:65]
        P.op("v", lambda e: e.tensor_reduce(m1, lrow, AX.X, ALU.max),
             [("lrow",)], [("m1",)])
        P.op("v", lambda e: e.tensor_scalar(inv8, lrow, m1, None, ALU.is_ge),
             [("lrow",), ("m1",)], [("inv",)])
        P.op("v", lambda e: e.scalar_tensor_tensor(
            r8, inv8, NEG, lrow, ALU.mult, ALU.add),
            [("inv",), ("lrow",)], [("msk",)])
        P.op("v", lambda e: e.tensor_reduce(m2, r8, AX.X, ALU.max),
             [("msk",)], [("m2",)])
        P.op("v", lambda e: e.tensor_scalar(g8, lrow, m2, None, ALU.is_ge),
             [("lrow",), ("m2",)], [("keep",)])
        P.op("v", lambda e: e.tensor_scalar(nm1, m1, -1.0, None, ALU.mult),
             [("m1",)], [("nm1",)])
        P.op("s", lambda e: e.activation(e8, lrow, AF.Exp, bias=nm1),
             [("lrow",), ("nm1",)], [("e8",)])
        P.op("v", lambda e: e.tensor_tensor(e8, e8, g8, ALU.mult),
             [("e8",), ("keep",)], [("e8",)])
        P.op("v", lambda e: e.tensor_reduce(zz, e8, AX.X, ALU.add),
             [("e8",)], [("zz",)])
        P.op("v", lambda e: e.reciprocal(rz8, zz), [("zz",)], [("rz8",)])
        P.op("v", lambda e: e.tensor_scalar(e8, e8, rz8, None, ALU.mult),
             [("e8",), ("rz8",)], [("g8f",)])
        P.op("v", lambda e: e.tensor_tensor(keep2, g8, inv8, ALU.subtract),
             [("keep",), ("inv",)], [("keep2",)])
        P.op("v", lambda e: e.tensor_tensor(tmp8, eids_s[:], inv8, ALU.mult),
             [("eids",), ("inv",)], [("t8",)])
        P.op("v", lambda e: e.tensor_reduce(sc4[0:1, 0:1], tmp8, AX.X, ALU.add),
             [("t8",)], [("sea",)])
        P.op("v", lambda e: e.tensor_tensor(tmp8, eids_s[:], keep2, ALU.mult),
             [("eids",), ("keep2",)], [("t8",)])
        P.op("v", lambda e: e.tensor_reduce(sc4[0:1, 1:2], tmp8, AX.X, ALU.add),
             [("t8",)], [("seb",)])
        P.op("v", lambda e: e.tensor_tensor(tmp8, e8, inv8, ALU.mult),
             [("g8f",), ("inv",)], [("t8",)])
        P.op("v", lambda e: e.tensor_reduce(sc4[0:1, 2:3], tmp8, AX.X, ALU.add),
             [("t8",)], [("sga",)])
        P.op("v", lambda e: e.tensor_tensor(tmp8, e8, keep2, ALU.mult),
             [("g8f",), ("keep2",)], [("t8",)])
        P.op("v", lambda e: e.tensor_reduce(sc4[0:1, 3:4], tmp8, AX.X, ALU.add),
             [("t8",)], [("sgb",)])
        P.dma("gp", se2_d, sc4[0:1, 0:2], reads=[("sea",), ("seb",)],
              writes=[("se2d",)])
        P.dma("gp", g2_d, sc4[0:1, 2:4], reads=[("sga",), ("sgb",)],
              writes=[("g2d",)])
        P.dma("gp", sef_bc[:], bass.AP(se2_d.tensor, 0, [[0, 128], [1, 2]]),
              reads=[("se2d",)], writes=[("sefbc",)])
        P.dma("gp", gbc2[:], bass.AP(g2_d.tensor, 0, [[0, 128], [1, 2]]),
              reads=[("g2d",)], writes=[("gbc2",)])
        P.dma("gp", se21[:], se2_d.rearrange("one two -> two one"),
              reads=[("se2d",)], writes=[("se21",)])
        P.op("v", lambda e: e.tensor_copy(idx1e[:], se21[:]),
             [("se21",)], [("idx1",)])
        for sel in range(2):
            P.op("v", lambda e, sel=sel: e.tensor_scalar(
                stat[:, 14:15], sef_bc[:, sel:sel + 1], 1024.0, None, ALU.mult),
                [("sefbc",)], [S(14)])
            for dc in range(ND):
                P.op("v", lambda e, dc=dc: e.scalar_tensor_tensor(
                    stat[:, 15:16], iop_s[:], float(dc * 128), stat[:, 14:15],
                    ALU.add, ALU.add), [("iop",), S(14)], [S(15)])
                P.op("v", lambda e, sel=sel, dc=dc: e.tensor_copy(
                    idxg[:, sel * 8 + dc: sel * 8 + dc + 1], stat[:, 15:16]),
                    [S(15)], [("idxg", sel, dc)])
        P.flush()

    ph_spill_xnT(xnT_A, xnT0_d, "0")
    v_A = AT([128, NT * D], BF16, 0)
    ph_v(wv0, xnT0_d, v_A, "0", 152 * KB)

    def kept0(j):
        q0 = 128 * j
        out = []
        for c in range(NKC):
            k_lo, k_hi = 512 * c, 512 * c + 511
            if k_lo <= q0 + 127 and k_hi >= q0:
                mind = 0
            else:
                mind = min(abs(q0 - k_hi), abs(k_lo - (q0 + 127)))
            if mind >= WIN:
                continue
            if k_hi < q0:
                out.append((c, "l", float(np.float32(GAMMA ** float(q0 - k_lo)))))
            elif k_lo > q0 + 127:
                out.append((c, "r", float(np.float32(GAMMA ** float(k_lo - q0)))))
            else:
                out.append((c, "d", (q0 % 512) // 128))
        return out

    ph_attn(kT_A, v_A, qT0_d, attn0_d, "0", causal=False, kept_fn=kept0)

    ewf_sb = AT([128, ND * 1024], F32, 0)
    ew_sb = AT([128, 2 * ND * 1024], BF16, 32 * KB)
    ebf_sb = AT([2, 1024], F32, 128 * KB)
    ebr_sb = AT([1, 2 * 1024], F32, 132 * KB)
    eb_sb = AT([1, 2 * 1024], BF16, 140 * KB)
    xcD = [AT([128, ND * 128], BF16, 144 * KB + i * 2 * KB) for i in range(2)]
    accD = AT([128, D], F32, 148 * KB)
    gtD = AT([128, D], F32, 152 * KB)
    x0t = [AT([128, D], F32, 156 * KB + i * 4 * KB) for i in range(2)]
    a0t = [AT([128, D], F32, 164 * KB + i * 4 * KB) for i in range(2)]
    for sel in range(2):
        for dc in range(ND):
            P.op("gp", lambda e, sel=sel, dc=dc: e.indirect_dma_start(
                ewf_sb[:, dc * 1024:(dc + 1) * 1024], None, ew_g,
                bass.IndirectOffsetOnAxis(
                    ap=idxg[:, sel * 8 + dc: sel * 8 + dc + 1], axis=0)),
                reads=[("idxg", sel, dc)], writes=[("ewf", dc)])
        P.op("v", lambda e, sel=sel: e.tensor_copy(
            ew_sb[:, sel * ND * 1024:(sel + 1) * ND * 1024], ewf_sb[:]),
            [("ewf", dc) for dc in range(ND)], [("ew", sel)])
    P.op("gp", lambda e: e.indirect_dma_start(
        ebf_sb[:], None, eb0,
        bass.IndirectOffsetOnAxis(ap=idx1e[:], axis=0)),
        reads=[("idx1",)], writes=[("ebf", 0)])
    P.dma("gp", eb2_d, ebf_sb[:], reads=[("ebf", 0)], writes=[("eb2d",)])
    P.dma("ld", ebr_sb[:],
          bass.AP(eb2_d.tensor, 0, [[2048, 1], [1, 2048]]),
          reads=[("eb2d",)], writes=[("ebr",)])
    P.op("v", lambda e: e.tensor_copy(eb_sb[:], ebr_sb[:]),
         [("ebr",)], [("eb",)])
    with (nc.psum_tensor([128, 1024], F32) as pe0,
          nc.psum_tensor([128, 1024], F32) as pe1):
        for j in range(NT):
            xb = xcD[j % 2]
            kx = ("Dxc", j % 2)
            P.dma("ld", xb[:].rearrange("p (dc k) -> p dc k", dc=ND),
                  xnT0_d[:, :, j * 128:(j + 1) * 128]
                  .rearrange("dc p k -> p dc k"),
                  reads=[("xnTd0", dc) for dc in range(ND)], writes=[kx])
            P.dma("ld", x0t[j % 2][:], x0[j * 128:(j + 1) * 128, :],
                  writes=[("Dx0", j % 2)])
            P.dma("ld", a0t[j % 2][:], attn0_d[j * 128:(j + 1) * 128, :],
                  reads=[("attn_d0", j)], writes=[("Da0", j % 2)])
            for sel in range(2):
                pe = (pe0, pe1)[sel]
                kp = ("Dpe", sel)
                for n in range(2):
                    P.op("t", lambda e, pe=pe, sel=sel, n=n: e.matmul(
                        pe[:, n * 512:(n + 1) * 512], ones_bf[:],
                        eb_sb[0:1, sel * 1024 + n * 512:
                              sel * 1024 + (n + 1) * 512],
                        start=True, stop=False),
                        [("eb",), KONE], [kp])
                for dc in range(ND):
                    for n in range(2):
                        P.op("t", lambda e, pe=pe, xb=xb, sel=sel, dc=dc, n=n:
                             e.matmul(pe[:, n * 512:(n + 1) * 512],
                                      xb[:, dc * 128:(dc + 1) * 128],
                                      ew_sb[:, sel * ND * 1024 + dc * 1024 +
                                            n * 512:
                                            sel * ND * 1024 + dc * 1024 +
                                            (n + 1) * 512],
                                      start=False, stop=(dc == ND - 1)),
                             [("ew", sel), kx], [kp])
                P.op("s", lambda e, pe=pe: e.activation(gtD[:], pe[:], AF.Gelu),
                     [kp], [("Dgt",)])
                if sel == 0:
                    P.op("v", lambda e: e.tensor_scalar(
                        accD[:], gtD[:], gbc2[:, 0:1], None, ALU.mult),
                        [("Dgt",), ("gbc2",)], [("Dacc",)])
                else:
                    P.op("v", lambda e: e.scalar_tensor_tensor(
                        accD[:], gtD[:], gbc2[:, 1:2], accD[:],
                        ALU.mult, ALU.add),
                        [("Dgt",), ("gbc2",), ("Dacc",)], [("Dacc",)])
            P.op("v", lambda e, j=j: e.tensor_tensor(
                accD[:], accD[:], x0t[j % 2][:], ALU.add),
                [("Dacc",), ("Dx0", j % 2)], [("Dacc",)])
            P.op("v", lambda e, j=j: e.tensor_tensor(
                x0t[j % 2][:], accD[:], a0t[j % 2][:], ALU.add),
                [("Dacc",), ("Da0", j % 2)], [("Dx0", j % 2)])
            P.dma("so", x1_d[j * 128:(j + 1) * 128, :], x0t[j % 2][:],
                  reads=[("Dx0", j % 2)], writes=[("x1d", j)])
        P.flush()

    xnT_E = AT([128, ND * T], BF16, 0)
    kT_E = AT([128, ND * T], BF16, 64 * KB)
    ph_ln(x1_d, xnT_E, "1", src_key=lambda j: ("x1d", j), xnf_d=xn1f_d)
    ph_plain_kq(wk1, wq1, xnT_E, kT_E, None, "1")
    ph_spill_xnT(xnT_E, xn1T_d, "1")
    v_E = AT([128, NT * D], BF16, 0)
    ph_v(wv1, xn1T_d, v_E, "1", 160 * KB, wq_d=wq1, qtm_d=q1tm_d)

    ph_attn_own(kT_E, v_E, q1tm_d, attn1_d, "1")


    wTp = AT([128, NHT * 128], BF16, 0)
    gmb = AT([128, D], F32, 96 * KB)
    bmb = AT([128, D], F32, 100 * KB)
    x1o = [AT([128, D], F32, 104 * KB + i * 4 * KB) for i in range(2)]
    xn2n = [AT([128, D], F32, 112 * KB), AT([128, D], F32, 133 * KB)]
    xn2b = [AT([128, D], BF16, 116 * KB), AT([128, D], BF16, 137 * KB)]
    x2T = [AT([128, ND * 128], BF16, 118 * KB),
           AT([128, ND * 128], BF16, 139 * KB)]
    wr_sb = AT([128, ND * 8], BF16, 120 * KB)
    br_sb = AT([1, 8], BF16, 121 * KB)
    w_all = AT([128, NHT * 8], F32, 122 * KB)
    probs = [AT([128, 8], F32, 130 * KB), AT([128, 8], F32, 141 * KB)]
    g0scr = [AT([128, D], F32, 124 * KB), AT([128, D], F32, 142 * KB)]
    tb8 = [AT([128, 16], F32, 131 * KB), AT([128, 16], F32, 146 * KB)]
    P.dma("ld", gmb[:], gamma_d, writes=[("gmb",)])
    P.dma("ld", bmb[:], beta_d, writes=[("bmb",)])
    P.dma("gp", wr_sb[:].rearrange("p (dc e) -> p dc e", dc=ND),
          Wr_d.rearrange("(dc p) e -> p dc e", p=128), writes=[("wr",)])
    P.dma("gp", br_sb[:], br_d, writes=[("brs",)])
    P.op("v", lambda e: e.memset(wTp[:], 0.0), writes=[("wTpz",)])
    with (nc.psum_tensor([128, 128], BF16) as pt0,
          nc.psum_tensor([128, 128], BF16) as pt1,
          nc.psum_tensor([128, 8], F32) as pr_,
          nc.psum_tensor([128, 128], F32) as pw_):
        for r in range(NHT):
            par = r % 2
            b = 16 * par
            rb = 32 + 16 * par
            xb = x1o[par]
            xn2np, xn2bp, x2Tp = xn2n[par], xn2b[par], x2T[par]
            probp, g0scrp, tb8p = probs[par], g0scr[par], tb8[par]
            kx = ("G0x", par)
            P.op("gp", lambda e, xb=xb, r=r: dyn_load(e, xb[:], xn1f_d, r),
                reads=[("idx",)] + [("xnf_d", jj) for jj in range(NT)],
                writes=[kx])
            P.op("v", lambda e, xb=xb, b=b, xn2np=xn2np, g0scrp=g0scrp: e.tensor_reduce(
                stat[:, b + 0:b + 1], xb[:], AX.X, ALU.add), [kx], [S(b + 0)])
            P.op("s", lambda e, xb=xb, b=b, xn2np=xn2np, g0scrp=g0scrp: e.activation(
                g0scrp[:], xb[:], AF.Square, accum_out=stat[:, b + 1:b + 2]),
                [kx], [("G0scr", par), S(b + 1)])
            P.op("v", lambda e, b=b, rb=rb, xn2np=xn2np, xn2bp=xn2bp, x2Tp=x2Tp, probp=probp, g0scrp=g0scrp, tb8p=tb8p: e.tensor_scalar(
                stat[:, b + 2:b + 3], stat[:, b + 0:b + 1], 1.0 / D, None, ALU.mult),
                [S(b + 0)], [S(b + 2)])
            P.op("v", lambda e, b=b, rb=rb, xn2np=xn2np, xn2bp=xn2bp, x2Tp=x2Tp, probp=probp, g0scrp=g0scrp, tb8p=tb8p: e.tensor_tensor(
                stat[:, b + 3:b + 4], stat[:, b + 2:b + 3], stat[:, b + 2:b + 3], ALU.mult),
                [S(b + 2)], [S(b + 3)])
            P.op("v", lambda e, b=b, rb=rb, xn2np=xn2np, xn2bp=xn2bp, x2Tp=x2Tp, probp=probp, g0scrp=g0scrp, tb8p=tb8p: e.tensor_scalar(
                stat[:, b + 4:b + 5], stat[:, b + 1:b + 2], 1.0 / D, None, ALU.mult),
                [S(b + 1)], [S(b + 4)])
            P.op("v", lambda e, b=b, rb=rb, xn2np=xn2np, xn2bp=xn2bp, x2Tp=x2Tp, probp=probp, g0scrp=g0scrp, tb8p=tb8p: e.tensor_tensor(
                stat[:, b + 5:b + 6], stat[:, b + 4:b + 5], stat[:, b + 3:b + 4], ALU.subtract),
                [S(b + 3), S(b + 4)], [S(b + 5)])
            P.op("v", lambda e, b=b, rb=rb, xn2np=xn2np, xn2bp=xn2bp, x2Tp=x2Tp, probp=probp, g0scrp=g0scrp, tb8p=tb8p: e.tensor_scalar(
                stat[:, b + 5:b + 6], stat[:, b + 5:b + 6], 1e-5, None, ALU.add),
                [S(b + 5)], [S(b + 5)])
            P.op("s", lambda e, b=b, rb=rb, xn2np=xn2np, xn2bp=xn2bp, x2Tp=x2Tp, probp=probp, g0scrp=g0scrp, tb8p=tb8p: e.activation(
                stat[:, b + 6:b + 7], stat[:, b + 5:b + 6], AF.Sqrt), [S(b + 5)], [S(b + 6)])
            P.op("v", lambda e, b=b, rb=rb, xn2np=xn2np, xn2bp=xn2bp, x2Tp=x2Tp, probp=probp, g0scrp=g0scrp, tb8p=tb8p: e.reciprocal(stat[:, b + 7:b + 8], stat[:, b + 6:b + 7]),
                 [S(b + 6)], [S(b + 7)])
            P.op("v", lambda e, b=b, rb=rb, xn2np=xn2np, xn2bp=xn2bp, x2Tp=x2Tp, probp=probp, g0scrp=g0scrp, tb8p=tb8p: e.scalar_tensor_tensor(
                stat[:, b + 8:b + 9], stat[:, b + 2:b + 3], -1.0, stat[:, b + 7:b + 8],
                ALU.mult, ALU.mult), [S(b + 2), S(b + 7)], [S(b + 8)])
            P.op("s", lambda e, xb=xb, b=b, xn2np=xn2np, g0scrp=g0scrp: e.activation(
                xn2np[:], xb[:], AF.Identity, bias=stat[:, b + 8:b + 9], scale=stat[:, b + 7:b + 8]),
                [kx, S(b + 7), S(b + 8)], [("G0n", par)])
            P.op("v", lambda e, b=b, rb=rb, xn2np=xn2np, xn2bp=xn2bp, x2Tp=x2Tp, probp=probp, g0scrp=g0scrp, tb8p=tb8p: e.tensor_tensor(
                xn2np[:], xn2np[:], gmb[:], ALU.mult),
                [("G0n", par), ("gmb",)], [("G0n", par)])
            P.op("v", lambda e, b=b, rb=rb, xn2np=xn2np, xn2bp=xn2bp, x2Tp=x2Tp, probp=probp, g0scrp=g0scrp, tb8p=tb8p: e.tensor_tensor(
                xn2bp[:], xn2np[:], bmb[:], ALU.add),
                [("G0n", par), ("bmb",)], [("G0b", par)])
            for dc in range(ND):
                pp = (pt0, pt1)[dc % 2]
                kp = ("G0pt", dc % 2)
                P.op("t", lambda e, pp=pp, dc=dc, xn2bp=xn2bp: e.transpose(
                    pp[:], xn2bp[:, dc * 128:(dc + 1) * 128], ident_bf[:]),
                    [("G0b", par), KID], [kp])
                P.op("v", lambda e, pp=pp, dc=dc, x2Tp=x2Tp: e.tensor_copy(
                    x2Tp[:, dc * 128:(dc + 1) * 128], pp[:]),
                    [kp], [("G0T", par, dc)])
            for dc in range(ND):
                P.dma("so", xn2T_d[dc][:, r * 128:(r + 1) * 128],
                      x2Tp[:, dc * 128:(dc + 1) * 128],
                      reads=[("G0T", par, dc)], writes=[("x2Td", r, dc)])
            P.op("t", lambda e, b=b, rb=rb, xn2np=xn2np, xn2bp=xn2bp, x2Tp=x2Tp, probp=probp, g0scrp=g0scrp, tb8p=tb8p: e.matmul(pr_[:], ones_bf[:], br_sb[:],
                                         start=True, stop=False),
                 [("brs",), KONE], [("G0pr",)])
            for dc in range(ND):
                P.op("t", lambda e, dc=dc, x2Tp=x2Tp: e.matmul(
                    pr_[:], x2Tp[:, dc * 128:(dc + 1) * 128],
                    wr_sb[:, dc * 8:(dc + 1) * 8],
                    start=False, stop=(dc == ND - 1)),
                    [("wr",), ("G0T", par, dc)], [("G0pr",)])
            P.op("v", lambda e, b=b, rb=rb, xn2np=xn2np, xn2bp=xn2bp, x2Tp=x2Tp, probp=probp, g0scrp=g0scrp, tb8p=tb8p: e.tensor_copy(probp[:], pr_[:]),
                 [("G0pr",)], [("G0p", par)])
            P.op("v", lambda e, b=b, rb=rb, xn2np=xn2np, xn2bp=xn2bp, x2Tp=x2Tp, probp=probp, g0scrp=g0scrp, tb8p=tb8p: e.tensor_reduce(
                stat[:, rb + 0:rb + 1], probp[:], AX.X, ALU.max), [("G0p", par)], [S(9)])
            P.op("v", lambda e, b=b, rb=rb, xn2np=xn2np, xn2bp=xn2bp, x2Tp=x2Tp, probp=probp, g0scrp=g0scrp, tb8p=tb8p: e.tensor_scalar(
                stat[:, rb + 1:rb + 2], stat[:, rb + 0:rb + 1], -1.0, None, ALU.mult),
                [S(9)], [S(10)])
            P.op("s", lambda e, b=b, rb=rb, xn2np=xn2np, xn2bp=xn2bp, x2Tp=x2Tp, probp=probp, g0scrp=g0scrp, tb8p=tb8p: e.activation(
                probp[:], probp[:], AF.Exp, bias=stat[:, rb + 1:rb + 2],
                accum_out=stat[:, rb + 2:rb + 3]),
                [("G0p", par), S(10)], [("G0p", par), S(11)])
            P.op("v", lambda e, b=b, rb=rb, xn2np=xn2np, xn2bp=xn2bp, x2Tp=x2Tp, probp=probp, g0scrp=g0scrp, tb8p=tb8p: e.reciprocal(stat[:, rb + 3:rb + 4], stat[:, rb + 2:rb + 3]),
                 [S(11)], [S(12)])
            P.op("v", lambda e, b=b, rb=rb, xn2np=xn2np, xn2bp=xn2bp, x2Tp=x2Tp, probp=probp, g0scrp=g0scrp, tb8p=tb8p: e.tensor_scalar(
                probp[:], probp[:], stat[:, rb + 3:rb + 4], None, ALU.mult),
                [("G0p", par), S(12)], [("G0p", par)])
            P.op("v", lambda e, b=b, rb=rb, xn2np=xn2np, xn2bp=xn2bp, x2Tp=x2Tp, probp=probp, g0scrp=g0scrp, tb8p=tb8p: e.tensor_reduce(
                stat[:, rb + 0:rb + 1], probp[:], AX.X, ALU.max), [("G0p", par)], [S(9)])
            P.op("v", lambda e, b=b, rb=rb, xn2np=xn2np, xn2bp=xn2bp, x2Tp=x2Tp, probp=probp, g0scrp=g0scrp, tb8p=tb8p: e.tensor_scalar(
                tb8p[:, 0:8], probp[:], stat[:, rb + 0:rb + 1], None, ALU.is_ge),
                [("G0p", par), S(9)], [("G0i", par)])
            P.op("v", lambda e, b=b, rb=rb, xn2np=xn2np, xn2bp=xn2bp, x2Tp=x2Tp, probp=probp, g0scrp=g0scrp, tb8p=tb8p: e.scalar_tensor_tensor(
                tb8p[:, 8:16], tb8p[:, 0:8], NEG, probp[:], ALU.mult, ALU.add),
                [("G0i", par), ("G0p", par)], [("G0m", par)])
            P.op("v", lambda e, b=b, rb=rb, xn2np=xn2np, xn2bp=xn2bp, x2Tp=x2Tp, probp=probp, g0scrp=g0scrp, tb8p=tb8p: e.tensor_reduce(
                stat[:, rb + 1:rb + 2], tb8p[:, 8:16], AX.X, ALU.max),
                [("G0m", par)], [S(10)])
            P.op("v", lambda e, b=b, rb=rb, xn2np=xn2np, xn2bp=xn2bp, x2Tp=x2Tp, probp=probp, g0scrp=g0scrp, tb8p=tb8p: e.tensor_scalar(
                tb8p[:, 0:8], probp[:], stat[:, rb + 1:rb + 2], None, ALU.is_ge),
                [("G0p", par), S(10)], [("G0i", par)])
            P.op("v", lambda e, r=r, probp=probp, tb8p=tb8p: e.tensor_tensor(
                w_all[:, r * 8:(r + 1) * 8], probp[:], tb8p[:, 0:8], ALU.mult),
                [("G0p", par), ("G0i", par)], [("w", r)])
            P.op("t", lambda e, r=r, probp=probp, tb8p=tb8p: e.transpose(
                pw_[0:8, :], w_all[:, r * 8:(r + 1) * 8], ident_f[:]),
                [("w", r), KIDF], [("G0pw",)])
            P.op("v", lambda e, r=r, probp=probp, tb8p=tb8p: e.tensor_copy(
                wTp[0:8, r * 128:(r + 1) * 128], pw_[0:8, :]),
                [("G0pw",), ("wTpz",)], [("wT", r)])
        P.flush()

    xn2h = AT([128, ND * 1024], BF16, 4 * KB)
    h_sb = AT([128, NF * 1024], BF16, 20 * KB)
    w2_sb = AT([128, NF * 512], BF16, 84 * KB)
    ctl1 = AT([1, 512], F32, 116 * KB)
    cppb = AT([1, 1024], BF16, 118 * KB)
    cppP = AT([128, D], BF16, 120 * KB)
    ct_ld = AT([1, 1024], F32, 123 * KB)
    cacc = AT([1, 1024], F32, 127 * KB)
    b1_sb = AT([128, NF * 8], F32, 131 * KB)
    acc_h = AT([128, 8 * D], F32, 132 * KB)
    w1st = [AT([128, ND * 512], BF16, 164 * KB + i * 8 * KB) for i in range(2)]
    x2t = AT([128, D], F32, 180 * KB)
    f1t = [AT([128, D], F32, 184 * KB), AT([128, D], F32, 184 * KB)]
    hpd = AT([128, NF], BF16, 188 * KB)

    P.dma("ld", b1_sb[:].rearrange("p (m e) -> p m e", m=NF),
          b1T_d.rearrange("(m p) e -> p m e", p=128), writes=[("b1",)])
    P.op("v", lambda e: e.memset(cppP[:], 0.0), writes=[("cppz",)])
    P.flush()

    with (nc.psum_tensor([128, 1024], F32) as ph0,
          nc.psum_tensor([128, 1024], F32) as ph1,
          nc.psum_tensor([128, 512], F32) as po0,
          nc.psum_tensor([128, 512], F32) as po1):
        for half in range(2):
            for dc in range(ND):
                P.op("ld", lambda e, dc=dc, half=half: e.dma_start(
                    out=xn2h[:, dc * 1024:(dc + 1) * 1024],
                    in_=xn2T_d[dc][:, half * 1024:(half + 1) * 1024]),
                    reads=[("x2Td", rr, dc) for rr in range(NHT)],
                    writes=[("xn2h", dc)])
            for e_ in range(8):
                if half == 0:
                    P.op("s", lambda e, e_=e_: e.activation(
                        hpd[:], b1_sb[:].rearrange(
                            "p (m e) -> p m e", m=NF)[:, :, e_], AF.Relu),
                        [("b1",)], [("hpd",)])
                for fg in range(NF // 4):
                    wb = w1st[fg % 2]
                    kw = ("w1", fg % 2)
                    P.op("ld", lambda e, wb=wb, e_=e_, fg=fg: e.dma_start(
                        out=wb[:].rearrange("p (dc f) -> p dc f", dc=ND),
                        in_=w1b_d[e_][:, fg * 512:(fg + 1) * 512]
                        .rearrange("(dc p) f -> p dc f", p=128)),
                        writes=[kw], pre=[(wbg, "wbg", 256)])
                    for fs in range(4):
                        ft = fg * 4 + fs
                        ph = (ph0, ph1)[ft % 2]
                        kp = ("ph", ft % 2)
                        for dc in range(ND):
                            for n in range(2):
                                P.op("t", lambda e, ph=ph, wb=wb, dc=dc, n=n,
                                     fs=fs:
                                     e.matmul(ph[:, n * 512:(n + 1) * 512],
                                              wb[:, dc * 512 + fs * 128:
                                                 dc * 512 + (fs + 1) * 128],
                                              xn2h[:, dc * 1024 + n * 512:
                                                   dc * 1024 + (n + 1) * 512],
                                              start=(dc == 0),
                                              stop=(dc == ND - 1)),
                                     [kw, ("xn2h", dc)], [kp])
                        P.op("s", lambda e, ph=ph, ft=ft, e_=e_: e.activation(
                            h_sb[:, ft * 1024:(ft + 1) * 1024], ph[:], AF.Relu,
                            bias=b1_sb[:, ft * 8 + e_: ft * 8 + e_ + 1]),
                            [kp, ("b1",)], [("h", ft)])
                for h2c in range(2):
                    P.op("ld", lambda e, e_=e_, h2c=h2c: e.dma_start(
                        out=w2_sb[:].rearrange("p (fc h) -> p fc h", fc=NF),
                        in_=w2b_d[e_][:, h2c * 512:(h2c + 1) * 512]
                        .rearrange("(fc p) h -> p fc h", p=128)),
                        writes=[("w2",)], pre=[(wbg, "wbg", 256)])
                    for r in range(8):
                        po = (po0, po1)[r % 2]
                        kp = ("po", r % 2)
                        for ft in range(NF):
                            P.op("t", lambda e, po=po, ft=ft, r=r: e.matmul(
                                po[:], h_sb[:, ft * 1024 + r * 128:
                                            ft * 1024 + (r + 1) * 128],
                                w2_sb[:, ft * 512:(ft + 1) * 512],
                                start=(ft == 0), stop=(ft == NF - 1)),
                                [("w2",), ("h", ft)], [kp])
                        rg = half * 8 + r
                        if e_ == 0:
                            P.op("v", lambda e, po=po, r=r, h2c=h2c, rg=rg:
                                 e.tensor_scalar(
                                     acc_h[:, r * D + h2c * 512:
                                           r * D + h2c * 512 + 512],
                                     po[:], w_all[:, rg * 8: rg * 8 + 1],
                                     None, ALU.mult),
                                 [kp, ("w", rg)], [("acc", r, h2c)])
                        else:
                            P.op("v", lambda e, po=po, r=r, h2c=h2c, rg=rg,
                                 e_=e_: e.scalar_tensor_tensor(
                                     acc_h[:, r * D + h2c * 512:
                                           r * D + h2c * 512 + 512],
                                     po[:],
                                     w_all[:, rg * 8 + e_: rg * 8 + e_ + 1],
                                     acc_h[:, r * D + h2c * 512:
                                           r * D + h2c * 512 + 512],
                                     ALU.mult, ALU.add),
                                 [kp, ("w", rg), ("acc", r, h2c)],
                                 [("acc", r, h2c)])
                    if half == 0:
                        po = po0
                        kp = ("po", 0)
                        for ft in range(NF):
                            P.op("t", lambda e, po=po, ft=ft: e.matmul(
                                po[0:1, :], hpd[:, ft:ft + 1],
                                w2_sb[:, ft * 512:(ft + 1) * 512],
                                start=(ft == 0), stop=(ft == NF - 1)),
                                [("hpd",), ("w2",)], [kp])
                        P.op("v", lambda e, po=po: e.tensor_copy(
                            ctl1[:], po[0:1, :]), [kp], [("ctl1",)])
                        P.dma("gp", ctil_d[e_:e_ + 1, h2c * 512:(h2c + 1) * 512],
                              ctl1[:], reads=[("ctl1",)],
                              writes=[("ctd", e_, h2c)])
            if half == 0:
                ctr = [("ctd", e2, h2) for e2 in range(8) for h2 in range(2)]
                P.dma("ld", cacc[:], ctil_d[0:1, :], reads=ctr,
                      writes=[("cacc",)])
                for e2 in range(1, 8):
                    P.dma("ld", ct_ld[:], ctil_d[e2:e2 + 1, :], reads=ctr,
                          writes=[("ctld",)])
                    P.op("v", lambda e: e.tensor_tensor(
                        cacc[:], cacc[:], ct_ld[:], ALU.add),
                        [("cacc",), ("ctld",)], [("cacc",)])
                for e2 in range(8):
                    P.dma("ld", ct_ld[:], b2_d[e2:e2 + 1, :], writes=[("ctld",)])
                    P.op("v", lambda e: e.tensor_tensor(
                        cacc[:], cacc[:], ct_ld[:], ALU.add),
                        [("cacc",), ("ctld",)], [("cacc",)])
                for e2 in range(8):
                    P.dma("ld", ct_ld[:], ctil_d[e2:e2 + 1, :], reads=ctr,
                          writes=[("ctld",)])
                    P.op("v", lambda e: e.tensor_tensor(
                        cppb[:], cacc[:], ct_ld[:], ALU.subtract),
                        [("cacc",), ("ctld",)], [("cppb",)])
                    P.dma("gp", cpp_d[e2:e2 + 1, :], cppb[:],
                          reads=[("cppb",)], writes=[("cppd", e2)])
                P.dma("ld", cppP[0:8, :], cpp_d[:, :],
                      reads=[("cppd", e2) for e2 in range(8)] + [("cppz",)],
                      writes=[("cppP",)])
            for r in range(8):
                rg = half * 8 + r
                for n in range(2):
                    P.op("t", lambda e, po=(po0, po1)[n], rg=rg, n=n:
                         e.matmul(po[:],
                                  wTp[:, rg * 128:(rg + 1) * 128],
                                  cppP[:, n * 512:(n + 1) * 512],
                                  start=True, stop=True),
                         [("wT", rg), ("cppP",)], [("po", n)])
                for n in range(2):
                    P.op("v", lambda e, po=(po0, po1)[n], r=r, n=n:
                         e.scalar_tensor_tensor(
                             acc_h[:, r * D + n * 512: r * D + (n + 1) * 512],
                             po[:], 1.0,
                             acc_h[:, r * D + n * 512: r * D + (n + 1) * 512],
                             ALU.mult, ALU.add),
                         [("po", n), ("acc", r, 0), ("acc", r, 1)],
                         [("acc", r, n)])
                fb = f1t[0]
                kf = ("f1", 0)
                P.op("gp", lambda e, fb=fb, rg=rg: dyn_load(e, fb[:], x1_d, rg),
                    reads=[("idx",)] + [("x1d", jj) for jj in range(NT)],
                    writes=[kf])
                P.op("v", lambda e, fb=fb, r=r: e.tensor_tensor(
                    x2t[:], acc_h[:, r * D:(r + 1) * D], fb[:], ALU.add),
                    [("acc", r, 0), ("acc", r, 1), kf], [("x2t",)])
                P.dma("ld", fb[:], attn1_d[rg * 128:(rg + 1) * 128, :],
                      reads=[("x2t",), ("attn_d1", rg)], writes=[kf])
                P.op("v", lambda e, fb=fb: e.tensor_tensor(
                    x2t[:], x2t[:], fb[:], ALU.add), [("x2t",), kf], [("x2t",)])
                P.op("gp", lambda e, fb=fb, rg=rg: dyn_load(
                    e, fb[:], xn1f_d, rg),
                    reads=[("idx",), ("x2t",)] +
                          [("xnf_d", jj) for jj in range(NT)],
                    writes=[kf])
                P.op("v", lambda e, fb=fb: e.tensor_tensor(
                    x2t[:], x2t[:], fb[:], ALU.add), [("x2t",), kf], [("x2t",)])
                P.dma("so", x2own[rg * 128:(rg + 1) * 128, :], x2t[:],
                      reads=[("x2t",)], writes=[("out", rg)])
        P.flush()

    P.op("ld", lambda e: e.nop(),
         deps=[P.lastw[("out", rg)] for rg in range(NHT)])
    P.flush()
    return nc


def kernel(**inputs):
    tabs = _host_tables()
    cm2 = (tabs.pop("cmask2_h0"), tabs.pop("cmask2_h1"))
    g = _prep_inputs(inputs)
    nc = build_program()
    X = inputs["L0_attn_X"]
    base = {}
    base.update(tabs)
    base.update({"wq0d": g["wq0d"], "wk0d": g["wk0d"], "wv0": g["wv0"],
                 "gwT": g["gwT"], "gbT": g["gbT"], "ew_g": g["ew_g"],
                 "eb0": g["eb0"], "eids_row": tabs["eids_row"],
                 "iota_p": tabs["iota_p"], "wq1": g["wq1"], "wk1": g["wk1"],
                 "wv1": g["wv1"], "gamma_bc": g["gamma_bc"],
                 "beta_bc": g["beta_bc"], "Wr": g["Wr"], "br_row": g["br_row"],
                 "W1": g["W1"], "b1T": g["b1T"], "W2": g["W2"], "b2": g["b2"]})
    in_maps = []
    for c in range(8):
        m = dict(base)
        m["x0"] = np.ascontiguousarray(X[c // 2])
        h = c % 2
        idx = np.empty((128, NHT), dtype=np.int32)
        for r in range(NHT):
            idx[:, r] = (2 * r + h) * 128 + np.arange(128)
        m["idx_own"] = idx
        m["cmask2"] = cm2[h]
        in_maps.append(m)
    res = run_bass_kernel_spmd(nc, in_maps, core_ids=list(range(8)))
    out = np.empty((B, T, D), dtype=np.float32)
    for b in range(B):
        ob = out[b].reshape(NT, 128, D)
        ob[0::2] = res.results[2 * b]["x2own"].reshape(NHT, 128, D)
        ob[1::2] = res.results[2 * b + 1]["x2own"].reshape(NHT, 128, D)
    return out




# version 21
# speedup vs baseline: 1.1420x; 1.0438x over previous
import numpy as np
import ml_dtypes
import concourse.bass as bass
import concourse.mybir as mybir
from concourse.bass_utils import run_bass_kernel_spmd

F32 = mybir.dt.float32
BF16 = mybir.dt.bfloat16
AF = mybir.ActivationFunctionType
ALU = mybir.AluOpType
AX = mybir.AxisListType

B, T, D = 4, 4096, 1024
NT = T // 128
NKC = T // 512
ND = D // 128
HTOK = T // 2
NHT = HTOK // 128
GAMMA = 0.9
WIN = 1024
FF = 4096
NF = FF // 128
NEG = -1e30
BF_NP = ml_dtypes.bfloat16
KB = 1024


def _host_tables():
    import jax
    import jax.numpy as jnp
    half = D // 2
    try:
        _cpu = jax.devices("cpu")[0]
    except Exception:
        _cpu = None
    import contextlib
    with (jax.default_device(_cpu) if _cpu is not None
          else contextlib.nullcontext()):
        scale_vec = (jnp.arange(0, D, 2, dtype=jnp.float32) + 0.4 * D) / (1.4 * D)
        power = jnp.arange(T, dtype=jnp.float32) / 512.0
        scale = scale_vec[None, :] ** power[:, None]
        inv_scale = 1.0 / scale
        inv_freq = 1.0 / 10000.0 ** (jnp.arange(half, dtype=jnp.float32) / half)
        sinusoid = jnp.arange(T, dtype=jnp.float32)[:, None] * inv_freq[None, :]
        cosq = np.asarray(jnp.cos(sinusoid) * scale)
        sinq = np.asarray(jnp.sin(sinusoid) * scale)
        cosk = np.asarray(jnp.cos(sinusoid) * inv_scale)
        sink = np.asarray(jnp.sin(sinusoid) * inv_scale)
    t = {}
    t["cosq"] = np.ascontiguousarray(cosq.T)
    t["sinq"] = np.ascontiguousarray(sinq.T)
    t["cosk"] = np.ascontiguousarray(cosk.T)
    t["sink"] = np.ascontiguousarray(sink.T)
    i = np.arange(128, dtype=np.float64)
    j = np.arange(512, dtype=np.float64)
    t["gvpos"] = (GAMMA ** i).astype(np.float32).reshape(128, 1)
    t["gvneg"] = (GAMMA ** (-i)).astype(np.float32).reshape(128, 1)
    t["gneg"] = np.ascontiguousarray(
        np.broadcast_to((GAMMA ** (-j)).astype(np.float32), (128, 512)))
    t["gpos"] = np.ascontiguousarray(
        np.broadcast_to((GAMMA ** j).astype(np.float32), (128, 512)))
    dd = np.zeros((128, 4 * 512), dtype=np.float32)
    for r in range(4):
        dd[:, r * 512:(r + 1) * 512] = (
            GAMMA ** np.abs(128.0 * r + i[:, None] - j[None, :])
        ).astype(np.float32)
    t["ddiag"] = dd
    cm = np.zeros((128, 128), dtype=np.float32)
    cm[np.triu_indices(128, 1)] = NEG
    t["cmask"] = cm
    i = np.arange(128)[:, None]
    c = np.arange(256)[None, :]
    t["cmask2_h0"] = np.where(c <= i, 0.0, NEG).astype(np.float32)
    t["cmask2_h1"] = np.where(c <= 128 + i, 0.0, NEG).astype(np.float32)
    t["ident_bf"] = np.eye(128, dtype=BF_NP)
    t["ident_f"] = np.eye(128, dtype=np.float32)
    t["ones_row"] = np.ones((1, 128), dtype=BF_NP)
    t["eids_row"] = np.arange(8, dtype=np.float32).reshape(1, 8)
    t["iota_p"] = np.arange(128, dtype=np.float32).reshape(128, 1)
    return t


def _prep_inputs(inp):
    g = {}
    wq, wk = inp["L0_attn_W_Q"], inp["L0_attn_W_K"]
    g["wq0d"] = np.ascontiguousarray(np.concatenate([wq[:, 0::2], wq[:, 1::2]], 1))
    g["wk0d"] = np.ascontiguousarray(np.concatenate([wk[:, 0::2], wk[:, 1::2]], 1))
    g["wv0"] = np.ascontiguousarray(inp["L0_attn_W_V"])
    g["gwT"] = np.ascontiguousarray(inp["L0_ffn_gate_w"].T)
    g["gbT"] = np.ascontiguousarray(inp["L0_ffn_gate_b"].reshape(8, 1))
    g["ew_g"] = np.ascontiguousarray(
        np.concatenate([inp["L0_ffn_expert_w"][e].T for e in range(8)], axis=0))
    g["eb0"] = np.ascontiguousarray(inp["L0_ffn_expert_b"])
    g["wq1"] = np.ascontiguousarray(inp["L1_attn_w_q"])
    g["wk1"] = np.ascontiguousarray(inp["L1_attn_w_k"])
    g["wv1"] = np.ascontiguousarray(inp["L1_attn_w_v"])
    g["gamma_bc"] = np.ascontiguousarray(np.broadcast_to(inp["L1_ffn_gamma"], (128, D)))
    g["beta_bc"] = np.ascontiguousarray(np.broadcast_to(inp["L1_ffn_beta"], (128, D)))
    g["Wr"] = np.ascontiguousarray(inp["L1_ffn_Wr"])
    g["br_row"] = np.ascontiguousarray(inp["L1_ffn_br"].reshape(1, 8))
    g["W1"] = np.ascontiguousarray(inp["L1_ffn_W1"])
    g["b1T"] = np.ascontiguousarray(inp["L1_ffn_b1"].T)
    g["W2"] = np.ascontiguousarray(inp["L1_ffn_W2"])
    g["b2"] = np.ascontiguousarray(inp["L1_ffn_b2"])
    return g


class Prog:

    QSEM = {"ld": 16, "gp": 16, "so": 16, "t": 1, "v": 1, "s": 1}
    SEC = {"ld": "sync", "gp": "gpsimd", "so": "scalar", "s": "scalar",
           "t": "tensor", "v": "vector"}
    DMAQ = ("ld", "gp", "so")
    NPOOL = 16

    def __init__(self, nc):
        self.nc = nc
        self.sem = {q: nc.alloc_semaphore("q_" + q) for q in self.QSEM}
        self.done = {q: nc.alloc_semaphore("d_" + q) for q in self.DMAQ}
        self.pool = {q: [nc.alloc_semaphore(f"p_{q}{i}")
                         for i in range(self.NPOOL)] for q in self.DMAQ}
        self.pool_cnt = {q: [0] * self.NPOOL for q in self.DMAQ}
        self.pool_rr = {q: 0 for q in self.DMAQ}
        self.cnt = {q: 0 for q in self.QSEM}
        self.done_cnt = {q: 0 for q in self.DMAQ}
        self.all = []
        self.start = 0
        self.lastw = {}
        self.readers = {}
        self.waited = {}

    def op(self, q, fn, reads=(), writes=(), deps=(), pre=(), bgsem=None):
        i = len(self.all)
        dep = set(d for d in deps if d is not None)
        for r in reads:
            w = self.lastw.get(r)
            if w is not None:
                dep.add(w)
        for w in writes:
            lw = self.lastw.get(w)
            if lw is not None:
                dep.add(lw)
            dep |= self.readers.get(w, set())
        dep.discard(i)
        self.all.append(dict(q=q, fn=fn, deps=dep, ticket=None, psem=None,
                             inc=False, pre=tuple(pre), bgsem=bgsem))
        for r in reads:
            self.readers.setdefault(r, set()).add(i)
        for w in writes:
            self.lastw[w] = i
            self.readers[w] = set()
        return i

    def dma(self, q, out, in_, reads=(), writes=(), deps=()):
        return self.op(q, (lambda e, o=out, s=in_: e.dma_start(out=o, in_=s)),
                       reads, writes, deps)

    def _skip(self, d, o_q, n0):
        p = self.all[d]
        if p["q"] == "t" and o_q == "t":
            return True
        if d < n0:
            return True
        return False

    def flush(self):
        n0 = self.start
        batch = self.all[n0:]
        if not batch:
            return
        prev_q = {q: self.cnt[q] for q in self.DMAQ}
        prev_pool = {q: list(self.pool_cnt[q]) for q in self.DMAQ}

        for j, o in enumerate(batch, n0):
            for d in o["deps"]:
                if self._skip(d, o["q"], n0):
                    continue
                self.all[d]["inc"] = True
        for o in batch:
            if o["bgsem"] is not None:
                continue
            if o["q"] in self.DMAQ:
                if o["inc"]:
                    qq = o["q"]
                    i = self.pool_rr[qq]
                    self.pool_rr[qq] = (i + 1) % self.NPOOL
                    o["psem"] = (qq, i)
                    o["pprev"] = self.pool_cnt[qq][i]
                    self.pool_cnt[qq][i] += 16
                    o["ticket"] = self.pool_cnt[qq][i]
                else:
                    self.cnt[o["q"]] += 16
                    o["ticket"] = self.cnt[o["q"]]
            elif o["inc"]:
                self.cnt[o["q"]] += 1
                o["ticket"] = self.cnt[o["q"]]
        fence = {}
        for q in self.DMAQ:
            self.done_cnt[q] += 1
            fence[q] = self.done_cnt[q]

        def emit(eng, queues):
            ekey = str(eng.engine)

            def waitv(semkey, sem, v):
                if v <= 0:
                    return
                key = (ekey, semkey)
                if self.waited.get(key, -1) < v:
                    eng.wait_ge(sem, v)
                    self.waited[key] = v

            for q in queues:
                if q in self.DMAQ:
                    waitv(q, self.sem[q], prev_q[q])
                    for i in range(self.NPOOL):
                        waitv(("pool", q, i), self.pool[q][i],
                              prev_pool[q][i])
                    eng.sem_inc(self.done[q], 1)
            for q in self.DMAQ:
                waitv("d_" + q, self.done[q], fence[q])

            for j, o in enumerate(batch, n0):
                if o["q"] not in queues:
                    continue
                for (sm, smkey, v) in o["pre"]:
                    waitv(smkey, sm, v)
                if o["bgsem"] is not None:
                    inst = o["fn"](eng)
                    inst.then_inc(o["bgsem"], 16)
                    continue
                if o["psem"] is not None:
                    qq, i = o["psem"]
                    waitv(("pool", qq, i), self.pool[qq][i], o["pprev"])
                for d in sorted(o["deps"]):
                    if self._skip(d, o["q"], n0):
                        continue
                    p = self.all[d]
                    pq = p["q"]
                    if pq in self.DMAQ:
                        if p["psem"] is not None:
                            qq, i = p["psem"]
                            waitv(("pool", qq, i), self.pool[qq][i],
                                  p["ticket"])
                        else:
                            assert self.SEC[pq] == self.SEC[o["q"]],                                 (d, pq, o["q"])
                            waitv(pq, self.sem[pq], p["ticket"])
                    else:
                        assert p["ticket"] is not None, (d, p["q"])
                        waitv(pq, self.sem[pq], p["ticket"])
                inst = o["fn"](eng)
                if o["q"] in self.DMAQ:
                    if o["psem"] is not None:
                        qq, i = o["psem"]
                        inst.then_inc(self.pool[qq][i], 16)
                    else:
                        inst.then_inc(self.sem[o["q"]], 16)
                elif o["inc"]:
                    inst.then_inc(self.sem[o["q"]], 1)

        with self.nc.Block() as blk:
            blk.sync(lambda e: emit(e, ("ld",)))
            blk.gpsimd(lambda e: emit(e, ("gp",)))
            blk.scalar(lambda e: emit(e, ("s", "so")))
            blk.tensor(lambda e: emit(e, ("t",)))
            blk.vector(lambda e: emit(e, ("v",)))
        self.start = len(self.all)


SB_BIG = 35840
SB_TOP = 229376
import os
_KPH = int(os.environ.get("KPH", "99"))


def build_program():
    nc = bass.Bass("TRN2", target_bir_lowering=False)

    def EI(name, shape, dtype=F32):
        return nc.dram_tensor(name, list(shape), dtype, kind="ExternalInput").ap()

    x0 = EI("x0", (T, D))
    wq0d = EI("wq0d", (D, D)); wk0d = EI("wk0d", (D, D)); wv0 = EI("wv0", (D, D))
    cosq = EI("cosq", (512, T)); sinq = EI("sinq", (512, T))
    cosk = EI("cosk", (512, T)); sink = EI("sink", (512, T))
    gwT_d = EI("gwT", (D, 8)); gbT_d = EI("gbT", (8, 1))
    ew_g = EI("ew_g", (8 * D, 1024)); eb0 = EI("eb0", (8, 1024))
    eids_d = EI("eids_row", (1, 8)); iotap_d = EI("iota_p", (128, 1))
    wq1 = EI("wq1", (D, D)); wk1 = EI("wk1", (D, D)); wv1 = EI("wv1", (D, D))
    gamma_d = EI("gamma_bc", (128, D)); beta_d = EI("beta_bc", (128, D))
    Wr_d = EI("Wr", (D, 8)); br_d = EI("br_row", (1, 8))
    W1_d = EI("W1", (8, D, FF)); b1T_d = EI("b1T", (FF, 8))
    W2_d = EI("W2", (8, FF, D)); b2_d = EI("b2", (8, 1024))
    gvpos_d = EI("gvpos", (128, 1)); gvneg_d = EI("gvneg", (128, 1))
    gneg_d = EI("gneg", (128, 512)); gpos_d = EI("gpos", (128, 512))
    ddiag_d = EI("ddiag", (128, 4 * 512)); cmask_d = EI("cmask", (128, 128))
    cmask2_d = EI("cmask2", (128, 256))
    idx_d = EI("idx_own", (128, NHT), mybir.dt.int32)
    identbf_d = EI("ident_bf", (128, 128), BF16)
    identf_d = EI("ident_f", (128, 128))
    ones_d = EI("ones_row", (1, 128), BF16)

    x2own = nc.dram_tensor("x2own", [HTOK, D], F32, kind="ExternalOutput").ap()

    def DSc(name, shape, dtype=F32):
        return nc.dram_tensor(name, list(shape), dtype, kind="Internal").ap()

    xnT0_d = DSc("xnT0_d", (ND, 128, T), BF16)
    qT0_d = DSc("qT0_d", (ND, 128, T), BF16)
    attn0_d = DSc("attn0_d", (T, D))
    x1_d = DSc("x1_d", (T, D))
    xn1f_d = DSc("xn1f_d", (T, D))
    xn1T_d = DSc("xn1T_d", (ND, 128, T), BF16)
    q1tm_d = DSc("q1tm_d", (T, D), BF16)
    attn1_d = DSc("attn1_d", (HTOK, D))
    xn2T_d = DSc("xn2T_d", (ND, 128, HTOK), BF16)
    row8_d = DSc("row8_d", (8, 1))
    grow_d = DSc("grow_d", (1, 8))
    se2_d = DSc("se2_d", (1, 2))
    g2_d = DSc("g2_d", (1, 2))
    eb2_d = DSc("eb2_d", (2, 1024))
    ctil_d = DSc("ctil_d", (8, 1024))
    w1b_d = DSc("w1b_d", (8, D, FF), BF16)
    w2b_d = DSc("w2b_d", (8, FF, D), BF16)
    cpp_d = DSc("cpp_d", (8, 1024), BF16)

    P = Prog(nc)

    ident_bf = nc.alloc_sbuf_tensor("ident_bf_s", [128, 128], BF16)
    ident_f = nc.alloc_sbuf_tensor("ident_f_s", [128, 128], F32)
    ones_bf = nc.alloc_sbuf_tensor("ones_bf_s", [1, 128], BF16)
    gv_pos = nc.alloc_sbuf_tensor("gv_pos", [128, 1], F32)
    gv_neg = nc.alloc_sbuf_tensor("gv_neg", [128, 1], F32)
    gneg_s = nc.alloc_sbuf_tensor("gneg_s", [128, 512], F32)
    gpos_s = nc.alloc_sbuf_tensor("gpos_s", [128, 512], F32)
    cmask_s = nc.alloc_sbuf_tensor("cmask_s", [128, 128], F32)
    cmask2_s = nc.alloc_sbuf_tensor("cmask2_s", [128, 256], F32)
    gates_bc = nc.alloc_sbuf_tensor("gates_bc", [128, 8], F32)
    stat = nc.alloc_sbuf_tensor("stat", [128, 64], F32)
    st8 = nc.alloc_sbuf_tensor("st8", [1, 72], F32)
    eids_s = nc.alloc_sbuf_tensor("eids_s", [1, 8], F32)
    iop_s = nc.alloc_sbuf_tensor("iop_s", [128, 1], F32)
    sef_bc = nc.alloc_sbuf_tensor("sef_bc", [128, 2], F32)
    gbc2 = nc.alloc_sbuf_tensor("gbc2", [128, 2], F32)
    idxg = nc.alloc_sbuf_tensor("idxg", [128, 16], mybir.dt.int32)
    idx1e = nc.alloc_sbuf_tensor("idx1e", [2, 1], mybir.dt.int32)
    se21 = nc.alloc_sbuf_tensor("se21", [2, 1], F32)
    idx_sb = nc.alloc_sbuf_tensor("idx_sb", [128, NHT], mybir.dt.int32)
    assert nc.bytes_allocated(bass.MemorySpace.SBUF) <= SB_BIG, \
        nc.bytes_allocated(bass.MemorySpace.SBUF)

    _atn = [0]
    def AT(shape, dtype, off):
        _atn[0] += 1
        end = SB_BIG + off + int(np.prod(shape[1:])) * mybir.dt.size(dtype)
        assert end <= SB_TOP, (shape, off, end)
        return nc.alloc_sbuf_tensor_at(f"at{_atn[0]}", list(shape), dtype,
                                       offset=SB_BIG + off)

    S = lambda c: ("stat", c)

    for dst, src in ((ident_bf, identbf_d), (ident_f, identf_d),
                     (ones_bf, ones_d), (gv_pos, gvpos_d), (gv_neg, gvneg_d),
                     (gneg_s, gneg_d), (gpos_s, gpos_d), (cmask_s, cmask_d),
                     (cmask2_s, cmask2_d)):
        P.dma("ld", dst[:], src, writes=[("const", dst.name)])
    P.dma("ld", idx_sb[:], idx_d, writes=[("idx",)])
    P.dma("ld", eids_s[:], eids_d, writes=[("eids",)])
    P.dma("ld", iop_s[:], iotap_d, writes=[("iop",)])
    KID = ("const", "ident_bf_s")
    KIDF = ("const", "ident_f_s")
    KONE = ("const", "ones_bf_s")

    def ph_ln(src_d, xnT, tag, src_key=None, xnf_d=None):
        xt = [AT([128, D], F32, 128 * KB + i * 4 * KB) for i in range(2)]
        scr = [AT([128, D], F32, 136 * KB + i * 4 * KB) for i in range(2)]
        xnb = [AT([128, D], BF16, 144 * KB + i * 2 * KB) for i in range(2)]
        xnf = [AT([128, D], F32, 148 * KB + i * 4 * KB) for i in range(2)]
        with (nc.psum_tensor([128, 128], BF16) as pt0,
              nc.psum_tensor([128, 128], BF16) as pt1):
            ptr = [pt0, pt1]
            for j in range(NT):
                par = j % 2
                b = 16 * par
                xb = xt[par]
                xnbp, xnfp, scrp = xnb[par], xnf[par], scr[par]
                kxt = (tag, "xt", par)
                P.dma("ld", xb[:], src_d[j * 128:(j + 1) * 128, :],
                      reads=([src_key(j)] if src_key else []), writes=[kxt])
                P.op("v", lambda e, xb=xb, b=b: e.tensor_reduce(
                    stat[:, b + 0:b + 1], xb[:], AX.X, ALU.add),
                    [kxt], [S(b + 0)])
                P.op("s", lambda e, xb=xb, b=b, scrp=scrp: e.activation(
                    scrp[:], xb[:], AF.Square, accum_out=stat[:, b + 1:b + 2]),
                    [kxt], [(tag, "scr", par), S(b + 1)])
                P.op("v", lambda e, b=b: e.tensor_scalar(
                    stat[:, b + 2:b + 3], stat[:, b + 0:b + 1], 1.0 / D, None,
                    ALU.mult), [S(b + 0)], [S(b + 2)])
                P.op("v", lambda e, b=b: e.tensor_tensor(
                    stat[:, b + 3:b + 4], stat[:, b + 2:b + 3],
                    stat[:, b + 2:b + 3], ALU.mult), [S(b + 2)], [S(b + 3)])
                P.op("v", lambda e, b=b: e.tensor_scalar(
                    stat[:, b + 4:b + 5], stat[:, b + 1:b + 2], 1.0 / D, None,
                    ALU.mult), [S(b + 1)], [S(b + 4)])
                P.op("v", lambda e, b=b: e.tensor_tensor(
                    stat[:, b + 5:b + 6], stat[:, b + 4:b + 5],
                    stat[:, b + 3:b + 4], ALU.subtract),
                    [S(b + 3), S(b + 4)], [S(b + 5)])
                P.op("v", lambda e, b=b: e.tensor_scalar(
                    stat[:, b + 5:b + 6], stat[:, b + 5:b + 6], 1e-5, None,
                    ALU.add), [S(b + 5)], [S(b + 5)])
                P.op("s", lambda e, b=b: e.activation(
                    stat[:, b + 6:b + 7], stat[:, b + 5:b + 6], AF.Sqrt),
                    [S(b + 5)], [S(b + 6)])
                P.op("v", lambda e, b=b: e.reciprocal(
                    stat[:, b + 7:b + 8], stat[:, b + 6:b + 7]),
                    [S(b + 6)], [S(b + 7)])
                P.op("v", lambda e, b=b: e.scalar_tensor_tensor(
                    stat[:, b + 8:b + 9], stat[:, b + 2:b + 3], -1.0,
                    stat[:, b + 7:b + 8], ALU.mult, ALU.mult),
                    [S(b + 2), S(b + 7)], [S(b + 8)])
                P.op("s", lambda e, xb=xb, b=b, xnbp=xnbp: e.activation(
                    xnbp[:], xb[:], AF.Identity, bias=stat[:, b + 8:b + 9],
                    scale=stat[:, b + 7:b + 8]),
                    [kxt, S(b + 7), S(b + 8)], [(tag, "xnb", par)])
                if xnf_d is not None:
                    P.op("s", lambda e, xb=xb, b=b, xnfp=xnfp: e.activation(
                        xnfp[:], xb[:], AF.Identity, bias=stat[:, b + 8:b + 9],
                        scale=stat[:, b + 7:b + 8]),
                        [kxt, S(b + 7), S(b + 8)], [(tag, "xnf", par)])
                    P.dma("so", xnf_d[j * 128:(j + 1) * 128, :], xnfp[:],
                          reads=[(tag, "xnf", par)], writes=[("xnf_d", j)])
                for dc in range(ND):
                    pp = ptr[dc % 2]
                    kp = (tag, "pt", dc % 2)
                    P.op("t", lambda e, pp=pp, dc=dc, xnbp=xnbp: e.transpose(
                        pp[:], xnbp[:, dc * 128:(dc + 1) * 128], ident_bf[:]),
                        [(tag, "xnb", par), KID], [kp])
                    P.op("v", lambda e, pp=pp, dc=dc, j=j: e.tensor_copy(
                        xnT[:, dc * T + j * 128: dc * T + (j + 1) * 128], pp[:]),
                        [kp], [("xnT" + tag, dc, j // 4)])
            P.flush()

    def ph_qk_xpos(w_d, cos_d, sin_d, xnT, tag, out_kT=None, out_q_d=None):
        t0 = tag[0]
        w_sb = AT([128, ND * D], BF16, 128 * KB)
        cs = [AT([128, 512], F32, 144 * KB + i * 2 * KB) for i in range(2)]
        sn = [AT([128, 512], F32, 148 * KB + i * 2 * KB) for i in range(2)]
        tm1 = AT([128, 512], F32, 152 * KB)
        tm2 = AT([128, 512], F32, 154 * KB)
        qst = AT([128, 2 * T], BF16, 156 * KB)
        P.dma("gp", w_sb[:].rearrange("p (dc o) -> p dc o", dc=ND),
              w_d.rearrange("(dc p) o -> p dc o", p=128), writes=[(tag, "w")])
        with (nc.psum_tensor([128, 512], F32) as pA0,
              nc.psum_tensor([128, 512], F32) as pA1,
              nc.psum_tensor([128, 512], F32) as pB0,
              nc.psum_tensor([128, 512], F32) as pB1):
            for pr in range(4):
                for c in range(NKC):
                    ib = c % 2
                    pa, pb = (pA0, pB0) if ib == 0 else (pA1, pB1)
                    ka, kb = (tag, "pa", ib), (tag, "pb", ib)
                    kc, ks = (tag, "cs", ib), (tag, "sn", ib)
                    P.dma("ld", cs[ib][:],
                          cos_d[(pr * 128):(pr + 1) * 128,
                                c * 512:(c + 1) * 512], writes=[kc])
                    P.dma("ld", sn[ib][:],
                          sin_d[(pr * 128):(pr + 1) * 128,
                                c * 512:(c + 1) * 512], writes=[ks])
                    for jo, ps, kk in ((pr, pa, ka), (pr + 4, pb, kb)):
                        for dc in range(ND):
                            P.op("t", lambda e, ps=ps, jo=jo, dc=dc, c=c:
                                 e.matmul(ps[:],
                                          w_sb[:, dc * D + jo * 128:
                                               dc * D + (jo + 1) * 128],
                                          xnT[:, dc * T + c * 512:
                                              dc * T + (c + 1) * 512],
                                          start=(dc == 0), stop=(dc == ND - 1)),
                                 [(tag, "w"), ("xnT" + t0, dc, c)], [kk])
                    P.op("v", lambda e, pa=pa, ib=ib: e.tensor_tensor(
                        tm1[:], pa[:], cs[ib][:], ALU.mult),
                        [ka, kc], [(tag, "tm1")])
                    P.op("v", lambda e, pb=pb, ib=ib: e.tensor_tensor(
                        tm2[:], pb[:], sn[ib][:], ALU.mult),
                        [kb, ks], [(tag, "tm2")])
                    if out_kT is not None:
                        P.op("v", lambda e, pr=pr, c=c: e.tensor_tensor(
                            out_kT[:, pr * T + c * 512: pr * T + (c + 1) * 512],
                            tm1[:], tm2[:], ALU.subtract),
                            [(tag, "tm1"), (tag, "tm2")], [("kT" + t0, pr, c)])
                    else:
                        P.op("v", lambda e, c=c: e.tensor_tensor(
                            qst[:, c * 512:(c + 1) * 512],
                            tm1[:], tm2[:], ALU.subtract),
                            [(tag, "tm1"), (tag, "tm2")], [(tag, "qst", 0, c)])
                    P.op("v", lambda e, pb=pb, ib=ib: e.tensor_tensor(
                        tm1[:], pb[:], cs[ib][:], ALU.mult),
                        [kb, kc], [(tag, "tm1")])
                    P.op("v", lambda e, pa=pa, ib=ib: e.tensor_tensor(
                        tm2[:], pa[:], sn[ib][:], ALU.mult),
                        [ka, ks], [(tag, "tm2")])
                    if out_kT is not None:
                        P.op("v", lambda e, pr=pr, c=c: e.tensor_tensor(
                            out_kT[:, (pr + 4) * T + c * 512:
                                   (pr + 4) * T + (c + 1) * 512],
                            tm1[:], tm2[:], ALU.add),
                            [(tag, "tm1"), (tag, "tm2")],
                            [("kT" + t0, pr + 4, c)])
                    else:
                        P.op("v", lambda e, c=c: e.tensor_tensor(
                            qst[:, T + c * 512: T + (c + 1) * 512],
                            tm1[:], tm2[:], ALU.add),
                            [(tag, "tm1"), (tag, "tm2")], [(tag, "qst", 1, c)])
                if out_q_d is not None:
                    for hh, jo in ((0, pr), (1, pr + 4)):
                        P.dma("so", out_q_d[jo][:, :],
                              qst[:, hh * T:(hh + 1) * T],
                              reads=[(tag, "qst", hh, c) for c in range(NKC)],
                              writes=[("qT_d" + t0, jo)])
            P.flush()

    def ph_spill_xnT(xnT, spill_d, tag):
        for dc in range(ND):
            P.dma("so", spill_d[dc][:, :], xnT[:, dc * T:(dc + 1) * T],
                  reads=[("xnT" + tag, dc, c) for c in range(NKC)],
                  writes=[("xnTd" + tag, dc)])

    def ph_v(wv_d, spill_d, v_sb, tag, xoff, wq_d=None, qtm_d=None):
        wv_sb = AT([128, ND * D], BF16, 128 * KB)
        xc = [AT([128, ND * 128], BF16, xoff + i * 2 * KB) for i in range(2)]
        P.dma("gp", wv_sb[:].rearrange("p (dc o) -> p dc o", dc=ND),
              wv_d.rearrange("(dc p) o -> p dc o", p=128), writes=[(tag, "wv")])
        if wq_d is not None:
            wq_sb = AT([128, ND * D], BF16, 144 * KB)
            qtt = [AT([128, D], BF16, 164 * KB + i * 2 * KB) for i in range(2)]
            P.dma("gp", wq_sb[:].rearrange("p (dc o) -> p dc o", dc=ND),
                  wq_d.rearrange("(dc p) o -> p dc o", p=128),
                  writes=[(tag, "wq")])
        with (nc.psum_tensor([128, 512], F32) as p0,
              nc.psum_tensor([128, 512], F32) as p1,
              nc.psum_tensor([128, 512], F32) as p2,
              nc.psum_tensor([128, 512], F32) as p3):
            for j in range(NT):
                xb = xc[j % 2]
                kx = (tag, "vxc", j % 2)
                P.dma("ld", xb[:].rearrange("p (dc k) -> p dc k", dc=ND),
                      spill_d[:, :, j * 128:(j + 1) * 128]
                      .rearrange("dc p k -> p dc k"),
                      reads=[("xnTd" + tag, dc) for dc in range(ND)],
                      writes=[kx])
                for n in range(2):
                    ps = (p0, p1)[n]
                    kk = (tag, "vps", n)
                    for dc in range(ND):
                        P.op("t", lambda e, ps=ps, xb=xb, dc=dc, n=n: e.matmul(
                            ps[:], xb[:, dc * 128:(dc + 1) * 128],
                            wv_sb[:, dc * D + n * 512: dc * D + (n + 1) * 512],
                            start=(dc == 0), stop=(dc == ND - 1)),
                            [(tag, "wv"), kx], [kk])
                    P.op("v", lambda e, ps=ps, j=j, n=n: e.tensor_copy(
                        v_sb[:, j * D + n * 512: j * D + (n + 1) * 512], ps[:]),
                        [kk], [("v" + tag, j)])
                if wq_d is not None:
                    qb2 = qtt[j % 2]
                    kq2 = (tag, "qtt", j % 2)
                    for n in range(2):
                        ps = (p2, p3)[n]
                        kk = (tag, "qps", n)
                        for dc in range(ND):
                            P.op("t", lambda e, ps=ps, xb=xb, dc=dc, n=n:
                                 e.matmul(ps[:],
                                          xb[:, dc * 128:(dc + 1) * 128],
                                          wq_sb[:, dc * D + n * 512:
                                                dc * D + (n + 1) * 512],
                                          start=(dc == 0), stop=(dc == ND - 1)),
                                 [(tag, "wq"), kx], [kk])
                        P.op("v", lambda e, ps=ps, qb2=qb2, n=n: e.tensor_copy(
                            qb2[:, n * 512:(n + 1) * 512], ps[:]),
                            [kk], [kq2 + (n,)])
                    P.dma("so", qtm_d[j * 128:(j + 1) * 128, :], qb2[:],
                          reads=[kq2 + (0,), kq2 + (1,)],
                          writes=[("q1tm", j)])
            P.flush()

    def ph_plain_kq(wk_d, wq_d, xnT, kT, q_d, tag):
        w_sb = AT([128, ND * D], BF16, 128 * KB)
        qst = AT([128, T], BF16, 152 * KB)
        for w_d, dest in ((wk_d, "k"),):
            P.dma("gp", w_sb[:].rearrange("p (dc o) -> p dc o", dc=ND),
                  w_d.rearrange("(dc p) o -> p dc o", p=128),
                  writes=[(tag, "w")])
            with (nc.psum_tensor([128, 512], F32) as p0,
                  nc.psum_tensor([128, 512], F32) as p1):
                for jo in range(ND):
                    for c in range(NKC):
                        ps = (p0, p1)[c % 2]
                        kk = (tag, "ps", c % 2)
                        for dc in range(ND):
                            P.op("t", lambda e, ps=ps, jo=jo, dc=dc, c=c:
                                 e.matmul(ps[:],
                                          w_sb[:, dc * D + jo * 128:
                                               dc * D + (jo + 1) * 128],
                                          xnT[:, dc * T + c * 512:
                                              dc * T + (c + 1) * 512],
                                          start=(dc == 0), stop=(dc == ND - 1)),
                                 [(tag, "w"), ("xnT" + tag, dc, c)], [kk])
                        if dest == "k":
                            P.op("v", lambda e, ps=ps, jo=jo, c=c: e.tensor_copy(
                                kT[:, jo * T + c * 512: jo * T + (c + 1) * 512],
                                ps[:]), [kk], [("kT" + tag, jo, c)])
                        else:
                            P.op("v", lambda e, ps=ps, c=c: e.tensor_copy(
                                qst[:, c * 512:(c + 1) * 512], ps[:]),
                                [kk], [(tag, "qst", c)])
                    if dest == "q":
                        P.dma("so", q_d[jo][:, :], qst[:],
                              reads=[(tag, "qst", c) for c in range(NKC)],
                              writes=[("qT_d" + tag, jo)])
                P.flush()

    def ph_attn(kT, v_sb, q_d, out_d, tag, causal, kept_fn):
        Ssb = AT([128, T], F32, 128 * KB)
        Ebf = [AT([128, T], BF16, 144 * KB), AT([128, T], BF16, 172 * KB)]
        dd_s = AT([128, 2048], F32, 152 * KB)
        qt = [AT([128, ND * 128], BF16, 160 * KB + i * 2 * KB) for i in range(2)]
        abf = [AT([128, 512], BF16, 164 * KB + i * KB) for i in range(2)]
        ats = [AT([128, 128], BF16, 166 * KB + i * KB) for i in range(2)]
        ats.append(AT([128, 128], BF16, 184 * KB))
        att = [AT([128, D], F32, 168 * KB), AT([128, D], F32, 180 * KB)]
        if not causal:
            P.dma("ld", dd_s[:], ddiag_d, writes=[("ddiag",)])
        with (nc.psum_tensor([128, 512], F32) as ps0,
              nc.psum_tensor([128, 512], F32) as ps1,
              nc.psum_tensor([128, 512], F32) as pav0,
              nc.psum_tensor([128, 512], F32) as pav1,
              nc.psum_tensor([128, 128], BF16) as ptA,
              nc.psum_tensor([128, 128], BF16) as ptB,
              nc.psum_tensor([128, 128], BF16) as ptC):
            for j in range(NT):
                par = j % 2
                b = 32 + 16 * par
                Eb = Ebf[par]
                kE = (tag, "E", par)
                atb = att[par]
                ext = (j + 1) * 128 if causal else T
                nch = (ext + 511) // 512
                qb = qt[par]
                kq = (tag, "qt", par)
                P.dma("ld", qb[:].rearrange("p (dc k) -> p dc k", dc=ND),
                      q_d[:, :, j * 128:(j + 1) * 128]
                      .rearrange("dc p k -> p dc k"),
                      reads=[("qT_d" + tag, dc) for dc in range(ND)],
                      writes=[kq])
                srd = []
                for c in range(nch):
                    ncols = min(512, ext - c * 512)
                    ps = (ps0, ps1)[c % 2]
                    kk = (tag, "ps", c % 2)
                    for dc in range(ND):
                        P.op("t", lambda e, ps=ps, qb=qb, dc=dc, c=c, ncols=ncols:
                             e.matmul(ps[:, 0:ncols],
                                      qb[:, dc * 128:(dc + 1) * 128],
                                      kT[:, dc * T + c * 512:
                                         dc * T + c * 512 + ncols],
                                      start=(dc == 0), stop=(dc == ND - 1)),
                             [kq, ("kT" + tag, dc, c)], [kk])
                    evsc = (1.0 / 32.0) if causal else 1.0
                    if causal and c == nch - 1:
                        if ncols > 128:
                            P.op("v", lambda e, ps=ps, c=c, ncols=ncols:
                                 e.tensor_scalar(
                                     Ssb[:, c * 512: c * 512 + ncols - 128],
                                     ps[:, 0:ncols - 128], evsc, None,
                                     ALU.mult),
                                 [kk], [(tag, "S", c)])
                            srd.append((tag, "S", c))
                        P.op("v", lambda e, ps=ps, c=c, ncols=ncols:
                             e.scalar_tensor_tensor(
                                 Ssb[:, c * 512 + ncols - 128: c * 512 + ncols],
                                 ps[:, ncols - 128:ncols], evsc, cmask_s[:],
                                 ALU.mult, ALU.add),
                             [kk, ("const", "cmask_s")], [(tag, "Sm", c)])
                        srd.append((tag, "Sm", c))
                    elif causal:
                        P.op("v", lambda e, ps=ps, c=c, ncols=ncols:
                             e.tensor_scalar(Ssb[:, c * 512:c * 512 + ncols],
                                             ps[:, 0:ncols], evsc, None,
                                             ALU.mult),
                             [kk], [(tag, "S", c)])
                        srd.append((tag, "S", c))
                    else:
                        P.op("v", lambda e, ps=ps, c=c, ncols=ncols:
                             e.tensor_copy(Ssb[:, c * 512:c * 512 + ncols],
                                           ps[:, 0:ncols]),
                             [kk], [(tag, "S", c)])
                        srd.append((tag, "S", c))
                P.op("v", lambda e, ext=ext, b=b: e.tensor_reduce(
                    stat[:, b + 9 - 9:b + 10 - 9], Ssb[:, 0:ext], AX.X,
                    ALU.max), srd, [S(b + 0)])
                P.op("v", lambda e, b=b: e.tensor_scalar(
                    stat[:, b + 1:b + 2], stat[:, b + 0:b + 1], -1.0, None,
                    ALU.mult), [S(b + 0)], [S(b + 1)])
                P.op("s", lambda e, ext=ext, b=b, Eb=Eb: e.activation(
                    Eb[:, 0:ext], Ssb[:, 0:ext], AF.Exp,
                    bias=stat[:, b + 1:b + 2],
                    accum_out=stat[:, b + 2:b + 3]),
                    srd + [S(b + 1)], [kE, S(b + 2)])
                P.op("v", lambda e, b=b: e.reciprocal(
                    stat[:, b + 3:b + 4], stat[:, b + 2:b + 3]),
                    [S(b + 2)], [S(b + 3)])
                kept = kept_fn(j)
                nsub_tot = sum((min(512, ext - c * 512) + 127) // 128
                               for (c, _, _) in kept)
                kav = [(tag, "pav", 0), (tag, "pav", 1)]
                sdone = 0
                for (c, typ, coef) in kept:
                    ncols = min(512, ext - c * 512)
                    ab = abf[c % 2]
                    kab = (tag, "abf", c % 2)
                    if causal:
                        P.op("v", lambda e, ab=ab, c=c, ncols=ncols, b=b, Eb=Eb:
                             e.tensor_scalar(ab[:, 0:ncols],
                                             Eb[:, c * 512:c * 512 + ncols],
                                             stat[:, b + 3:b + 4], None,
                                             ALU.mult),
                             [kE, S(b + 3)], [kab])
                    elif typ == "d":
                        P.op("v", lambda e, ab=ab, c=c, r=coef, b=b, Eb=Eb:
                             e.scalar_tensor_tensor(
                                 ab[:], Eb[:, c * 512:(c + 1) * 512],
                                 stat[:, b + 3:b + 4],
                                 dd_s[:, r * 512:(r + 1) * 512],
                                 ALU.mult, ALU.mult),
                             [kE, S(b + 3), ("ddiag",)], [kab])
                    else:
                        gv = gv_pos if typ == "l" else gv_neg
                        gb = gneg_s if typ == "l" else gpos_s
                        gbk = ("const", "gneg_s" if typ == "l" else "gpos_s")
                        P.op("v", lambda e, gv=gv, coef=coef, b=b:
                             e.tensor_scalar(
                                 stat[:, b + 4:b + 5], gv[:],
                                 stat[:, b + 3:b + 4], coef,
                                 ALU.mult, ALU.mult), [S(b + 3)], [S(b + 4)])
                        P.op("v", lambda e, ab=ab, c=c, gb=gb, b=b, Eb=Eb:
                             e.scalar_tensor_tensor(
                                 ab[:], Eb[:, c * 512:(c + 1) * 512],
                                 stat[:, b + 4:b + 5], gb[:], ALU.mult,
                                 ALU.mult),
                             [kE, S(b + 4), gbk], [kab])
                    nsub = (ncols + 127) // 128
                    for sub in range(nsub):
                        s_k = c * 4 + sub
                        pt = (ptA, ptB, ptC)[s_k % 3]
                        kpt = (tag, "pt", s_k % 3)
                        at = ats[s_k % 3]
                        kat = (tag, "at", s_k % 3)
                        P.op("t", lambda e, pt=pt, ab=ab, sub=sub: e.transpose(
                            pt[:], ab[:, sub * 128:(sub + 1) * 128],
                            ident_bf[:]), [kab, KID], [kpt])
                        P.op("v", lambda e, pt=pt, at=at: e.tensor_copy(
                            at[:], pt[:]), [kpt], [kat])
                        sdone += 1
                        last = (sdone == nsub_tot)
                        for n in range(2):
                            P.op("t", lambda e, at=at, s_k=s_k, n=n,
                                 first=(sdone == 1), last=last,
                                 pv=(pav0, pav1)[n]:
                                 e.matmul(pv[:], at[:],
                                          v_sb[:, s_k * D + n * 512:
                                               s_k * D + (n + 1) * 512],
                                          start=first, stop=last),
                                 [kat, ("v" + tag, s_k)], [kav[n]])
                P.op("v", lambda e, atb=atb: e.tensor_copy(
                    atb[:, 0:512], pav0[:]), [kav[0]], [(tag, "att", par, 0)])
                P.op("v", lambda e, atb=atb: e.tensor_copy(
                    atb[:, 512:1024], pav1[:]), [kav[1]], [(tag, "att", par, 1)])
                P.dma("so", out_d[j * 128:(j + 1) * 128, :], atb[:],
                      reads=[(tag, "att", par, 0), (tag, "att", par, 1)],
                      writes=[("attn_d" + tag, j)])
            P.flush()


    def dyn_load(e, dst, src_d, rg):
        return e.indirect_dma_start(
            dst, None, src_d,
            bass.IndirectOffsetOnAxis(ap=idx_sb[:, rg:rg + 1], axis=0))

    def ph_attn_own(kT, v_sb, qtm_d, out_d, tag):
        Ssb = AT([128, T], F32, 128 * KB)
        Ebf = [AT([128, T], BF16, 144 * KB), AT([128, T], BF16, 172 * KB)]
        qrow = [AT([128, D], BF16, 152 * KB + i * 2 * KB) for i in range(2)]
        qt = [AT([128, ND * 128], BF16, 160 * KB + i * 2 * KB) for i in range(2)]
        abf = [AT([128, 512], BF16, 164 * KB + i * KB) for i in range(2)]
        ats = [AT([128, 128], BF16, 166 * KB + i * KB) for i in range(2)]
        ats.append(AT([128, 128], BF16, 184 * KB))
        att = [AT([128, D], F32, 168 * KB), AT([128, D], F32, 180 * KB)]
        with (nc.psum_tensor([128, 512], F32) as ps0,
              nc.psum_tensor([128, 512], F32) as ps1,
              nc.psum_tensor([128, 512], F32) as pav0,
              nc.psum_tensor([128, 512], F32) as pav1,
              nc.psum_tensor([128, 128], BF16) as ptA,
              nc.psum_tensor([128, 128], BF16) as ptB,
              nc.psum_tensor([128, 128], BF16) as ptC):
            for rg in range(NHT):
                par = rg % 2
                b = 32 + 16 * par
                Eb = Ebf[par]
                kE = (tag, "E", par)
                atb = att[par]
                ext = (2 * rg + 2) * 128
                nch = (ext + 511) // 512
                qr = qrow[par]
                kqr = (tag, "qr", par)
                qb = qt[par]
                kq = (tag, "qt", par)
                P.op("gp", lambda e, qr=qr, rg=rg: dyn_load(e, qr[:], qtm_d, rg),
                     reads=[("idx",)] + [("q1tm", jj) for jj in range(NT)],
                     writes=[kqr])
                for dc in range(ND):
                    pt = (ptA, ptB)[dc % 2]
                    kpt = (tag, "pt", dc % 2)
                    P.op("t", lambda e, pt=pt, qr=qr, dc=dc: e.transpose(
                        pt[:], qr[:, dc * 128:(dc + 1) * 128], ident_bf[:]),
                        [kqr, KID], [kpt])
                    P.op("v", lambda e, pt=pt, qb=qb, dc=dc: e.tensor_copy(
                        qb[:, dc * 128:(dc + 1) * 128], pt[:]), [kpt], [kq])
                srd = []
                evsc = 1.0 / 32.0
                for c in range(nch):
                    ncols = min(512, ext - c * 512)
                    ps = (ps0, ps1)[c % 2]
                    kk = (tag, "ps", c % 2)
                    for dc in range(ND):
                        P.op("t", lambda e, ps=ps, qb=qb, dc=dc, c=c,
                             ncols=ncols:
                             e.matmul(ps[:, 0:ncols],
                                      qb[:, dc * 128:(dc + 1) * 128],
                                      kT[:, dc * T + c * 512:
                                         dc * T + c * 512 + ncols],
                                      start=(dc == 0), stop=(dc == ND - 1)),
                             [kq, ("kT" + tag, dc, c)], [kk])
                    if c == nch - 1:
                        if ncols > 256:
                            P.op("v", lambda e, ps=ps, c=c, ncols=ncols:
                                 e.tensor_scalar(
                                     Ssb[:, c * 512: c * 512 + ncols - 256],
                                     ps[:, 0:ncols - 256], evsc, None,
                                     ALU.mult),
                                 [kk], [(tag, "S", c)])
                            srd.append((tag, "S", c))
                        P.op("v", lambda e, ps=ps, c=c, ncols=ncols:
                             e.scalar_tensor_tensor(
                                 Ssb[:, c * 512 + ncols - 256: c * 512 + ncols],
                                 ps[:, ncols - 256:ncols], evsc, cmask2_s[:],
                                 ALU.mult, ALU.add),
                             [kk, ("const", "cmask2_s")], [(tag, "Sm", c)])
                        srd.append((tag, "Sm", c))
                    else:
                        P.op("v", lambda e, ps=ps, c=c, ncols=ncols:
                             e.tensor_scalar(Ssb[:, c * 512:c * 512 + ncols],
                                             ps[:, 0:ncols], evsc, None,
                                             ALU.mult),
                             [kk], [(tag, "S", c)])
                        srd.append((tag, "S", c))
                P.op("v", lambda e, ext=ext, b=b: e.tensor_reduce(
                    stat[:, b + 0:b + 1], Ssb[:, 0:ext], AX.X,
                    ALU.max), srd, [S(b + 0)])
                P.op("v", lambda e, b=b: e.tensor_scalar(
                    stat[:, b + 1:b + 2], stat[:, b + 0:b + 1], -1.0, None,
                    ALU.mult), [S(b + 0)], [S(b + 1)])
                P.op("s", lambda e, ext=ext, b=b, Eb=Eb: e.activation(
                    Eb[:, 0:ext], Ssb[:, 0:ext], AF.Exp,
                    bias=stat[:, b + 1:b + 2],
                    accum_out=stat[:, b + 2:b + 3]),
                    srd + [S(b + 1)], [kE, S(b + 2)])
                P.op("v", lambda e, b=b: e.reciprocal(
                    stat[:, b + 3:b + 4], stat[:, b + 2:b + 3]),
                    [S(b + 2)], [S(b + 3)])
                nsub_tot = ext // 128
                kav = [(tag, "pav", 0), (tag, "pav", 1)]
                sdone = 0
                for c in range(nch):
                    ncols = min(512, ext - c * 512)
                    ab = abf[c % 2]
                    kab = (tag, "abf", c % 2)
                    P.op("v", lambda e, ab=ab, c=c, ncols=ncols, b=b, Eb=Eb:
                         e.tensor_scalar(ab[:, 0:ncols],
                                         Eb[:, c * 512:c * 512 + ncols],
                                         stat[:, b + 3:b + 4], None,
                                         ALU.mult),
                         [kE, S(b + 3)], [kab])
                    for sub in range(ncols // 128):
                        s_k = c * 4 + sub
                        pt = (ptA, ptB, ptC)[s_k % 3]
                        kpt = (tag, "pt", s_k % 3)
                        at = ats[s_k % 3]
                        kat = (tag, "at", s_k % 3)
                        P.op("t", lambda e, pt=pt, ab=ab, sub=sub: e.transpose(
                            pt[:], ab[:, sub * 128:(sub + 1) * 128],
                            ident_bf[:]), [kab, KID], [kpt])
                        P.op("v", lambda e, pt=pt, at=at: e.tensor_copy(
                            at[:], pt[:]), [kpt], [kat])
                        sdone += 1
                        last = (sdone == nsub_tot)
                        for n in range(2):
                            P.op("t", lambda e, at=at, s_k=s_k, n=n,
                                 first=(sdone == 1), last=last,
                                 pv=(pav0, pav1)[n]:
                                 e.matmul(pv[:], at[:],
                                          v_sb[:, s_k * D + n * 512:
                                               s_k * D + (n + 1) * 512],
                                          start=first, stop=last),
                                 [kat, ("v" + tag, s_k)], [kav[n]])
                P.op("v", lambda e, atb=atb: e.tensor_copy(
                    atb[:, 0:512], pav0[:]), [kav[0]], [(tag, "att", par, 0)])
                P.op("v", lambda e, atb=atb: e.tensor_copy(
                    atb[:, 512:1024], pav1[:]), [kav[1]], [(tag, "att", par, 1)])
                P.dma("so", out_d[rg * 128:(rg + 1) * 128, :], atb[:],
                      reads=[(tag, "att", par, 0), (tag, "att", par, 1)],
                      writes=[("attn_d" + tag, rg)])
            P.flush()

    xnT_A = AT([128, ND * T], BF16, 0)
    kT_A = AT([128, ND * T], BF16, 64 * KB)

    wbg = nc.alloc_semaphore("wbg")
    for e_ in range(8):
        P.op("gp", lambda e, e_=e_: e.dma_start(out=w1b_d[e_], in_=W1_d[e_]),
             bgsem=wbg)
        P.op("gp", lambda e, e_=e_: e.dma_start(out=w2b_d[e_], in_=W2_d[e_]),
             bgsem=wbg)
    ph_ln(x0, xnT_A, "0")
    ph_qk_xpos(wk0d, cosk, sink, xnT_A, "0k", out_kT=kT_A)
    ph_qk_xpos(wq0d, cosq, sinq, xnT_A, "0q", out_q_d=qT0_d)

    gw_sb = AT([128, ND * 8], BF16, 128 * KB)
    Lacc = AT([8, T], F32, 130 * KB)
    gb_sb = AT([8, 1], F32, 146 * KB)
    ls8 = AT([8, 2], F32, 147 * KB)
    P.dma("gp", gw_sb[:].rearrange("p (dc e) -> p dc e", dc=ND),
          gwT_d.rearrange("(dc p) e -> p dc e", p=128), writes=[("gw",)])
    P.dma("ld", gb_sb[:], gbT_d, writes=[("gb",)])
    with nc.psum_tensor([8, 512], F32) as pg:
        for c in range(NKC):
            for dc in range(ND):
                P.op("t", lambda e, dc=dc, c=c: e.matmul(
                    pg[:], gw_sb[:, dc * 8:(dc + 1) * 8],
                    xnT_A[:, dc * T + c * 512: dc * T + (c + 1) * 512],
                    start=(dc == 0), stop=(dc == ND - 1)),
                    [("gw",), ("xnT0", dc, c)], [("pg",)])
            P.op("v", lambda e, c=c: e.tensor_copy(
                Lacc[:, c * 512:(c + 1) * 512], pg[:]),
                [("pg",)], [("Lacc", c)])
        P.op("v", lambda e: e.tensor_reduce(
            ls8[:, 0:1], Lacc[:], AX.X, ALU.add),
            [("Lacc", c) for c in range(NKC)], [("Ls",)])
        P.op("v", lambda e: e.scalar_tensor_tensor(
            ls8[:, 1:2], ls8[:, 0:1], 1.0 / T, gb_sb[:], ALU.mult, ALU.add),
            [("Ls",), ("gb",)], [("Lg",)])
        P.dma("gp", row8_d, ls8[:, 1:2], reads=[("Lg",)], writes=[("r8d",)])
        lrow = st8[0:1, 0:8]
        P.dma("gp", lrow, row8_d.rearrange("p one -> one p"),
              reads=[("r8d",)], writes=[("lrow",)])
        r8 = st8[0:1, 8:16]
        g8 = st8[0:1, 16:24]
        e8 = st8[0:1, 24:32]
        m1 = st8[0:1, 32:33]; nm1 = st8[0:1, 33:34]; m2 = st8[0:1, 34:35]
        zz = st8[0:1, 35:36]; rz8 = st8[0:1, 36:37]
        inv8 = st8[0:1, 37:45]
        keep2 = st8[0:1, 45:53]
        tmp8 = st8[0:1, 53:61]
        sc4 = st8[0:1, 61:65]
        P.op("v", lambda e: e.tensor_reduce(m1, lrow, AX.X, ALU.max),
             [("lrow",)], [("m1",)])
        P.op("v", lambda e: e.tensor_scalar(inv8, lrow, m1, None, ALU.is_ge),
             [("lrow",), ("m1",)], [("inv",)])
        P.op("v", lambda e: e.scalar_tensor_tensor(
            r8, inv8, NEG, lrow, ALU.mult, ALU.add),
            [("inv",), ("lrow",)], [("msk",)])
        P.op("v", lambda e: e.tensor_reduce(m2, r8, AX.X, ALU.max),
             [("msk",)], [("m2",)])
        P.op("v", lambda e: e.tensor_scalar(g8, lrow, m2, None, ALU.is_ge),
             [("lrow",), ("m2",)], [("keep",)])
        P.op("v", lambda e: e.tensor_scalar(nm1, m1, -1.0, None, ALU.mult),
             [("m1",)], [("nm1",)])
        P.op("s", lambda e: e.activation(e8, lrow, AF.Exp, bias=nm1),
             [("lrow",), ("nm1",)], [("e8",)])
        P.op("v", lambda e: e.tensor_tensor(e8, e8, g8, ALU.mult),
             [("e8",), ("keep",)], [("e8",)])
        P.op("v", lambda e: e.tensor_reduce(zz, e8, AX.X, ALU.add),
             [("e8",)], [("zz",)])
        P.op("v", lambda e: e.reciprocal(rz8, zz), [("zz",)], [("rz8",)])
        P.op("v", lambda e: e.tensor_scalar(e8, e8, rz8, None, ALU.mult),
             [("e8",), ("rz8",)], [("g8f",)])
        P.op("v", lambda e: e.tensor_tensor(keep2, g8, inv8, ALU.subtract),
             [("keep",), ("inv",)], [("keep2",)])
        P.op("v", lambda e: e.tensor_tensor(tmp8, eids_s[:], inv8, ALU.mult),
             [("eids",), ("inv",)], [("t8",)])
        P.op("v", lambda e: e.tensor_reduce(sc4[0:1, 0:1], tmp8, AX.X, ALU.add),
             [("t8",)], [("sea",)])
        P.op("v", lambda e: e.tensor_tensor(tmp8, eids_s[:], keep2, ALU.mult),
             [("eids",), ("keep2",)], [("t8",)])
        P.op("v", lambda e: e.tensor_reduce(sc4[0:1, 1:2], tmp8, AX.X, ALU.add),
             [("t8",)], [("seb",)])
        P.op("v", lambda e: e.tensor_tensor(tmp8, e8, inv8, ALU.mult),
             [("g8f",), ("inv",)], [("t8",)])
        P.op("v", lambda e: e.tensor_reduce(sc4[0:1, 2:3], tmp8, AX.X, ALU.add),
             [("t8",)], [("sga",)])
        P.op("v", lambda e: e.tensor_tensor(tmp8, e8, keep2, ALU.mult),
             [("g8f",), ("keep2",)], [("t8",)])
        P.op("v", lambda e: e.tensor_reduce(sc4[0:1, 3:4], tmp8, AX.X, ALU.add),
             [("t8",)], [("sgb",)])
        P.dma("gp", se2_d, sc4[0:1, 0:2], reads=[("sea",), ("seb",)],
              writes=[("se2d",)])
        P.dma("gp", g2_d, sc4[0:1, 2:4], reads=[("sga",), ("sgb",)],
              writes=[("g2d",)])
        P.dma("gp", sef_bc[:], bass.AP(se2_d.tensor, 0, [[0, 128], [1, 2]]),
              reads=[("se2d",)], writes=[("sefbc",)])
        P.dma("gp", gbc2[:], bass.AP(g2_d.tensor, 0, [[0, 128], [1, 2]]),
              reads=[("g2d",)], writes=[("gbc2",)])
        P.dma("gp", se21[:], se2_d.rearrange("one two -> two one"),
              reads=[("se2d",)], writes=[("se21",)])
        P.op("v", lambda e: e.tensor_copy(idx1e[:], se21[:]),
             [("se21",)], [("idx1",)])
        for sel in range(2):
            P.op("v", lambda e, sel=sel: e.tensor_scalar(
                stat[:, 14:15], sef_bc[:, sel:sel + 1], 1024.0, None, ALU.mult),
                [("sefbc",)], [S(14)])
            for dc in range(ND):
                P.op("v", lambda e, dc=dc: e.scalar_tensor_tensor(
                    stat[:, 15:16], iop_s[:], float(dc * 128), stat[:, 14:15],
                    ALU.add, ALU.add), [("iop",), S(14)], [S(15)])
                P.op("v", lambda e, sel=sel, dc=dc: e.tensor_copy(
                    idxg[:, sel * 8 + dc: sel * 8 + dc + 1], stat[:, 15:16]),
                    [S(15)], [("idxg", sel, dc)])
        P.flush()

    ph_spill_xnT(xnT_A, xnT0_d, "0")
    v_A = AT([128, NT * D], BF16, 0)
    ph_v(wv0, xnT0_d, v_A, "0", 152 * KB)

    def kept0(j):
        q0 = 128 * j
        out = []
        for c in range(NKC):
            k_lo, k_hi = 512 * c, 512 * c + 511
            if k_lo <= q0 + 127 and k_hi >= q0:
                mind = 0
            else:
                mind = min(abs(q0 - k_hi), abs(k_lo - (q0 + 127)))
            if mind >= WIN:
                continue
            if k_hi < q0:
                out.append((c, "l", float(np.float32(GAMMA ** float(q0 - k_lo)))))
            elif k_lo > q0 + 127:
                out.append((c, "r", float(np.float32(GAMMA ** float(k_lo - q0)))))
            else:
                out.append((c, "d", (q0 % 512) // 128))
        return out

    ph_attn(kT_A, v_A, qT0_d, attn0_d, "0", causal=False, kept_fn=kept0)

    ewf_sb = AT([128, ND * 1024], F32, 0)
    ew_sb = AT([128, 2 * ND * 1024], BF16, 32 * KB)
    ebf_sb = AT([2, 1024], F32, 128 * KB)
    ebr_sb = AT([1, 2 * 1024], F32, 132 * KB)
    eb_sb = AT([1, 2 * 1024], BF16, 140 * KB)
    xcD = [AT([128, ND * 128], BF16, 144 * KB + i * 2 * KB) for i in range(2)]
    accD = AT([128, D], F32, 148 * KB)
    gtD = AT([128, D], F32, 152 * KB)
    x0t = [AT([128, D], F32, 156 * KB + i * 4 * KB) for i in range(2)]
    a0t = [AT([128, D], F32, 164 * KB + i * 4 * KB) for i in range(2)]
    for sel in range(2):
        for dc in range(ND):
            P.op("gp", lambda e, sel=sel, dc=dc: e.indirect_dma_start(
                ewf_sb[:, dc * 1024:(dc + 1) * 1024], None, ew_g,
                bass.IndirectOffsetOnAxis(
                    ap=idxg[:, sel * 8 + dc: sel * 8 + dc + 1], axis=0)),
                reads=[("idxg", sel, dc)], writes=[("ewf", dc)])
        P.op("v", lambda e, sel=sel: e.tensor_copy(
            ew_sb[:, sel * ND * 1024:(sel + 1) * ND * 1024], ewf_sb[:]),
            [("ewf", dc) for dc in range(ND)], [("ew", sel)])
    P.op("gp", lambda e: e.indirect_dma_start(
        ebf_sb[:], None, eb0,
        bass.IndirectOffsetOnAxis(ap=idx1e[:], axis=0)),
        reads=[("idx1",)], writes=[("ebf", 0)])
    P.dma("gp", eb2_d, ebf_sb[:], reads=[("ebf", 0)], writes=[("eb2d",)])
    P.dma("ld", ebr_sb[:],
          bass.AP(eb2_d.tensor, 0, [[2048, 1], [1, 2048]]),
          reads=[("eb2d",)], writes=[("ebr",)])
    P.op("v", lambda e: e.tensor_copy(eb_sb[:], ebr_sb[:]),
         [("ebr",)], [("eb",)])
    with (nc.psum_tensor([128, 1024], F32) as pe0,
          nc.psum_tensor([128, 1024], F32) as pe1):
        for j in range(NT):
            xb = xcD[j % 2]
            kx = ("Dxc", j % 2)
            P.dma("ld", xb[:].rearrange("p (dc k) -> p dc k", dc=ND),
                  xnT0_d[:, :, j * 128:(j + 1) * 128]
                  .rearrange("dc p k -> p dc k"),
                  reads=[("xnTd0", dc) for dc in range(ND)], writes=[kx])
            P.dma("ld", x0t[j % 2][:], x0[j * 128:(j + 1) * 128, :],
                  writes=[("Dx0", j % 2)])
            P.dma("ld", a0t[j % 2][:], attn0_d[j * 128:(j + 1) * 128, :],
                  reads=[("attn_d0", j)], writes=[("Da0", j % 2)])
            for sel in range(2):
                pe = (pe0, pe1)[sel]
                kp = ("Dpe", sel)
                for n in range(2):
                    P.op("t", lambda e, pe=pe, sel=sel, n=n: e.matmul(
                        pe[:, n * 512:(n + 1) * 512], ones_bf[:],
                        eb_sb[0:1, sel * 1024 + n * 512:
                              sel * 1024 + (n + 1) * 512],
                        start=True, stop=False),
                        [("eb",), KONE], [kp])
                for dc in range(ND):
                    for n in range(2):
                        P.op("t", lambda e, pe=pe, xb=xb, sel=sel, dc=dc, n=n:
                             e.matmul(pe[:, n * 512:(n + 1) * 512],
                                      xb[:, dc * 128:(dc + 1) * 128],
                                      ew_sb[:, sel * ND * 1024 + dc * 1024 +
                                            n * 512:
                                            sel * ND * 1024 + dc * 1024 +
                                            (n + 1) * 512],
                                      start=False, stop=(dc == ND - 1)),
                             [("ew", sel), kx], [kp])
                P.op("s", lambda e, pe=pe: e.activation(gtD[:], pe[:], AF.Gelu),
                     [kp], [("Dgt",)])
                if sel == 0:
                    P.op("v", lambda e: e.tensor_scalar(
                        accD[:], gtD[:], gbc2[:, 0:1], None, ALU.mult),
                        [("Dgt",), ("gbc2",)], [("Dacc",)])
                else:
                    P.op("v", lambda e: e.scalar_tensor_tensor(
                        accD[:], gtD[:], gbc2[:, 1:2], accD[:],
                        ALU.mult, ALU.add),
                        [("Dgt",), ("gbc2",), ("Dacc",)], [("Dacc",)])
            P.op("v", lambda e, j=j: e.tensor_tensor(
                accD[:], accD[:], x0t[j % 2][:], ALU.add),
                [("Dacc",), ("Dx0", j % 2)], [("Dacc",)])
            P.op("v", lambda e, j=j: e.tensor_tensor(
                x0t[j % 2][:], accD[:], a0t[j % 2][:], ALU.add),
                [("Dacc",), ("Da0", j % 2)], [("Dx0", j % 2)])
            P.dma("so", x1_d[j * 128:(j + 1) * 128, :], x0t[j % 2][:],
                  reads=[("Dx0", j % 2)], writes=[("x1d", j)])
        P.flush()

    xnT_E = AT([128, ND * T], BF16, 0)
    kT_E = AT([128, ND * T], BF16, 64 * KB)
    ph_ln(x1_d, xnT_E, "1", src_key=lambda j: ("x1d", j), xnf_d=xn1f_d)
    ph_plain_kq(wk1, wq1, xnT_E, kT_E, None, "1")
    ph_spill_xnT(xnT_E, xn1T_d, "1")
    v_E = AT([128, NT * D], BF16, 0)
    ph_v(wv1, xn1T_d, v_E, "1", 160 * KB, wq_d=wq1, qtm_d=q1tm_d)

    ph_attn_own(kT_E, v_E, q1tm_d, attn1_d, "1")


    wTp = AT([128, NHT * 128], BF16, 0)
    gmb = AT([128, D], F32, 96 * KB)
    bmb = AT([128, D], F32, 100 * KB)
    x1o = [AT([128, D], F32, 104 * KB + i * 4 * KB) for i in range(2)]
    xn2n = [AT([128, D], F32, 112 * KB), AT([128, D], F32, 133 * KB)]
    xn2b = [AT([128, D], BF16, 116 * KB), AT([128, D], BF16, 137 * KB)]
    x2T = [AT([128, ND * 128], BF16, 118 * KB),
           AT([128, ND * 128], BF16, 139 * KB)]
    wr_sb = AT([128, ND * 8], BF16, 120 * KB)
    br_sb = AT([1, 8], BF16, 121 * KB)
    w_all = AT([128, NHT * 8], F32, 122 * KB)
    probs = [AT([128, 8], F32, 130 * KB), AT([128, 8], F32, 141 * KB)]
    g0scr = [AT([128, D], F32, 124 * KB), AT([128, D], F32, 142 * KB)]
    tb8 = [AT([128, 16], F32, 131 * KB), AT([128, 16], F32, 146 * KB)]
    P.dma("ld", gmb[:], gamma_d, writes=[("gmb",)])
    P.dma("ld", bmb[:], beta_d, writes=[("bmb",)])
    P.dma("gp", wr_sb[:].rearrange("p (dc e) -> p dc e", dc=ND),
          Wr_d.rearrange("(dc p) e -> p dc e", p=128), writes=[("wr",)])
    P.dma("gp", br_sb[:], br_d, writes=[("brs",)])
    P.op("v", lambda e: e.memset(wTp[:], 0.0), writes=[("wTpz",)])
    with (nc.psum_tensor([128, 128], BF16) as pt0,
          nc.psum_tensor([128, 128], BF16) as pt1,
          nc.psum_tensor([128, 8], F32) as pr_,
          nc.psum_tensor([128, 128], F32) as pw_):
        for r in range(NHT):
            par = r % 2
            b = 16 * par
            rb = 32 + 16 * par
            xb = x1o[par]
            xn2np, xn2bp, x2Tp = xn2n[par], xn2b[par], x2T[par]
            probp, g0scrp, tb8p = probs[par], g0scr[par], tb8[par]
            kx = ("G0x", par)
            P.op("gp", lambda e, xb=xb, r=r: dyn_load(e, xb[:], xn1f_d, r),
                reads=[("idx",)] + [("xnf_d", jj) for jj in range(NT)],
                writes=[kx])
            P.op("v", lambda e, xb=xb, b=b, xn2np=xn2np, g0scrp=g0scrp: e.tensor_reduce(
                stat[:, b + 0:b + 1], xb[:], AX.X, ALU.add), [kx], [S(b + 0)])
            P.op("s", lambda e, xb=xb, b=b, xn2np=xn2np, g0scrp=g0scrp: e.activation(
                g0scrp[:], xb[:], AF.Square, accum_out=stat[:, b + 1:b + 2]),
                [kx], [("G0scr", par), S(b + 1)])
            P.op("v", lambda e, b=b, rb=rb, xn2np=xn2np, xn2bp=xn2bp, x2Tp=x2Tp, probp=probp, g0scrp=g0scrp, tb8p=tb8p: e.tensor_scalar(
                stat[:, b + 2:b + 3], stat[:, b + 0:b + 1], 1.0 / D, None, ALU.mult),
                [S(b + 0)], [S(b + 2)])
            P.op("v", lambda e, b=b, rb=rb, xn2np=xn2np, xn2bp=xn2bp, x2Tp=x2Tp, probp=probp, g0scrp=g0scrp, tb8p=tb8p: e.tensor_tensor(
                stat[:, b + 3:b + 4], stat[:, b + 2:b + 3], stat[:, b + 2:b + 3], ALU.mult),
                [S(b + 2)], [S(b + 3)])
            P.op("v", lambda e, b=b, rb=rb, xn2np=xn2np, xn2bp=xn2bp, x2Tp=x2Tp, probp=probp, g0scrp=g0scrp, tb8p=tb8p: e.tensor_scalar(
                stat[:, b + 4:b + 5], stat[:, b + 1:b + 2], 1.0 / D, None, ALU.mult),
                [S(b + 1)], [S(b + 4)])
            P.op("v", lambda e, b=b, rb=rb, xn2np=xn2np, xn2bp=xn2bp, x2Tp=x2Tp, probp=probp, g0scrp=g0scrp, tb8p=tb8p: e.tensor_tensor(
                stat[:, b + 5:b + 6], stat[:, b + 4:b + 5], stat[:, b + 3:b + 4], ALU.subtract),
                [S(b + 3), S(b + 4)], [S(b + 5)])
            P.op("v", lambda e, b=b, rb=rb, xn2np=xn2np, xn2bp=xn2bp, x2Tp=x2Tp, probp=probp, g0scrp=g0scrp, tb8p=tb8p: e.tensor_scalar(
                stat[:, b + 5:b + 6], stat[:, b + 5:b + 6], 1e-5, None, ALU.add),
                [S(b + 5)], [S(b + 5)])
            P.op("s", lambda e, b=b, rb=rb, xn2np=xn2np, xn2bp=xn2bp, x2Tp=x2Tp, probp=probp, g0scrp=g0scrp, tb8p=tb8p: e.activation(
                stat[:, b + 6:b + 7], stat[:, b + 5:b + 6], AF.Sqrt), [S(b + 5)], [S(b + 6)])
            P.op("v", lambda e, b=b, rb=rb, xn2np=xn2np, xn2bp=xn2bp, x2Tp=x2Tp, probp=probp, g0scrp=g0scrp, tb8p=tb8p: e.reciprocal(stat[:, b + 7:b + 8], stat[:, b + 6:b + 7]),
                 [S(b + 6)], [S(b + 7)])
            P.op("v", lambda e, b=b, rb=rb, xn2np=xn2np, xn2bp=xn2bp, x2Tp=x2Tp, probp=probp, g0scrp=g0scrp, tb8p=tb8p: e.scalar_tensor_tensor(
                stat[:, b + 8:b + 9], stat[:, b + 2:b + 3], -1.0, stat[:, b + 7:b + 8],
                ALU.mult, ALU.mult), [S(b + 2), S(b + 7)], [S(b + 8)])
            P.op("s", lambda e, xb=xb, b=b, xn2np=xn2np, g0scrp=g0scrp: e.activation(
                xn2np[:], xb[:], AF.Identity, bias=stat[:, b + 8:b + 9], scale=stat[:, b + 7:b + 8]),
                [kx, S(b + 7), S(b + 8)], [("G0n", par)])
            P.op("v", lambda e, b=b, rb=rb, xn2np=xn2np, xn2bp=xn2bp, x2Tp=x2Tp, probp=probp, g0scrp=g0scrp, tb8p=tb8p: e.tensor_tensor(
                xn2np[:], xn2np[:], gmb[:], ALU.mult),
                [("G0n", par), ("gmb",)], [("G0n", par)])
            P.op("v", lambda e, b=b, rb=rb, xn2np=xn2np, xn2bp=xn2bp, x2Tp=x2Tp, probp=probp, g0scrp=g0scrp, tb8p=tb8p: e.tensor_tensor(
                xn2bp[:], xn2np[:], bmb[:], ALU.add),
                [("G0n", par), ("bmb",)], [("G0b", par)])
            for dc in range(ND):
                pp = (pt0, pt1)[dc % 2]
                kp = ("G0pt", dc % 2)
                P.op("t", lambda e, pp=pp, dc=dc, xn2bp=xn2bp: e.transpose(
                    pp[:], xn2bp[:, dc * 128:(dc + 1) * 128], ident_bf[:]),
                    [("G0b", par), KID], [kp])
                P.op("v", lambda e, pp=pp, dc=dc, x2Tp=x2Tp: e.tensor_copy(
                    x2Tp[:, dc * 128:(dc + 1) * 128], pp[:]),
                    [kp], [("G0T", par, dc)])
            for dc in range(ND):
                P.dma("so", xn2T_d[dc][:, r * 128:(r + 1) * 128],
                      x2Tp[:, dc * 128:(dc + 1) * 128],
                      reads=[("G0T", par, dc)], writes=[("x2Td", r, dc)])
            P.op("t", lambda e, b=b, rb=rb, xn2np=xn2np, xn2bp=xn2bp, x2Tp=x2Tp, probp=probp, g0scrp=g0scrp, tb8p=tb8p: e.matmul(pr_[:], ones_bf[:], br_sb[:],
                                         start=True, stop=False),
                 [("brs",), KONE], [("G0pr",)])
            for dc in range(ND):
                P.op("t", lambda e, dc=dc, x2Tp=x2Tp: e.matmul(
                    pr_[:], x2Tp[:, dc * 128:(dc + 1) * 128],
                    wr_sb[:, dc * 8:(dc + 1) * 8],
                    start=False, stop=(dc == ND - 1)),
                    [("wr",), ("G0T", par, dc)], [("G0pr",)])
            P.op("v", lambda e, b=b, rb=rb, xn2np=xn2np, xn2bp=xn2bp, x2Tp=x2Tp, probp=probp, g0scrp=g0scrp, tb8p=tb8p: e.tensor_copy(probp[:], pr_[:]),
                 [("G0pr",)], [("G0p", par)])
            P.op("v", lambda e, b=b, rb=rb, xn2np=xn2np, xn2bp=xn2bp, x2Tp=x2Tp, probp=probp, g0scrp=g0scrp, tb8p=tb8p: e.tensor_reduce(
                stat[:, rb + 0:rb + 1], probp[:], AX.X, ALU.max), [("G0p", par)], [S(9)])
            P.op("v", lambda e, b=b, rb=rb, xn2np=xn2np, xn2bp=xn2bp, x2Tp=x2Tp, probp=probp, g0scrp=g0scrp, tb8p=tb8p: e.tensor_scalar(
                stat[:, rb + 1:rb + 2], stat[:, rb + 0:rb + 1], -1.0, None, ALU.mult),
                [S(9)], [S(10)])
            P.op("s", lambda e, b=b, rb=rb, xn2np=xn2np, xn2bp=xn2bp, x2Tp=x2Tp, probp=probp, g0scrp=g0scrp, tb8p=tb8p: e.activation(
                probp[:], probp[:], AF.Exp, bias=stat[:, rb + 1:rb + 2],
                accum_out=stat[:, rb + 2:rb + 3]),
                [("G0p", par), S(10)], [("G0p", par), S(11)])
            P.op("v", lambda e, b=b, rb=rb, xn2np=xn2np, xn2bp=xn2bp, x2Tp=x2Tp, probp=probp, g0scrp=g0scrp, tb8p=tb8p: e.reciprocal(stat[:, rb + 3:rb + 4], stat[:, rb + 2:rb + 3]),
                 [S(11)], [S(12)])
            P.op("v", lambda e, b=b, rb=rb, xn2np=xn2np, xn2bp=xn2bp, x2Tp=x2Tp, probp=probp, g0scrp=g0scrp, tb8p=tb8p: e.tensor_scalar(
                probp[:], probp[:], stat[:, rb + 3:rb + 4], None, ALU.mult),
                [("G0p", par), S(12)], [("G0p", par)])
            P.op("v", lambda e, b=b, rb=rb, xn2np=xn2np, xn2bp=xn2bp, x2Tp=x2Tp, probp=probp, g0scrp=g0scrp, tb8p=tb8p: e.tensor_reduce(
                stat[:, rb + 0:rb + 1], probp[:], AX.X, ALU.max), [("G0p", par)], [S(9)])
            P.op("v", lambda e, b=b, rb=rb, xn2np=xn2np, xn2bp=xn2bp, x2Tp=x2Tp, probp=probp, g0scrp=g0scrp, tb8p=tb8p: e.tensor_scalar(
                tb8p[:, 0:8], probp[:], stat[:, rb + 0:rb + 1], None, ALU.is_ge),
                [("G0p", par), S(9)], [("G0i", par)])
            P.op("v", lambda e, b=b, rb=rb, xn2np=xn2np, xn2bp=xn2bp, x2Tp=x2Tp, probp=probp, g0scrp=g0scrp, tb8p=tb8p: e.scalar_tensor_tensor(
                tb8p[:, 8:16], tb8p[:, 0:8], NEG, probp[:], ALU.mult, ALU.add),
                [("G0i", par), ("G0p", par)], [("G0m", par)])
            P.op("v", lambda e, b=b, rb=rb, xn2np=xn2np, xn2bp=xn2bp, x2Tp=x2Tp, probp=probp, g0scrp=g0scrp, tb8p=tb8p: e.tensor_reduce(
                stat[:, rb + 1:rb + 2], tb8p[:, 8:16], AX.X, ALU.max),
                [("G0m", par)], [S(10)])
            P.op("v", lambda e, b=b, rb=rb, xn2np=xn2np, xn2bp=xn2bp, x2Tp=x2Tp, probp=probp, g0scrp=g0scrp, tb8p=tb8p: e.tensor_scalar(
                tb8p[:, 0:8], probp[:], stat[:, rb + 1:rb + 2], None, ALU.is_ge),
                [("G0p", par), S(10)], [("G0i", par)])
            P.op("v", lambda e, r=r, probp=probp, tb8p=tb8p: e.tensor_tensor(
                w_all[:, r * 8:(r + 1) * 8], probp[:], tb8p[:, 0:8], ALU.mult),
                [("G0p", par), ("G0i", par)], [("w", r)])
            P.op("t", lambda e, r=r, probp=probp, tb8p=tb8p: e.transpose(
                pw_[0:8, :], w_all[:, r * 8:(r + 1) * 8], ident_f[:]),
                [("w", r), KIDF], [("G0pw",)])
            P.op("v", lambda e, r=r, probp=probp, tb8p=tb8p: e.tensor_copy(
                wTp[0:8, r * 128:(r + 1) * 128], pw_[0:8, :]),
                [("G0pw",), ("wTpz",)], [("wT", r)])
        P.flush()

    xn2h = AT([128, ND * 1024], BF16, 4 * KB)
    h_sb = AT([128, NF * 1024], BF16, 20 * KB)
    w2_sb = AT([128, NF * 512], BF16, 84 * KB)
    ctl1 = AT([1, 512], F32, 116 * KB)
    cppb = AT([1, 1024], BF16, 118 * KB)
    cppP = AT([128, D], BF16, 120 * KB)
    ct_ld = AT([1, 1024], F32, 123 * KB)
    cacc = AT([1, 1024], F32, 127 * KB)
    b1_sb = AT([128, NF * 8], F32, 131 * KB)
    acc_h = AT([128, 8 * D], F32, 132 * KB)
    w1st = [AT([128, ND * 512], BF16, 164 * KB + i * 8 * KB) for i in range(2)]
    x2t = AT([128, D], F32, 180 * KB)
    f1t = [AT([128, D], F32, 184 * KB), AT([128, D], F32, 184 * KB)]
    hpd = AT([128, NF], BF16, 188 * KB)

    P.dma("ld", b1_sb[:].rearrange("p (m e) -> p m e", m=NF),
          b1T_d.rearrange("(m p) e -> p m e", p=128), writes=[("b1",)])
    P.op("v", lambda e: e.memset(cppP[:], 0.0), writes=[("cppz",)])
    P.flush()

    with (nc.psum_tensor([128, 1024], F32) as ph0,
          nc.psum_tensor([128, 1024], F32) as ph1,
          nc.psum_tensor([128, 512], F32) as po0,
          nc.psum_tensor([128, 512], F32) as po1):
        for half in range(2):
            for dc in range(ND):
                P.op("ld", lambda e, dc=dc, half=half: e.dma_start(
                    out=xn2h[:, dc * 1024:(dc + 1) * 1024],
                    in_=xn2T_d[dc][:, half * 1024:(half + 1) * 1024]),
                    reads=[("x2Td", rr, dc) for rr in range(NHT)],
                    writes=[("xn2h", dc)])
            for e_ in range(8):
                if half == 0:
                    P.op("s", lambda e, e_=e_: e.activation(
                        hpd[:], b1_sb[:].rearrange(
                            "p (m e) -> p m e", m=NF)[:, :, e_], AF.Relu),
                        [("b1",)], [("hpd",)])
                for fg in range(NF // 4):
                    wb = w1st[fg % 2]
                    kw = ("w1", fg % 2)
                    P.op("ld", lambda e, wb=wb, e_=e_, fg=fg: e.dma_start(
                        out=wb[:].rearrange("p (dc f) -> p dc f", dc=ND),
                        in_=w1b_d[e_][:, fg * 512:(fg + 1) * 512]
                        .rearrange("(dc p) f -> p dc f", p=128)),
                        writes=[kw], pre=[(wbg, "wbg", 256)])
                    for fs in range(4):
                        ft = fg * 4 + fs
                        ph = (ph0, ph1)[ft % 2]
                        kp = ("ph", ft % 2)
                        for dc in range(ND):
                            for n in range(2):
                                P.op("t", lambda e, ph=ph, wb=wb, dc=dc, n=n,
                                     fs=fs:
                                     e.matmul(ph[:, n * 512:(n + 1) * 512],
                                              wb[:, dc * 512 + fs * 128:
                                                 dc * 512 + (fs + 1) * 128],
                                              xn2h[:, dc * 1024 + n * 512:
                                                   dc * 1024 + (n + 1) * 512],
                                              start=(dc == 0),
                                              stop=(dc == ND - 1)),
                                     [kw, ("xn2h", dc)], [kp])
                        P.op("s", lambda e, ph=ph, ft=ft, e_=e_: e.activation(
                            h_sb[:, ft * 1024:(ft + 1) * 1024], ph[:], AF.Relu,
                            bias=b1_sb[:, ft * 8 + e_: ft * 8 + e_ + 1]),
                            [kp, ("b1",)], [("h", ft)])
                for h2c in range(2):
                    P.op("ld", lambda e, e_=e_, h2c=h2c: e.dma_start(
                        out=w2_sb[:].rearrange("p (fc h) -> p fc h", fc=NF),
                        in_=w2b_d[e_][:, h2c * 512:(h2c + 1) * 512]
                        .rearrange("(fc p) h -> p fc h", p=128)),
                        writes=[("w2",)], pre=[(wbg, "wbg", 256)])
                    for r in range(8):
                        po = (po0, po1)[r % 2]
                        kp = ("po", r % 2)
                        for ft in range(NF):
                            P.op("t", lambda e, po=po, ft=ft, r=r: e.matmul(
                                po[:], h_sb[:, ft * 1024 + r * 128:
                                            ft * 1024 + (r + 1) * 128],
                                w2_sb[:, ft * 512:(ft + 1) * 512],
                                start=(ft == 0), stop=(ft == NF - 1)),
                                [("w2",), ("h", ft)], [kp])
                        rg = half * 8 + r
                        if e_ == 0:
                            P.op("v", lambda e, po=po, r=r, h2c=h2c, rg=rg:
                                 e.tensor_scalar(
                                     acc_h[:, r * D + h2c * 512:
                                           r * D + h2c * 512 + 512],
                                     po[:], w_all[:, rg * 8: rg * 8 + 1],
                                     None, ALU.mult),
                                 [kp, ("w", rg)], [("acc", r, h2c)])
                        else:
                            P.op("v", lambda e, po=po, r=r, h2c=h2c, rg=rg,
                                 e_=e_: e.scalar_tensor_tensor(
                                     acc_h[:, r * D + h2c * 512:
                                           r * D + h2c * 512 + 512],
                                     po[:],
                                     w_all[:, rg * 8 + e_: rg * 8 + e_ + 1],
                                     acc_h[:, r * D + h2c * 512:
                                           r * D + h2c * 512 + 512],
                                     ALU.mult, ALU.add),
                                 [kp, ("w", rg), ("acc", r, h2c)],
                                 [("acc", r, h2c)])
                    if half == 0:
                        po = po0
                        kp = ("po", 0)
                        for ft in range(NF):
                            P.op("t", lambda e, po=po, ft=ft: e.matmul(
                                po[0:1, :], hpd[:, ft:ft + 1],
                                w2_sb[:, ft * 512:(ft + 1) * 512],
                                start=(ft == 0), stop=(ft == NF - 1)),
                                [("hpd",), ("w2",)], [kp])
                        P.op("v", lambda e, po=po: e.tensor_copy(
                            ctl1[:], po[0:1, :]), [kp], [("ctl1",)])
                        P.dma("gp", ctil_d[e_:e_ + 1, h2c * 512:(h2c + 1) * 512],
                              ctl1[:], reads=[("ctl1",)],
                              writes=[("ctd", e_, h2c)])
            if half == 0:
                ctr = [("ctd", e2, h2) for e2 in range(8) for h2 in range(2)]
                P.dma("ld", cacc[:], ctil_d[0:1, :], reads=ctr,
                      writes=[("cacc",)])
                for e2 in range(1, 8):
                    P.dma("ld", ct_ld[:], ctil_d[e2:e2 + 1, :], reads=ctr,
                          writes=[("ctld",)])
                    P.op("v", lambda e: e.tensor_tensor(
                        cacc[:], cacc[:], ct_ld[:], ALU.add),
                        [("cacc",), ("ctld",)], [("cacc",)])
                for e2 in range(8):
                    P.dma("ld", ct_ld[:], b2_d[e2:e2 + 1, :], writes=[("ctld",)])
                    P.op("v", lambda e: e.tensor_tensor(
                        cacc[:], cacc[:], ct_ld[:], ALU.add),
                        [("cacc",), ("ctld",)], [("cacc",)])
                for e2 in range(8):
                    P.dma("ld", ct_ld[:], ctil_d[e2:e2 + 1, :], reads=ctr,
                          writes=[("ctld",)])
                    P.op("v", lambda e: e.tensor_tensor(
                        cppb[:], cacc[:], ct_ld[:], ALU.subtract),
                        [("cacc",), ("ctld",)], [("cppb",)])
                    P.dma("gp", cpp_d[e2:e2 + 1, :], cppb[:],
                          reads=[("cppb",)], writes=[("cppd", e2)])
                P.dma("ld", cppP[0:8, :], cpp_d[:, :],
                      reads=[("cppd", e2) for e2 in range(8)] + [("cppz",)],
                      writes=[("cppP",)])
            for r in range(8):
                rg = half * 8 + r
                for n in range(2):
                    P.op("t", lambda e, po=(po0, po1)[n], rg=rg, n=n:
                         e.matmul(po[:],
                                  wTp[:, rg * 128:(rg + 1) * 128],
                                  cppP[:, n * 512:(n + 1) * 512],
                                  start=True, stop=True),
                         [("wT", rg), ("cppP",)], [("po", n)])
                for n in range(2):
                    P.op("v", lambda e, po=(po0, po1)[n], r=r, n=n:
                         e.scalar_tensor_tensor(
                             acc_h[:, r * D + n * 512: r * D + (n + 1) * 512],
                             po[:], 1.0,
                             acc_h[:, r * D + n * 512: r * D + (n + 1) * 512],
                             ALU.mult, ALU.add),
                         [("po", n), ("acc", r, 0), ("acc", r, 1)],
                         [("acc", r, n)])
                fb = f1t[0]
                kf = ("f1", 0)
                P.op("gp", lambda e, fb=fb, rg=rg: dyn_load(e, fb[:], x1_d, rg),
                    reads=[("idx",)] + [("x1d", jj) for jj in range(NT)],
                    writes=[kf])
                P.op("v", lambda e, fb=fb, r=r: e.tensor_tensor(
                    x2t[:], acc_h[:, r * D:(r + 1) * D], fb[:], ALU.add),
                    [("acc", r, 0), ("acc", r, 1), kf], [("x2t",)])
                P.dma("ld", fb[:], attn1_d[rg * 128:(rg + 1) * 128, :],
                      reads=[("x2t",), ("attn_d1", rg)], writes=[kf])
                P.op("v", lambda e, fb=fb: e.tensor_tensor(
                    x2t[:], x2t[:], fb[:], ALU.add), [("x2t",), kf], [("x2t",)])
                P.op("gp", lambda e, fb=fb, rg=rg: dyn_load(
                    e, fb[:], xn1f_d, rg),
                    reads=[("idx",), ("x2t",)] +
                          [("xnf_d", jj) for jj in range(NT)],
                    writes=[kf])
                P.op("v", lambda e, fb=fb: e.tensor_tensor(
                    x2t[:], x2t[:], fb[:], ALU.add), [("x2t",), kf], [("x2t",)])
                P.dma("so", x2own[rg * 128:(rg + 1) * 128, :], x2t[:],
                      reads=[("x2t",)], writes=[("out", rg)])
        P.flush()

    P.op("ld", lambda e: e.nop(),
         deps=[P.lastw[("out", rg)] for rg in range(NHT)])
    P.flush()
    return nc


def kernel(**inputs):
    tabs = _host_tables()
    cm2 = (tabs.pop("cmask2_h0"), tabs.pop("cmask2_h1"))
    g = _prep_inputs(inputs)
    nc = build_program()
    X = inputs["L0_attn_X"]
    base = {}
    base.update(tabs)
    base.update({"wq0d": g["wq0d"], "wk0d": g["wk0d"], "wv0": g["wv0"],
                 "gwT": g["gwT"], "gbT": g["gbT"], "ew_g": g["ew_g"],
                 "eb0": g["eb0"], "eids_row": tabs["eids_row"],
                 "iota_p": tabs["iota_p"], "wq1": g["wq1"], "wk1": g["wk1"],
                 "wv1": g["wv1"], "gamma_bc": g["gamma_bc"],
                 "beta_bc": g["beta_bc"], "Wr": g["Wr"], "br_row": g["br_row"],
                 "W1": g["W1"], "b1T": g["b1T"], "W2": g["W2"], "b2": g["b2"]})
    in_maps = []
    for c in range(8):
        m = dict(base)
        m["x0"] = np.ascontiguousarray(X[c // 2])
        h = c % 2
        idx = np.empty((128, NHT), dtype=np.int32)
        for r in range(NHT):
            idx[:, r] = (2 * r + h) * 128 + np.arange(128)
        m["idx_own"] = idx
        m["cmask2"] = cm2[h]
        in_maps.append(m)
    res = run_bass_kernel_spmd(nc, in_maps, core_ids=list(range(8)))
    out = np.empty((B, T, D), dtype=np.float32)
    for b in range(B):
        ob = out[b].reshape(NT, 128, D)
        ob[0::2] = res.results[2 * b]["x2own"].reshape(NHT, 128, D)
        ob[1::2] = res.results[2 * b + 1]["x2own"].reshape(NHT, 128, D)
    return out


